# version 04
# baseline (speedup 1.0000x reference)
from contextlib import ExitStack

import numpy as np

import concourse.bass as bass
import concourse.mybir as mybir

F32 = mybir.dt.float32
BF16 = mybir.dt.bfloat16
AF = mybir.ActivationFunctionType
ALU = mybir.AluOpType
AX = mybir.AxisListType

D = 4096
NCORES = 8
TLOC = 1024
NTB = TLOC // 128
LN_EPS = 1e-5

ENGINES = ("tensor", "vector", "scalar", "gpsimd", "sync")


def chunk_ranges(core):
    b, c = divmod(core, 4)
    a0 = 512 * c
    b0 = 512 * (7 - c)
    return b, (a0, a0 + 512), (b0, b0 + 512)


def slice_owned(x_full, core):
    b, (a0, a1), (b0, b1) = chunk_ranges(core)
    return np.ascontiguousarray(
        np.concatenate([x_full[b, a0:a1], x_full[b, b0:b1]], axis=0))


def unslice_owned(shards, out):
    for core in range(NCORES):
        b, (a0, a1), (b0, b1) = chunk_ranges(core)
        out[b, a0:a1] = shards[core][:512]
        out[b, b0:b1] = shards[core][512:]
    return out


def pack_slabs(w_in_out, slab_cols):
    IN, OUT = w_in_out.shape
    nk = IN // 128
    ns = OUT // slab_cols
    w = w_in_out.reshape(nk, 128, ns, slab_cols).transpose(2, 1, 0, 3)
    return np.ascontiguousarray(w.reshape(ns, 128, nk * slab_cols))


class Ctx:

    def __init__(self, nc, es):
        self.nc = nc
        self.es = es
        self.s = {}
        self.c = {}
        self.q = {e: [] for e in ENGINES}

    def sems(self, *names):
        for n in names:
            if n not in self.s:
                self.s[n] = self.es.enter_context(self.nc.semaphore(n))
                self.c[n] = 0

    def sb(self, name, shape, dtype):
        return self.es.enter_context(self.nc.sbuf_tensor(name, shape, dtype))

    def ps(self, name, shape, dtype=F32):
        return self.es.enter_context(self.nc.psum_tensor(name, shape, dtype))

    def wait(self, eng, sem, val=None):
        if val is None:
            val = self.c[sem]
        if val <= 0:
            return
        h = self.s[sem]
        self.q[eng].append(lambda e: e.wait_ge(h, val))

    def op(self, eng, fn, sem=None, n=1):
        if sem is None:
            self.q[eng].append(fn)
            return None
        h = self.s[sem]
        self.c[sem] += n
        self.q[eng].append(lambda e: fn(e).then_inc(h, n))
        return self.c[sem]

    def flush(self):
        with self.nc.Block() as block:
            for name in ENGINES:
                thunks = self.q[name]
                if not thunks:
                    continue
                getattr(block, name)(lambda e, th=tuple(thunks): [f(e) for f in th])
        self.q = {e: [] for e in ENGINES}


def alloc_stats(ctx, tag, sq=None):
    return {
        "sq": sq if sq is not None else ctx.sb(f"{tag}_sq", [128, D], BF16),
        "ss": ctx.sb(f"{tag}_ss", [128, 1], F32),
        "sm": ctx.sb(f"{tag}_sm", [128, 1], F32),
        "mu": ctx.sb(f"{tag}_mu", [128, 1], F32),
        "var": ctx.sb(f"{tag}_var", [128, 1], F32),
        "rvar": ctx.sb(f"{tag}_rvar", [128, 1], F32),
        "rstd": ctx.sb(f"{tag}_rstd", [128, 1], F32),
        "nmr": ctx.sb(f"{tag}_nmr", [128, 1], F32),
    }


def emit_ln_stats(ctx, xb, st, sq=None, ready=None):
    V, S = "vector", "scalar"
    sq = st["sq"] if sq is None else sq
    if ready is not None:
        ctx.wait(S, *ready)
        ctx.wait(V, *ready)
    ss_n = ctx.op(S, lambda s: s.activation(out=sq[:], in_=xb[:], func=AF.Square,
                                            accum_out=st["ss"][:]), "act")
    ctx.op(V, lambda v: v.reduce_sum(st["sm"][:], xb[:], axis=AX.X), "dve")
    ctx.wait(V, "dve")
    ctx.op(V, lambda v: v.tensor_scalar(out=st["mu"][:], in0=st["sm"][:],
                                        scalar1=1.0 / D, scalar2=None, op0=ALU.mult), "dve")
    ctx.wait(V, "dve")
    ctx.op(V, lambda v: v.tensor_tensor(out=st["var"][:], in0=st["mu"][:],
                                        in1=st["mu"][:], op=ALU.mult), "dve")
    ctx.wait(V, "dve")
    ctx.wait(V, "act", ss_n)
    ctx.op(V, lambda v: v.scalar_tensor_tensor(
        out=st["var"][:], in0=st["ss"][:], scalar=1.0 / D, in1=st["var"][:],
        op0=ALU.mult, op1=ALU.subtract), "dve")
    ctx.wait(V, "dve")
    ctx.op(V, lambda v: v.tensor_scalar(out=st["var"][:], in0=st["var"][:],
                                        scalar1=LN_EPS, scalar2=None, op0=ALU.add), "dve")
    ctx.wait(V, "dve")
    ctx.op(V, lambda v: v.reciprocal(st["rvar"][:], st["var"][:]), "dve")
    ctx.wait(S, "dve")
    ctx.op(S, lambda s: s.activation(out=st["rstd"][:], in_=st["rvar"][:],
                                     func=AF.Sqrt), "act")
    ctx.wait(V, "act")
    return ctx.op(V, lambda v: v.scalar_tensor_tensor(
        out=st["nmr"][:], in0=st["mu"][:], scalar=-1.0, in1=st["rstd"][:],
        op0=ALU.mult, op1=ALU.mult), "dve")


def build_l0ffn_program():
    nc = bass.Bass("TRN2", target_bir_lowering=False)
    xin = nc.dram_tensor("xin", [TLOC, D], F32, kind="ExternalInput").ap()
    wup = nc.dram_tensor("wup_s", [8, 128, 32 * 512], F32, kind="ExternalInput").ap()
    gate_t = nc.dram_tensor("gate_t", [D, 4], F32, kind="ExternalInput").ap()
    ebias = nc.dram_tensor("ebias", [1, 4], F32, kind="ExternalInput").ap()
    lora_at = nc.dram_tensor("lora_at", [D, 32], F32, kind="ExternalInput").ap()
    lora_bs = nc.dram_tensor("lora_bs", [32, D], F32, kind="ExternalInput").ap()
    t8 = nc.dram_tensor("t8", [4, 32], F32, kind="ExternalInput").ap()
    ident = nc.dram_tensor("ident", [128, 128], F32, kind="ExternalInput").ap()
    xout = nc.dram_tensor("xout", [TLOC, D], F32, kind="ExternalOutput").ap()

    with ExitStack() as es:
        ctx = Ctx(nc, es)
        emit_l0ffn(ctx, xin, xout, wup, gate_t, ebias, lora_at, lora_bs, t8, ident)
        ctx.wait("sync", "dst0")
        ctx.wait("sync", "dst1")
        ctx.flush()
    return nc


def emit_l0ffn(ctx, xin, xout, wup, gate_t, ebias, lora_at, lora_bs, t8, ident,
               pre="", tiles=None, ncol=512):
    if tiles is None:
        tiles = [(i * 128, 0, 128) for i in range(NTB)]
    nt = len(tiles)
    noc = D // ncol
    nsub = 512 // ncol
    ctx.sems("dc", "dx0", "dx1", "dxs0", "dxs1", "dxs2", "dxs3", "dw0", "dw1",
             "pe", "act", "dve", "dst0", "dst1")
    T, V, S, G, Y = "tensor", "vector", "scalar", "gpsimd", "sync"

    gate_sb = ctx.sb(pre + "gate_sb", [128, 32, 4], BF16)
    ebias_sb = ctx.sb(pre + "ebias_sb", [128, 4], F32)
    la_sb = ctx.sb(pre + "la_sb", [128, 32, 32], BF16)
    lb_sb = ctx.sb(pre + "lb_sb", [32, D], BF16)
    t8_sb = ctx.sb(pre + "t8_sb", [4, 32], BF16)
    id_sb = ctx.sb(pre + "id_sb", [128, 128], BF16)
    xb = [ctx.sb(pre + f"xb{i}", [128, D], F32) for i in range(2)]
    xnb = [ctx.sb(pre + f"xnb{i}", [128, D], BF16) for i in range(2)]
    xnt = ctx.sb(pre + "xnt", [128, nt * D], BF16)
    wsl = [ctx.sb(pre + f"wsl{i}", [128, 32, ncol], BF16) for i in range(2)]
    zsel = ctx.sb(pre + "zsel", [32, nt * 128], BF16)
    logits = ctx.sb(pre + "logits", [128, 4], F32)
    gmax = ctx.sb(pre + "gmax", [128, 1], F32)
    onehot = ctx.sb(pre + "onehot", [128, 4], BF16)
    onehT = ctx.sb(pre + "onehT", [4, 128], BF16)
    mask_sb = ctx.sb(pre + "mask_sb", [32, 128], BF16)
    xsl = [ctx.sb(pre + f"xsl{i}", [128, ncol], F32) for i in range(4)]
    xo_sb = [ctx.sb(pre + f"xo{i}", [128, ncol], F32) for i in range(2)]
    st = alloc_stats(ctx, pre + "st", sq=xnb[0])

    psT = [ctx.ps(pre + f"psT{i}", [128, 512], BF16) for i in range(2)]
    psg = ctx.ps(pre + "psg", [128, 4])
    psoh = ctx.ps(pre + "psoh", [4, 128], BF16)
    psm = ctx.ps(pre + "psm", [32, 128])
    psz = ctx.ps(pre + "psz", [32, 128])
    pso = [ctx.ps(pre + f"pso{i}", [128, ncol]) for i in range(2)]

    for fn in (
        lambda g: g.dma_start(out=gate_sb[:], in_=gate_t.rearrange("(k p) e -> p k e", p=128)),
        lambda g: g.dma_start(out=la_sb[:], in_=lora_at.rearrange("(k p) e -> p k e", p=128)),
        lambda g: g.dma_start(out=lb_sb[:], in_=lora_bs[:]),
        lambda g: g.dma_start(out=t8_sb[:], in_=t8[:]),
        lambda g: g.dma_start(out=id_sb[:], in_=ident[:]),
        lambda g: g.dma_start(out=ebias_sb[:], in_=ebias.broadcast_to([128, 4])),
    ):
        ctx.wait(G, "dc")
        ctx.op(G, fn, "dc", 16)
    consts = ctx.c["dc"]

    wsl_ready = [None, None]
    wsl_free = [0, 0]
    def _slab_src(oc):
        if nsub == 1:
            return wup[oc].rearrange("p (k j) -> p k j", j=ncol)
        return wup[oc // nsub].rearrange("p (k j) -> p k j", j=512)[
            :, :, (oc % nsub) * ncol:(oc % nsub + 1) * ncol]

    ctx.wait(G, "dc", consts)
    ctx.op(G, lambda g: g.dma_start(out=wsl[0][:], in_=_slab_src(0)), "dw0", 16)
    wsl_ready[0] = ("dw0", ctx.c["dw0"])

    xb_ready = [None] * nt
    for i in range(min(2, nt)):
        r0 = tiles[i][0]
        ctx.op(Y, lambda sy, i=i, r0=r0: sy.dma_start(
            out=xb[i][:], in_=xin[r0:r0 + 128, :]), f"dx{i}", 16)
        xb_ready[i] = (f"dx{i}", ctx.c[f"dx{i}"])

    tp_done = [0] * nt
    gate_done = [0] * nt
    xnb_free = [0, 0]
    psT_free = [0, 0]
    psg_free = 0
    psz_free = 0
    psoh_free = 0
    psm_free = 0
    onehot_free = 0
    onehT_free = 0
    mask_free = 0

    for tb in range(nt):
        par = tb % 2
        r0 = tiles[tb][0]

        if tb >= 2:
            ctx.wait("scalar", "pe", xnb_free[par])
        emit_ln_stats(ctx, xb[par], st, sq=xnb[par], ready=xb_ready[tb])
        ctx.wait(V, "dve")
        if tb >= 2:
            ctx.wait(V, "pe", xnb_free[par])
        ln_apply = ctx.op(V, lambda v, par=par: v.tensor_scalar(
            out=xnb[par][:], in0=xb[par][:], scalar1=st["rstd"][:],
            scalar2=st["nmr"][:], op0=ALU.mult, op1=ALU.add), "dve")

        if tb + 2 < nt:
            ctx.wait(Y, "dve", ln_apply)
            nr0 = tiles[tb + 2][0]
            ctx.op(Y, lambda sy, nr0=nr0, par=par: sy.dma_start(
                out=xb[par][:], in_=xin[nr0:nr0 + 128, :]), f"dx{par}", 16)
            xb_ready[tb + 2] = (f"dx{par}", ctx.c[f"dx{par}"])

        ctx.wait(T, "dc", consts)
        ctx.wait(T, "dve", ln_apply)
        for f4 in range(8):
            tpar = f4 % 2
            ctx.wait(T, "dve", psT_free[tpar])
            for ff in range(4):
                f = f4 * 4 + ff
                pe_n = ctx.op(T, lambda t, par=par, tpar=tpar, f=f, ff=ff: t.transpose(
                    psT[tpar][:, ff * 128:(ff + 1) * 128],
                    xnb[par][:, f * 128:(f + 1) * 128], id_sb[:]),
                    "pe" if ff == 3 else None)
            ctx.wait(V, "pe", pe_n)
            psT_free[tpar] = ctx.op(V, lambda v, tb=tb, tpar=tpar, f4=f4: v.tensor_copy(
                xnt[:, tb * D + f4 * 512: tb * D + (f4 + 1) * 512], psT[tpar][:]), "dve")
        xnb_free[par] = ctx.c["pe"]
        tp_done[tb] = ctx.c["dve"]

        ctx.wait(T, "dve", max(psg_free, tp_done[tb]))
        for k in range(32):
            ctx.op(T, lambda t, tb=tb, k=k: t.matmul(
                psg[:], xnt[:, tb * D + k * 128:tb * D + (k + 1) * 128],
                gate_sb[:, k, :],
                start=(k == 0), stop=(k == 31)), "pe" if k == 31 else None)
        ctx.wait(V, "pe")
        ctx.wait(V, "dc", consts)
        psg_free = ctx.op(V, lambda v: v.tensor_tensor(out=logits[:], in0=psg[:],
                                            in1=ebias_sb[:], op=ALU.add), "dve")
        ctx.wait(V, "dve")
        ctx.op(V, lambda v: v.reduce_max(gmax[:], logits[:], axis=AX.X), "dve")
        ctx.wait(V, "dve")
        ctx.wait(V, "pe", onehot_free)
        oh_n = ctx.op(V, lambda v: v.tensor_scalar(
            out=onehot[:], in0=logits[:], scalar1=gmax[:], scalar2=None,
            op0=ALU.is_equal), "dve")
        ctx.wait(T, "dve", max(oh_n, psoh_free))
        ohT_pe = ctx.op(T, lambda t: t.transpose(psoh[:], onehot[:], id_sb[:]), "pe")
        onehot_free = ohT_pe
        ctx.wait(V, "pe", max(ohT_pe, onehT_free))
        ohT_n = ctx.op(V, lambda v: v.tensor_copy(onehT[:], psoh[:]), "dve")
        psoh_free = ohT_n
        ctx.wait(T, "dve", ohT_n)
        ctx.wait(T, "act", psm_free)
        m_pe = ctx.op(T, lambda t: t.matmul(psm[:], t8_sb[:], onehT[:],
                                            start=True, stop=True), "pe")
        onehT_free = m_pe
        ctx.wait(S, "pe", m_pe)
        ctx.wait(S, "dve", mask_free)
        msk_n = ctx.op(S, lambda s: s.activation(out=mask_sb[:], in_=psm[:],
                                                 func=AF.Copy), "act")
        psm_free = msk_n
        ctx.wait(T, "dve", psz_free)
        for k in range(32):
            ctx.op(T, lambda t, tb=tb, k=k: t.matmul(
                psz[:], la_sb[:, k, :],
                xnt[:, tb * D + k * 128:tb * D + (k + 1) * 128],
                start=(k == 0), stop=(k == 31)), "pe" if k == 31 else None)
        ctx.wait(V, "pe")
        ctx.wait(V, "act", msk_n)
        gate_done[tb] = ctx.op(V, lambda v, tb=tb: v.tensor_tensor(
            out=zsel[:, tb * 128:(tb + 1) * 128], in0=psz[:], in1=mask_sb[:],
            op=ALU.mult), "dve")
        psz_free = gate_done[tb]
        mask_free = gate_done[tb]

    pso_free = [0, 0]
    xsl_free = [0] * 4
    xo_free = [0, 0]
    it = 0
    for oc in range(noc):
        wpar = oc % 2
        if oc + 1 < noc:
            npar = (oc + 1) % 2
            ctx.wait(G, "pe", wsl_free[npar])
            ctx.op(G, lambda g, oc=oc, npar=npar: g.dma_start(
                out=wsl[npar][:], in_=_slab_src(oc + 1)), f"dw{npar}", 16)
            wsl_ready[npar] = (f"dw{npar}", ctx.c[f"dw{npar}"])

        for tb in range(nt):
            par = it % 2
            xp = it % 4
            r0 = tiles[tb][0]
            ctx.wait(Y, "dve", xsl_free[xp])
            ctx.op(Y, lambda sy, r0=r0, oc=oc, xp=xp: sy.dma_start(
                out=xsl[xp][:], in_=xin[r0:r0 + 128, oc * ncol:(oc + 1) * ncol]),
                f"dxs{xp}", 16)
            xsl_n = (f"dxs{xp}", ctx.c[f"dxs{xp}"])

            ctx.wait(T, *wsl_ready[wpar])
            ctx.wait(T, "dve", max(tp_done[tb], pso_free[par]))
            for k in range(32):
                ctx.op(T, lambda t, tb=tb, k=k, par=par, wpar=wpar: t.matmul(
                    pso[par][:], xnt[:, tb * D + k * 128:tb * D + (k + 1) * 128],
                    wsl[wpar][:, k, :],
                    start=(k == 0), stop=False))
            ctx.wait(T, "dve", gate_done[tb])
            mm_n = ctx.op(T, lambda t, tb=tb, oc=oc, par=par: t.matmul(
                pso[par][:], zsel[:, tb * 128:(tb + 1) * 128],
                lb_sb[0:32, oc * ncol:(oc + 1) * ncol],
                start=False, stop=True), "pe")
            if tb == nt - 1:
                wsl_free[wpar] = mm_n

            ctx.wait(V, "pe", mm_n)
            ctx.wait(V, *xsl_n)
            ctx.wait(V, f"dst{par}", xo_free[par])
            ep_n = ctx.op(V, lambda v, par=par, xp=xp: v.scalar_tensor_tensor(
                out=xo_sb[par][:], in0=pso[par][:], scalar=0.5, in1=xsl[xp][:],
                op0=ALU.mult, op1=ALU.add), "dve")
            pso_free[par] = ep_n
            xsl_free[xp] = ep_n
            lo, hi = tiles[tb][1], tiles[tb][2]
            ctx.wait(Y, "dve", ep_n)
            xo_free[par] = ctx.op(Y, lambda sy, r0=r0, oc=oc, par=par, lo=lo, hi=hi:
                                  sy.dma_start(
                out=xout[r0 + lo:r0 + hi, oc * ncol:(oc + 1) * ncol],
                in_=xo_sb[par][lo:hi, :]), f"dst{par}", 16)
            it += 1


def stage_l0ffn_weights(inp):
    W_up = np.asarray(inp["L0_ffn_W_up"], np.float32)
    gate = np.asarray(inp["L0_ffn_gate_w"], np.float32)
    eb = np.asarray(inp["L0_ffn_expert_bias"], np.float32)
    A = np.asarray(inp["L0_ffn_lora_A"], np.float32)
    B = np.asarray(inp["L0_ffn_lora_B"], np.float32)
    t8 = np.zeros((4, 32), np.float32)
    for e in range(4):
        t8[e, e * 8:(e + 1) * 8] = 1.0
    return {
        "wup_s": pack_slabs(np.ascontiguousarray(W_up.T), 512),
        "gate_t": np.ascontiguousarray(gate.T),
        "ebias": eb.reshape(1, 4),
        "lora_at": np.ascontiguousarray(A.transpose(2, 0, 1).reshape(D, 32)),
        "lora_bs": np.ascontiguousarray(B.transpose(0, 2, 1).reshape(32, D)),
        "t8": t8,
        "ident": np.eye(128, dtype=np.float32),
    }



TIN = 1028


def xin_row(owned_idx):
    return 1 + owned_idx if owned_idx < 512 else 3 + owned_idx


def sub(ctx, es):
    c = Ctx.__new__(Ctx)
    c.__dict__.update(ctx.__dict__)
    c.es = es
    return c


def build_l0attn_program():
    nc = bass.Bass("TRN2", target_bir_lowering=False)
    xin = nc.dram_tensor("xin_h", [TIN, D], F32, kind="ExternalInput").ap()
    convw = nc.dram_tensor("convw_s", [3, 6, 128, 32 * 512], F32, kind="ExternalInput").ap()
    convb = nc.dram_tensor("convb", [1, 3072], F32, kind="ExternalInput").ap()
    wq = nc.dram_tensor("wq_r", [1024, 1024], F32, kind="ExternalInput").ap()
    wk = nc.dram_tensor("wk_r", [1024, 1024], F32, kind="ExternalInput").ap()
    wv = nc.dram_tensor("wv_r", [1024, 1024], F32, kind="ExternalInput").ap()
    wo = nc.dram_tensor("wo_r", [1024, 1024], F32, kind="ExternalInput").ap()
    bqc = nc.dram_tensor("bq_c", [128, 8], F32, kind="ExternalInput").ap()
    bkc = nc.dram_tensor("bk_c", [128, 8], F32, kind="ExternalInput").ap()
    bvr = nc.dram_tensor("bv_r", [1, 1024], F32, kind="ExternalInput").ap()
    bor = nc.dram_tensor("bo_r", [1, 1024], F32, kind="ExternalInput").ap()
    projw = nc.dram_tensor("projw_s", [8, 128, 8 * 512], F32, kind="ExternalInput").ap()
    projb = nc.dram_tensor("projb", [1, D], F32, kind="ExternalInput").ap()
    ident = nc.dram_tensor("ident", [128, 128], F32, kind="ExternalInput").ap()
    trimask = nc.dram_tensor("trimask", [128, 128], F32, kind="ExternalInput").ap()
    xout = nc.dram_tensor("xout", [TLOC, D], F32, kind="ExternalOutput").ap()

    with ExitStack() as es:
        ctx = Ctx(nc, es)
        emit_l0attn(ctx, xin, xout, convw, convb, wq, wk, wv, wo, bqc, bkc,
                    bvr, bor, projw, projb, ident, trimask)
        ctx.wait("sync", "dst0")
        ctx.wait("sync", "dst1")
        ctx.flush()
    return nc


def emit_l0attn(ctx, xin, xout, convw, convb, wq, wk, wv, wo, bqc, bkc,
                bvr, bor, projw, projb, ident, trimask):
    ctx.sems("dc", "dx0", "dx1", "dxs0", "dxs1", "dxs2", "dxs3",
             "dw0", "dw1", "pe", "act", "dve", "dst0", "dst1")
    T, V, S, G, Y = "tensor", "vector", "scalar", "gpsimd", "sync"

    id_sb = ctx.sb("id_sb", [128, 128], BF16)
    tri_sb = ctx.sb("tri_sb", [128, 128], BF16)
    ones_sb = ctx.sb("ones_sb", [1, 128], BF16)
    convb_sb = ctx.sb("convb_sb", [1, 3072], BF16)
    bq_sb = ctx.sb("bq_sb", [128, 8], F32)
    bk_sb = ctx.sb("bk_sb", [128, 8], F32)
    bv_sb = ctx.sb("bv_sb", [1, 1024], BF16)
    bo_sb = ctx.sb("bo_sb", [1, 1024], BF16)
    projb_sb = ctx.sb("projb_sb", [1, D], BF16)
    qkvc = ctx.sb("qkvc", [128, NTB, 3072], BF16)

    for fn in (
        lambda g: g.dma_start(out=id_sb[:], in_=ident[:]),
        lambda g: g.dma_start(out=tri_sb[:], in_=trimask[:]),
        lambda g: g.dma_start(out=convb_sb[:], in_=convb[:]),
        lambda g: g.dma_start(out=bv_sb[:], in_=bvr[:]),
        lambda g: g.dma_start(out=bo_sb[:], in_=bor[:]),
        lambda g: g.dma_start(out=projb_sb[:], in_=projb[:]),
    ):
        ctx.wait(G, "dc")
        ctx.op(G, fn, "dc", 16)
    ctx.wait(G, "dc")
    ctx.op(G, lambda g: g.dma_start(out=bq_sb[:], in_=bqc[:]), "dc", 16)
    ctx.wait(G, "dc")
    ctx.op(G, lambda g: g.dma_start(out=bk_sb[:], in_=bkc[:]), "dc", 16)
    consts = ctx.c["dc"]
    ctx.op(V, lambda v: v.memset(ones_sb[:], 1.0), "dve")
    ctx.wait(V, "dc", consts)
    ctx.op(V, lambda v: v.tensor_scalar(out=bq_sb[:], in0=bq_sb[:], scalar1=0.125,
                                        scalar2=None, op0=ALU.mult), "dve")
    bq_ready = ctx.c["dve"]
    ones_ready = ctx.c["dve"]

    qkvc_ready = [0] * NTB
    ab0 = [xin_row(128 * i) for i in range(NTB)]

    with ExitStack() as p12:
        c12 = sub(ctx, p12)
        xnt = c12.sb("xnt2", [128, 32, TIN], BF16)

        with ExitStack() as p1:
            c1 = sub(ctx, p1)
            xb = [c1.sb(f"a_xb{i}", [128, D], F32) for i in range(2)]
            xnb = [c1.sb(f"a_xnb{i}", [128, D], BF16) for i in range(2)]
            st = alloc_stats(c1, "a_st", sq=xnb[0])
            psT = [c1.ps(f"a_psT{i}", [128, 512], BF16) for i in range(2)]

            t0s = [min(i * 128, TIN - 128) for i in range(9)]
            xb_ready = [None] * 9
            for i in range(2):
                ctx.op(Y, lambda sy, i=i: sy.dma_start(
                    out=xb[i][:], in_=xin[t0s[i]:t0s[i] + 128, :]), f"dx{i % 2}", 16)
                xb_ready[i] = (f"dx{i % 2}", ctx.c[f"dx{i % 2}"])

            xnb_free = [0, 0]
            psT_free = [0, 0]
            for i in range(9):
                par = i % 2
                if i >= 2:
                    ctx.wait(S, "pe", xnb_free[par])
                emit_ln_stats(ctx, xb[par], st, sq=xnb[par], ready=xb_ready[i])
                ctx.wait(V, "dve")
                if i >= 2:
                    ctx.wait(V, "pe", xnb_free[par])
                ln_apply = ctx.op(V, lambda v, par=par: v.tensor_scalar(
                    out=xnb[par][:], in0=xb[par][:], scalar1=st["rstd"][:],
                    scalar2=st["nmr"][:], op0=ALU.mult, op1=ALU.add), "dve")
                if i + 2 < 9:
                    ctx.wait(Y, "dve", ln_apply)
                    ctx.op(Y, lambda sy, i=i, par=par: sy.dma_start(
                        out=xb[par][:], in_=xin[t0s[i + 2]:t0s[i + 2] + 128, :]),
                        f"dx{par}", 16)
                    xb_ready[i + 2] = (f"dx{par}", ctx.c[f"dx{par}"])
                ctx.wait(T, "dc", consts)
                ctx.wait(T, "dve", ln_apply)
                for f4 in range(8):
                    tp = f4 % 2
                    ctx.wait(T, "dve", psT_free[tp])
                    for ff in range(4):
                        f = f4 * 4 + ff
                        pe_n = ctx.op(T, lambda t, par=par, tp=tp, f=f, ff=ff: t.transpose(
                            psT[tp][:, ff * 128:(ff + 1) * 128],
                            xnb[par][:, f * 128:(f + 1) * 128], id_sb[:]),
                            "pe" if ff == 3 else None)
                    ctx.wait(V, "pe", pe_n)
                    psT_free[tp] = ctx.op(V, lambda v, i=i, tp=tp, f4=f4: v.tensor_copy(
                        xnt[:, f4 * 4:(f4 + 1) * 4, t0s[i]:t0s[i] + 128], psT[tp][:]), "dve")
                xnb_free[par] = ctx.c["pe"]
            p1_pe = ctx.c["pe"]
            p1_dve = ctx.c["dve"]
            p1_act = ctx.c["act"]

        with ExitStack() as p2:
            c2 = sub(ctx, p2)
            csl = [c2.sb(f"csl{i}", [128, 32 * 512], BF16) for i in range(2)]
            psc = [c2.ps(f"psc{i}", [128, 512]) for i in range(8)]

            csl_ready = [None, None]
            csl_free = [0, 0]
            ctx.wait(G, "pe", p1_pe)
            ctx.wait(G, "dve", p1_dve)
            ctx.wait(G, "act", p1_act)
            ctx.wait(G, "dc", consts)
            ctx.op(G, lambda g: g.dma_start(out=csl[0][:], in_=convw[0, 0]), "dw0", 16)
            csl_ready[0] = ("dw0", ctx.c["dw0"])

            sl = 0
            for oc6 in range(6):
                for dt in range(3):
                    par = sl % 2
                    if sl + 1 < 18:
                        npar = (sl + 1) % 2
                        noc, ndt = divmod(sl + 1, 3)
                        ctx.wait(G, "pe", csl_free[npar])
                        ctx.op(G, lambda g, noc=noc, ndt=ndt, npar=npar: g.dma_start(
                            out=csl[npar][:], in_=convw[ndt, noc]), f"dw{npar}", 16)
                        csl_ready[npar] = (f"dw{npar}", ctx.c[f"dw{npar}"])
                    ctx.wait(T, *csl_ready[par])
                    if sl == 0:
                        ctx.wait(T, "dve", p1_dve)
                    for ab in range(NTB):
                        if dt == 0:
                            ctx.wait(T, "dve", qkvc_ready[ab])
                        for k in range(32):
                            is_last_of_slab = (ab == NTB - 1 and k == 31)
                            ctx.op(T, lambda t, ab=ab, k=k, dt=dt, par=par:
                                   t.matmul(psc[ab][:],
                                            xnt[:, k, ab0[ab] + dt - 1: ab0[ab] + dt + 127],
                                            csl[par][:, k * 512:(k + 1) * 512],
                                            start=(dt == 0 and k == 0), stop=False),
                                   "pe" if is_last_of_slab else None)
                    csl_free[par] = ctx.c["pe"]
                    sl += 1
                ctx.wait(T, "dve", ones_ready)
                for ab in range(NTB):
                    mm_n = ctx.op(T, lambda t, ab=ab, oc6=oc6: t.matmul(
                        psc[ab][:], ones_sb[:],
                        convb_sb[:, oc6 * 512:(oc6 + 1) * 512],
                        start=False, stop=True), "pe")
                    ctx.wait(V, "pe", mm_n)
                    qkvc_ready[ab] = ctx.op(V, lambda v, ab=ab, oc6=oc6: v.tensor_copy(
                        qkvc[:, ab, oc6 * 512:(oc6 + 1) * 512], psc[ab][:]), "dve")
            p2_pe = ctx.c["pe"]
            p2_dve = ctx.c["dve"]

    o2t_ready = [0] * NTB
    with ExitStack() as p3:
        c3 = sub(ctx, p3)
        o2t = c3.sb("o2t", [128, 8, TLOC], BF16)
        wq_sb = c3.sb("wq_sb", [128, 8, 1024], BF16)
        wk_sb = c3.sb("wk_sb", [128, 8, 1024], BF16)
        wv_sb = c3.sb("wv_sb", [128, 8, 1024], BF16)
        wo_sb = c3.sb("wo_sb", [128, 8, 1024], BF16)
        qkvT = [c3.sb(f"qkvT{i}", [128, 24, 128], BF16) for i in range(2)]
        qhT = [c3.sb(f"qhT{i}", [128, 8, 128], BF16) for i in range(2)]
        khT = [c3.sb(f"khT{i}", [128, 8, 128], BF16) for i in range(2)]
        vh = [c3.sb(f"vh{i}", [128, 16, 65], BF16) for i in range(2)]
        probs = [c3.sb(f"probs{i}", [128, 128], BF16) for i in range(2)]
        rz = [c3.sb(f"rz{i}", [128, 1], F32) for i in range(2)]
        oh = [c3.sb(f"oh{i}", [128, 16 * 64], BF16) for i in range(2)]
        o2 = [c3.sb(f"o2{i}", [128, 1024], BF16) for i in range(2)]
        psT3 = [c3.ps(f"psT3_{i}", [128, 512], BF16) for i in range(2)]
        psqk = [c3.ps(f"psqk{i}", [128, 128]) for i in range(2)]
        psv = c3.ps("psv", [128, 512])
        pss = [c3.ps(f"pss{i}", [128, 512]) for i in range(2)]
        pso3 = psqk

        for fn in (
            lambda g: g.dma_start(out=wq_sb[:], in_=wq.rearrange("(k p) o -> p k o", p=128)),
            lambda g: g.dma_start(out=wk_sb[:], in_=wk.rearrange("(k p) o -> p k o", p=128)),
            lambda g: g.dma_start(out=wv_sb[:], in_=wv.rearrange("(k p) o -> p k o", p=128)),
            lambda g: g.dma_start(out=wo_sb[:], in_=wo.rearrange("(k p) o -> p k o", p=128)),
        ):
            ctx.wait(G, "pe", p2_pe)
            ctx.wait(G, "dve", p2_dve)
            ctx.wait(G, "dc")
            ctx.op(G, fn, "dc", 16)
        w3_ready = ctx.c["dc"]
        for i in range(2):
            ctx.op(V, lambda v, i=i: v.memset(vh[i][:, :, 64:65], 1.0), "dve")
        vh_ones_dve = ctx.c["dve"]

        qkvT_free = [0, 0]
        psT3_free = [0, 0]
        qhT_free = [0, 0]
        khT_free = [0, 0]
        vh_free = [0, 0]
        oh_free = [0, 0]
        o2_free = [0, 0]
        psqk_free = [0, 0]
        psv_free = 0
        pss_free = [0, 0]
        pso3_free = [0, 0]
        pswo_free = 0
        probs_free = [0, 0]
        rz_free = [0, 0]

        ctx.wait(T, "dve", p2_dve)
        for ab in range(NTB):
            par = ab % 2
            ctx.wait(T, "dve", max(qkvc_ready[ab], qkvT_free[par]))
            for f4 in range(6):
                tp = f4 % 2
                ctx.wait(T, "dve", psT3_free[tp])
                for ff in range(4):
                    f = f4 * 4 + ff
                    pe_n = ctx.op(T, lambda t, ab=ab, f=f, ff=ff, tp=tp: t.transpose(
                        psT3[tp][:, ff * 128:(ff + 1) * 128],
                        qkvc[:, ab, f * 128:(f + 1) * 128], id_sb[:]),
                        "pe" if ff == 3 else None)
                ctx.wait(V, "pe", pe_n)
                psT3_free[tp] = ctx.op(V, lambda v, par=par, f4=f4, tp=tp: v.tensor_copy(
                    qkvT[par][:, f4 * 4:(f4 + 1) * 4, :], psT3[tp][:]), "dve")
            qkvT_ready = ctx.c["dve"]

            ctx.wait(T, "dc", w3_ready)
            for which, wsb, dsttile, dstfree, bias, sc in (
                    ("q", wq_sb, qhT, qhT_free, bq_sb, 0.125),
                    ("k", wk_sb, khT, khT_free, bk_sb, 1.0)):
                base = 0 if which == "q" else 8
                for hc in range(8):
                    pq = hc % 2
                    ctx.wait(T, "dve", qkvT_ready)
                    ctx.wait(T, "dve", psqk_free[pq])
                    ctx.wait(T, "act", pso3_free[pq])
                    for k in range(8):
                        ctx.op(T, lambda t, k=k, hc=hc, par=par, pq=pq,
                               wsb=wsb, base=base: t.matmul(
                            psqk[pq][:], wsb[:, k, hc * 128:(hc + 1) * 128],
                            qkvT[par][:, base + k, :],
                            start=(k == 0), stop=(k == 7)), "pe" if k == 7 else None)
                    ctx.wait(V, "pe", ctx.c["pe"])
                    if which == "q":
                        ctx.wait(V, "dve", bq_ready)
                    if ab >= 2:
                        ctx.wait(V, "pe", dstfree[par])
                    psqk_free[pq] = ctx.op(V, lambda v, hc=hc, par=par, pq=pq,
                                           dsttile=dsttile, bias=bias, sc=sc:
                                           v.tensor_scalar(
                        out=dsttile[par][:, hc, :], in0=psqk[pq][:],
                        scalar1=sc, scalar2=bias[:, hc:hc + 1],
                        op0=ALU.mult, op1=ALU.add), "dve")
            qk_ready = ctx.c["dve"]

            for vc2 in range(2):
                ctx.wait(T, "dve", max(qkvT_ready, ones_ready))
                ctx.wait(T, "act", max(psv_free, pswo_free))
                for k in range(8):
                    ctx.op(T, lambda t, k=k, vc2=vc2, par=par: t.matmul(
                        psv[:], qkvT[par][:, 16 + k, :],
                        wv_sb[:, k, vc2 * 512:(vc2 + 1) * 512],
                        start=(k == 0), stop=False))
                mm = ctx.op(T, lambda t, vc2=vc2: t.matmul(
                    psv[:], ones_sb[:], bv_sb[:, vc2 * 512:(vc2 + 1) * 512],
                    start=False, stop=True), "pe")
                ctx.wait(S, "pe", mm)
                ctx.wait(S, "dve", vh_ones_dve)
                if ab >= 2:
                    ctx.wait(S, "pe", vh_free[par])
                psv_free = ctx.op(S, lambda s, vc2=vc2, par=par: s.activation(
                    out=vh[par][:, vc2 * 8:(vc2 + 1) * 8, 0:64],
                    in_=psv[:], func=AF.Copy), "act")
            v_ready = ctx.c["act"]

            for h in range(16):
                hp = h % 2
                p0 = 64 * (h % 2)
                kt = h // 2
                ctx.wait(T, "dve", qk_ready)
                ctx.wait(T, "act", pss_free[hp])
                s_pe = ctx.op(T, lambda t, par=par, p0=p0, kt=kt, hp=hp: t.matmul(
                    pss[hp][:, 0:128], khT[par][p0:p0 + 64, kt, :], qhT[par][p0:p0 + 64, kt, :],
                    start=True, stop=True), "pe")
                ctx.wait(S, "pe", max(s_pe, probs_free[hp]))
                e_n = ctx.op(S, lambda s, hp=hp: s.activation(
                    out=probs[hp][:], in_=pss[hp][:, 0:128], func=AF.Exp), "act")
                pss_free[hp] = e_n
                ctx.wait(V, "act", e_n)
                m_n = ctx.op(V, lambda v, hp=hp: v.tensor_tensor(
                    out=probs[hp][:], in0=probs[hp][:], in1=tri_sb[:], op=ALU.mult), "dve")
                ctx.wait(T, "dve", m_n)
                ctx.wait(T, "act", max(v_ready, pso3_free[hp]))
                pv_pe = ctx.op(T, lambda t, h=h, par=par, hp=hp: t.matmul(
                    pso3[hp][:, 0:65], probs[hp][:], vh[par][:, h, :],
                    start=True, stop=True), "pe")
                probs_free[hp] = pv_pe
                ctx.wait(V, "pe", pv_pe)
                ctx.wait(V, "act", rz_free[hp])
                r_n = ctx.op(V, lambda v, hp=hp: v.reciprocal(rz[hp][:], pso3[hp][:, 64:65]), "dve")
                ctx.wait(S, "dve", r_n)
                if ab >= 2:
                    ctx.wait(S, "pe", oh_free[par])
                pso3_free[hp] = ctx.op(S, lambda s, h=h, par=par, hp=hp: s.activation(
                    out=oh[par][:, h * 64:(h + 1) * 64], in_=pso3[hp][:, 0:64],
                    func=AF.Copy, scale=rz[hp][:]), "act")
                rz_free[hp] = pso3_free[hp]
            oh_ready = ctx.c["act"]

            ctx.wait(T, "act", oh_ready)
            for f4 in range(2):
                tp = f4 % 2
                ctx.wait(T, "dve", psT3_free[tp])
                for ff in range(4):
                    f = f4 * 4 + ff
                    pe_n = ctx.op(T, lambda t, par=par, f=f, ff=ff, tp=tp: t.transpose(
                        psT3[tp][:, ff * 128:(ff + 1) * 128],
                        oh[par][:, f * 128:(f + 1) * 128], id_sb[:]),
                        "pe" if ff == 3 else None)
                ctx.wait(V, "pe", pe_n)
                psT3_free[tp] = ctx.op(V, lambda v, par=par, f4=f4, tp=tp: v.tensor_copy(
                    qkvT[par][:, f4 * 4:(f4 + 1) * 4, :], psT3[tp][:]), "dve")
            ohT_ready = ctx.c["dve"]
            oh_free[par] = ctx.c["pe"]

            for wc in range(2):
                ctx.wait(T, "dve", max(ohT_ready, ones_ready))
                ctx.wait(T, "act", max(pswo_free, v_ready))
                for k in range(8):
                    ctx.op(T, lambda t, k=k, wc=wc, par=par: t.matmul(
                        psv[:], qkvT[par][:, k, :],
                        wo_sb[:, k, wc * 512:(wc + 1) * 512],
                        start=(k == 0), stop=False))
                mm = ctx.op(T, lambda t, wc=wc: t.matmul(
                    psv[:], ones_sb[:], bo_sb[:, wc * 512:(wc + 1) * 512],
                    start=False, stop=True), "pe")
                ctx.wait(S, "pe", mm)
                if ab >= 2:
                    ctx.wait(S, "pe", o2_free[par])
                pswo_free = ctx.op(S, lambda s, wc=wc, par=par: s.activation(
                    out=o2[par][:, wc * 512:(wc + 1) * 512], in_=psv[:],
                    func=AF.Copy), "act")
            o2_ready = ctx.c["act"]

            ctx.wait(T, "act", o2_ready)
            for f4 in range(2):
                tp = f4 % 2
                ctx.wait(T, "dve", psT3_free[tp])
                for ff in range(4):
                    f = f4 * 4 + ff
                    pe_n = ctx.op(T, lambda t, par=par, f=f, ff=ff, tp=tp: t.transpose(
                        psT3[tp][:, ff * 128:(ff + 1) * 128],
                        o2[par][:, f * 128:(f + 1) * 128], id_sb[:]),
                        "pe" if ff == 3 else None)
                ctx.wait(V, "pe", pe_n)
                psT3_free[tp] = ctx.op(V, lambda v, ab=ab, f4=f4, tp=tp: v.tensor_copy(
                    o2t[:, f4 * 4:(f4 + 1) * 4, ab * 128:(ab + 1) * 128], psT3[tp][:]), "dve")
            o2t_ready[ab] = ctx.c["dve"]
            o2_free[par] = ctx.c["pe"]
            qkvT_free[par] = ctx.c["pe"]
            qhT_free[par] = ctx.c["pe"]
            khT_free[par] = ctx.c["pe"]
            vh_free[par] = ctx.c["pe"]
        p3_pe = ctx.c["pe"]
        p3_dve = ctx.c["dve"]
        p3_act = ctx.c["act"]

        with ExitStack() as p4:
            c4 = sub(ctx, p4)
            pjs = [c4.sb(f"pjs{i}", [128, 8 * 512], BF16) for i in range(2)]
            xsl = [c4.sb(f"p_xsl{i}", [128, 512], F32) for i in range(4)]
            xo_sb = [c4.sb(f"p_xo{i}", [128, 512], F32) for i in range(2)]
            psp = [psv, pss[0]]

            pjs_ready = [None, None]
            pjs_free = [0, 0]
            ctx.wait(T, "act", p3_act)
            ctx.wait(T, "dve", p3_dve)
            ctx.wait(G, "pe", p3_pe)
            ctx.wait(G, "dve", p3_dve)
            ctx.wait(G, "act", p3_act)
            ctx.wait(G, "dw0")
            ctx.wait(G, "dw1")
            ctx.op(G, lambda g: g.dma_start(out=pjs[0][:], in_=projw[0]), "dw0", 16)
            pjs_ready[0] = ("dw0", ctx.c["dw0"])

            psp_free = [0, 0]
            xsl_free = [0] * 4
            xo_free = [0, 0]
            it = 0
            for oc in range(8):
                wp = oc % 2
                if oc + 1 < 8:
                    npar = (oc + 1) % 2
                    ctx.wait(G, "pe", pjs_free[npar])
                    ctx.op(G, lambda g, oc=oc, npar=npar: g.dma_start(
                        out=pjs[npar][:], in_=projw[oc + 1]), f"dw{npar}", 16)
                    pjs_ready[npar] = (f"dw{npar}", ctx.c[f"dw{npar}"])
                for tb in range(NTB):
                    par = it % 2
                    xp = it % 4
                    ctx.wait(Y, "dve", xsl_free[xp])
                    r0 = xin_row(tb * 128)
                    ctx.op(Y, lambda sy, oc=oc, xp=xp, r0=r0: sy.dma_start(
                        out=xsl[xp][:], in_=xin[r0:r0 + 128, oc * 512:(oc + 1) * 512]),
                        f"dxs{xp}", 16)
                    xsl_n = (f"dxs{xp}", ctx.c[f"dxs{xp}"])

                    ctx.wait(T, *pjs_ready[wp])
                    ctx.wait(T, "dve", max(o2t_ready[tb], psp_free[par], ones_ready))
                    for k in range(8):
                        ctx.op(T, lambda t, tb=tb, k=k, par=par, wp=wp: t.matmul(
                            psp[par][:], o2t[:, k, tb * 128:(tb + 1) * 128],
                            pjs[wp][:, k * 512:(k + 1) * 512],
                            start=(k == 0), stop=False))
                    mm_n = ctx.op(T, lambda t, oc=oc, par=par: t.matmul(
                        psp[par][:], ones_sb[:], projb_sb[:, oc * 512:(oc + 1) * 512],
                        start=False, stop=True), "pe")
                    if tb == NTB - 1:
                        pjs_free[wp] = mm_n

                    ctx.wait(V, "pe", mm_n)
                    ctx.wait(V, *xsl_n)
                    ctx.wait(V, f"dst{par}", xo_free[par])
                    ep_n = ctx.op(V, lambda v, par=par, xp=xp: v.tensor_tensor(
                        out=xo_sb[par][:], in0=psp[par][:], in1=xsl[xp][:],
                        op=ALU.add), "dve")
                    psp_free[par] = ep_n
                    xsl_free[xp] = ep_n
                    ctx.wait(Y, "dve", ep_n)
                    xo_free[par] = ctx.op(Y, lambda sy, tb=tb, oc=oc, par=par: sy.dma_start(
                        out=xout[tb * 128:(tb + 1) * 128, oc * 512:(oc + 1) * 512],
                        in_=xo_sb[par][:]), f"dst{par}", 16)
                    it += 1


def stage_l0attn_weights(inp):
    cw = np.asarray(inp["L0_attn_conv_w"], np.float32)
    convw = np.stack([pack_slabs(np.ascontiguousarray(cw[dt]), 512) for dt in range(3)])
    wq = np.ascontiguousarray(np.asarray(inp["L0_attn_wq"], np.float32).reshape(1024, 1024))
    wk = np.ascontiguousarray(np.asarray(inp["L0_attn_wk"], np.float32).reshape(1024, 1024))
    wv = np.ascontiguousarray(np.asarray(inp["L0_attn_wv"], np.float32).reshape(1024, 1024))
    wo = np.ascontiguousarray(np.asarray(inp["L0_attn_wo"], np.float32).reshape(1024, 1024))
    bq = np.asarray(inp["L0_attn_bq"], np.float32).reshape(1024)
    bk = np.asarray(inp["L0_attn_bk"], np.float32).reshape(1024)
    bv = np.asarray(inp["L0_attn_bv"], np.float32).reshape(1024)
    bo = np.asarray(inp["L0_attn_bo"], np.float32)
    pw = np.asarray(inp["L0_attn_proj_w"], np.float32)
    pb = np.asarray(inp["L0_attn_proj_b"], np.float32)
    cb = np.asarray(inp["L0_attn_conv_b"], np.float32)
    k_idx, q_idx = np.meshgrid(np.arange(128), np.arange(128), indexing="ij")
    tri_kq = (k_idx <= q_idx).astype(np.float32)
    return {
        "convw_s": convw,
        "convb": cb.reshape(1, 3072),
        "wq_r": wq, "wk_r": wk, "wv_r": wv, "wo_r": wo,
        "bq_c": np.ascontiguousarray(bq.reshape(8, 128).T),
        "bk_c": np.ascontiguousarray(bk.reshape(8, 128).T),
        "bv_r": bv.reshape(1, 1024),
        "bo_r": bo.reshape(1, 1024),
        "projw_s": pack_slabs(pw, 512),
        "projb": pb.reshape(1, D),
        "ident": np.eye(128, dtype=np.float32),
        "trimask": tri_kq,
    }


def slice_haloed(x_full, core):
    b, (a0, a1), (b0, b1) = chunk_ranges(core)
    out = np.zeros((TIN, D), x_full.dtype)
    out[1:513] = x_full[b, a0:a1]
    out[515:1027] = x_full[b, b0:b1]
    if a0 > 0:
        out[0] = x_full[b, a0 - 1]
    if a1 < 4096:
        out[513] = x_full[b, a1]
    if b0 > 0:
        out[514] = x_full[b, b0 - 1]
    if b1 < 4096:
        out[1027] = x_full[b, b1]
    return out


def build_l1ffn_program():
    nc = bass.Bass("TRN2", target_bir_lowering=False)
    xin = nc.dram_tensor("xin", [TLOC, D], F32, kind="ExternalInput").ap()
    a2 = nc.dram_tensor("a2", [128, 64], F32, kind="ExternalInput").ap()
    b2 = nc.dram_tensor("b2", [128, 64], F32, kind="ExternalInput").ap()
    betar = nc.dram_tensor("betar", [64, 64], F32, kind="ExternalInput").ap()
    gcols = nc.dram_tensor("gcols", [128, 32], F32, kind="ExternalInput").ap()
    biasr = nc.dram_tensor("biasr", [1, D], F32, kind="ExternalInput").ap()
    ident = nc.dram_tensor("ident", [128, 128], F32, kind="ExternalInput").ap()
    xout = nc.dram_tensor("xout", [TLOC, D], F32, kind="ExternalOutput").ap()
    cdram = nc.dram_tensor("c_scratch", [64, 64], F32).ap()

    with ExitStack() as es:
        ctx = Ctx(nc, es)
        emit_l1ffn(ctx, xin, xout, a2, b2, betar, gcols, biasr, ident, cdram)
        ctx.wait("sync", "dst0")
        ctx.wait("sync", "dst1")
        ctx.flush()
    return nc


def emit_l1ffn(ctx, xin, xout, a2, b2, betar, gcols, biasr, ident, cdram, pre="f_"):
    ctx.sems("dc", "dcy", "dcy2", "dx0", "dx1", "dw0", "pe", "act", "dve", "dst0", "dst1")
    T, V, S, G, Y = "tensor", "vector", "scalar", "gpsimd", "sync"

    id_sb = ctx.sb(pre + "id", [128, 128], BF16)
    a2_sb = ctx.sb(pre + "a2", [128, 64], BF16)
    b2_sb = ctx.sb(pre + "b2", [128, 64], BF16)
    betar_sb = ctx.sb(pre + "betar", [64, 64], BF16)
    gc_sb = ctx.sb(pre + "gc", [128, 32], F32)
    bias_bc = ctx.sb(pre + "biasbc", [128, D], F32)
    cb2 = ctx.sb(pre + "cb2", [128, D], BF16)
    m1_sb = ctx.sb(pre + "m1", [64, 64], BF16)
    m1t_sb = ctx.sb(pre + "m1t", [64, 64], BF16)
    c_sb = ctx.sb(pre + "c", [64, 64], F32)

    xb = [ctx.sb(pre + f"xb{i}", [128, D], F32) for i in range(2)]
    xn2 = [ctx.sb(pre + f"xn2{i}", [128, D], BF16) for i in range(2)]
    hT = [ctx.sb(pre + f"hT{i}", [128, 32, 128], BF16) for i in range(2)]
    u_sb = [ctx.sb(pre + f"u{i}", [128, D], BF16) for i in range(2)]
    uT = [ctx.sb(pre + f"uT{i}", [128, 32, 128], BF16) for i in range(2)]
    y_sb = [ctx.sb(pre + f"y{i}", [128, D], F32) for i in range(2)]
    g_sb = ctx.sb(pre + "g", [128, D], F32)
    e1_sb = ctx.sb(pre + "e1", [128, D], F32)
    xo_sb = y_sb
    st = alloc_stats(ctx, pre + "st", sq=xn2[0])
    r1c = ctx.sb(pre + "r1", [128, 1], F32)
    v2e = ctx.sb(pre + "v2e", [128, 1], F32)
    r2c = ctx.sb(pre + "r2", [128, 1], F32)
    rstd2 = ctx.sb(pre + "rstd2", [128, 1], F32)
    s2col = ctx.sb(pre + "s2col", [128, 1], F32)
    Rc = ctx.sb(pre + "R", [128, 1], F32)
    nmR = ctx.sb(pre + "nmR", [128, 1], F32)

    psT = [ctx.ps(pre + f"psT{i}", [128, 512], BF16) for i in range(2)]
    psu = [ctx.ps(pre + f"psu{i}", [128, 64]) for i in range(2)]
    psy = [ctx.ps(pre + f"psy{i}", [128, 64]) for i in range(2)]
    psc1 = ctx.ps(pre + "psc1", [64, 64])
    psc2 = ctx.ps(pre + "psc2", [64, 64], BF16)

    for fn in (
        lambda g: g.dma_start(out=id_sb[:], in_=ident[:]),
        lambda g: g.dma_start(out=a2_sb[:], in_=a2[:]),
        lambda g: g.dma_start(out=b2_sb[:], in_=b2[:]),
        lambda g: g.dma_start(out=betar_sb[:], in_=betar[:]),
        lambda g: g.dma_start(out=gc_sb[:], in_=gcols[:]),
    ):
        ctx.wait(G, "dc")
        ctx.op(G, fn, "dc", 16)
    consts = ctx.c["dc"]
    ctx.op(Y, lambda sy: sy.dma_start(out=bias_bc[:], in_=biasr.broadcast_to([128, D])), "dcy", 16)
    bias_ready = ("dcy", ctx.c["dcy"])

    ctx.wait(T, "dc", consts)
    n = ctx.op(T, lambda t: t.matmul(psc1[:], b2_sb[0:64, :], betar_sb[:],
                                     start=True, stop=True), "pe")
    ctx.wait(V, "pe", n)
    n = ctx.op(V, lambda v: v.tensor_copy(m1_sb[:], psc1[:]), "dve")
    ctx.wait(T, "dve", n)
    n = ctx.op(T, lambda t: t.transpose(psc2[:], m1_sb[:], id_sb[0:64, 0:64]), "pe")
    ctx.wait(V, "pe", n)
    n = ctx.op(V, lambda v: v.tensor_copy(m1t_sb[:], psc2[:]), "dve")
    ctx.wait(T, "dve", n)
    n = ctx.op(T, lambda t: t.matmul(psc1[:], m1t_sb[:], a2_sb[0:64, :],
                                     start=True, stop=True), "pe")
    ctx.wait(V, "pe", n)
    n = ctx.op(V, lambda v: v.tensor_copy(c_sb[:], psc1[:]), "dve")
    ctx.wait(Y, "dve", n)
    ctx.op(Y, lambda sy: sy.dma_start(out=cdram[:], in_=c_sb[:]), "dcy2", 16)
    ctx.wait(G, "dcy2")
    ctx.wait(G, "dc", consts)
    ctx.op(G, lambda g: g.dma_start(
        out=cb2[:], in_=cdram.rearrange("q b -> (q b)").unsqueeze(0).broadcast_to([128, D])), "dc", 16)
    cb2_ready = ("dc", ctx.c["dc"])

    xb_ready = [None] * NTB
    for i in range(2):
        ctx.op(Y, lambda sy, i=i: sy.dma_start(
            out=xb[i][:], in_=xin[i * 128:(i + 1) * 128, :]), f"dx{i % 2}", 16)
        xb_ready[i] = (f"dx{i % 2}", ctx.c[f"dx{i % 2}"])

    xn2_free = [0, 0]
    hT_free = [0, 0]
    uT_free = [0, 0]
    u_free = [0, 0]
    psT_free = [0, 0]
    psT_free_sem = [None, None]
    psu_free = [0, 0]
    psy_free = [0, 0]
    y_free = [0, 0]
    xo_free = [0, 0]
    xb_free = [0, 0]
    xn2_freed_dve = [0, 0]

    for tb in range(NTB):
        par = tb % 2
        if tb >= 2:
            ctx.wait(S, "pe", xn2_free[par])
            ctx.wait(S, "dve", xn2_freed_dve[par])
        emit_ln_stats(ctx, xb[par], st, sq=xn2[par], ready=xb_ready[tb])
        ctx.wait(V, "dve")
        ctx.op(V, lambda v: v.tensor_scalar(out=v2e[:], in0=st["rvar"][:],
                                            scalar1=-LN_EPS, scalar2=1.0 + LN_EPS,
                                            op0=ALU.mult, op1=ALU.add), "dve")
        ctx.wait(V, "dve")
        ctx.op(V, lambda v: v.reciprocal(r2c[:], v2e[:]), "dve")
        ctx.wait(S, "dve")
        ctx.op(S, lambda s: s.activation(out=rstd2[:], in_=r2c[:], func=AF.Sqrt), "act")
        ctx.op(S, lambda s: s.activation(out=s2col[:], in_=v2e[:], func=AF.Sqrt), "act")
        ctx.wait(V, "act")
        ctx.op(V, lambda v: v.tensor_tensor(out=Rc[:], in0=st["rstd"][:],
                                            in1=rstd2[:], op=ALU.mult), "dve")
        ctx.wait(V, "dve")
        ctx.op(V, lambda v: v.scalar_tensor_tensor(
            out=nmR[:], in0=st["mu"][:], scalar=-1.0, in1=Rc[:],
            op0=ALU.mult, op1=ALU.mult), "dve")
        ctx.wait(V, "dve")
        if tb >= 2:
            ctx.wait(V, "pe", xn2_free[par])
        xn2_n = ctx.op(V, lambda v, par=par: v.tensor_scalar(
            out=xn2[par][:], in0=xb[par][:], scalar1=Rc[:], scalar2=nmR[:],
            op0=ALU.mult, op1=ALU.add), "dve")

        ctx.wait(T, "dc", consts)
        ctx.wait(T, "dve", xn2_n)
        for f4 in range(8):
            tp = f4 % 2
            if psT_free_sem[tp]:
                ctx.wait(T, psT_free_sem[tp], psT_free[tp])
            for ff in range(4):
                f = f4 * 4 + ff
                pe_n = ctx.op(T, lambda t, par=par, tp=tp, f=f, ff=ff: t.transpose(
                    psT[tp][:, ff * 128:(ff + 1) * 128],
                    xn2[par][:, f * 128:(f + 1) * 128], id_sb[:]),
                    "pe" if ff == 3 else None)
            ctx.wait(V, "pe", pe_n)
            if tb >= 2:
                ctx.wait(V, "pe", hT_free[par])
            for ff in range(4):
                f = f4 * 4 + ff
                n = ctx.op(V, lambda v, par=par, tp=tp, f=f, ff=ff: v.tensor_scalar(
                    out=hT[par][:, f, :], in0=psT[tp][:, ff * 128:(ff + 1) * 128],
                    scalar1=gc_sb[:, f:f + 1], scalar2=None, op0=ALU.mult), "dve")
            psT_free[tp] = n
            psT_free_sem[tp] = "dve"
        xn2_free[par] = ctx.c["pe"]
        hT_ready = ctx.c["dve"]

        ctx.wait(T, "dve", hT_ready)
        for p in range(64):
            pp = p % 2
            base = 64 * (p % 2)
            f = p // 2
            ctx.wait(T, "dve", psu_free[pp])
            mm = ctx.op(T, lambda t, par=par, f=f, base=base, pp=pp: t.matmul(
                psu[pp][:], hT[par][base:base + 64, f, :], a2_sb[base:base + 64, :],
                start=True, stop=True), "pe")
            ctx.wait(V, "pe", mm)
            psu_free[pp] = ctx.op(V, lambda v, par=par, p=p, pp=pp: v.tensor_copy(
                u_sb[par][:, p::64], psu[pp][:]), "dve")
        u_ready = ctx.c["dve"]

        ctx.wait(T, "dve", u_ready)
        for f4 in range(8):
            tp = f4 % 2
            ctx.wait(T, psT_free_sem[tp], psT_free[tp])
            for ff in range(4):
                f = f4 * 4 + ff
                pe_n = ctx.op(T, lambda t, par=par, tp=tp, f=f, ff=ff: t.transpose(
                    psT[tp][:, ff * 128:(ff + 1) * 128],
                    u_sb[par][:, f * 128:(f + 1) * 128], id_sb[:]),
                    "pe" if ff == 3 else None)
            ctx.wait(V, "pe", pe_n)
            if tb >= 2:
                ctx.wait(V, "pe", uT_free[par])
            psT_free[tp] = ctx.op(V, lambda v, par=par, tp=tp, f4=f4: v.tensor_copy(
                uT[par][:, f4 * 4:(f4 + 1) * 4, :], psT[tp][:]), "dve")
            psT_free_sem[tp] = "dve"
        uT_ready = ctx.c["dve"]
        u_freed_dve_cur = ctx.c["pe"]

        ctx.wait(T, "dve", uT_ready)
        ctx.wait(V, *cb2_ready)
        for b in range(64):
            pp = b % 2
            base = 64 * (b % 2)
            f = b // 2
            ctx.wait(T, "dve", psy_free[pp])
            mm = ctx.op(T, lambda t, par=par, f=f, base=base, pp=pp: t.matmul(
                psy[pp][:], uT[par][base:base + 64, f, :], b2_sb[base:base + 64, :],
                start=True, stop=True), "pe")
            ctx.wait(V, "pe", mm)
            if b < 2 and tb >= 2:
                ctx.wait(V, "act", y_free[par])
                ctx.wait(V, f"dst{par}", xo_free[par])
            psy_free[pp] = ctx.op(V, lambda v, par=par, b=b, pp=pp: v.scalar_tensor_tensor(
                out=y_sb[par][:, b::64], in0=psy[pp][:], scalar=1.0,
                in1=cb2[:, b::64], op0=ALU.mult, op1=ALU.add), "dve")
        y_ready = ctx.c["dve"]

        ctx.wait(S, "dve", y_ready)
        gelu_n = ctx.op(S, lambda s, par=par: s.activation(
            out=g_sb[:], in_=y_sb[par][:], func=AF.Gelu), "act")
        y_free[par] = gelu_n
        ctx.wait(V, "act", gelu_n)
        ctx.wait(V, *bias_ready)
        e1_n = ctx.op(V, lambda v: v.tensor_tensor(
            out=e1_sb[:], in0=g_sb[:], in1=bias_bc[:], op=ALU.add), "dve")
        ctx.wait(V, "dve", e1_n)
        e2_n = ctx.op(V, lambda v, par=par: v.scalar_tensor_tensor(
            out=e1_sb[:], in0=xn2[par][:], scalar=s2col[:], in1=e1_sb[:],
            op0=ALU.mult, op1=ALU.add), "dve")
        ctx.wait(V, "dve", e2_n)
        e3_n = ctx.op(V, lambda v, par=par: v.scalar_tensor_tensor(
            out=xo_sb[par][:], in0=e1_sb[:], scalar=0.5, in1=xb[par][:],
            op0=ALU.mult, op1=ALU.add), "dve")
        xb_free[par] = e3_n
        ctx.wait(Y, "dve", e3_n)
        xo_free[par] = ctx.op(Y, lambda sy, tb=tb, par=par: sy.dma_start(
            out=xout[tb * 128:(tb + 1) * 128, :], in_=xo_sb[par][:]), f"dst{par}", 16)

        if tb + 2 < NTB:
            ctx.wait(Y, "dve", e3_n)
            ctx.op(Y, lambda sy, tb=tb, par=par: sy.dma_start(
                out=xb[par][:], in_=xin[(tb + 2) * 128:(tb + 3) * 128, :]), f"dx{par}", 16)
            xb_ready[tb + 2] = (f"dx{par}", ctx.c[f"dx{par}"])

        hT_free[par] = ctx.c["pe"]
        uT_free[par] = ctx.c["pe"]
        xn2_freed_dve[par] = e2_n


def stage_l1ffn_weights(inp):
    A = np.asarray(inp["L1_ffn_A"], np.float32)
    B = np.asarray(inp["L1_ffn_B"], np.float32)
    beta = np.asarray(inp["L1_ffn_ln_beta"], np.float32)
    gamma = np.asarray(inp["L1_ffn_ln_gamma"], np.float32)
    bias = np.asarray(inp["L1_ffn_bias"], np.float32)
    return {
        "a2": np.ascontiguousarray(np.concatenate([A, A], axis=0)),
        "b2": np.ascontiguousarray(np.concatenate([B, B], axis=0)),
        "betar": np.ascontiguousarray(beta.reshape(64, 64)),
        "gcols": np.ascontiguousarray(gamma.reshape(32, 128).T),
        "biasr": bias.reshape(1, D),
        "ident": np.eye(128, dtype=np.float32),
    }


def emit_l1attn_a(ctx, xin, wqs, wks, wvs, cs, ident, qt, kt, vhm,
                  kt2d=None, vhm2d=None, agk=None, agv=None):
    ctx.sems("dc", "dx0", "dx1", "dw0", "dw1", "pe", "act", "dve", "dst0", "dst1")
    T, V, S, G, Y = "tensor", "vector", "scalar", "gpsimd", "sync"

    id_sb = ctx.sb("q_id", [128, 128], BF16)
    cs_sb = ctx.sb("q_cs", [128, NTB, 6], F32)
    xnt = ctx.sb("q_xnt", [128, 32, TLOC], BF16)
    wsl = [ctx.sb(f"q_wsl{i}", [128, 32 * 512], BF16) for i in range(2)]
    ro_sb = [ctx.sb(f"q_ro{i}", [128, 512], BF16) for i in range(2)]
    vo_sb = [ctx.sb(f"q_vo{i}", [128, 4, 129], BF16) for i in range(2)]
    tt_sb = [ctx.sb(f"q_tt{i}", [128, 4, 128], BF16) for i in range(2)]
    for i in range(2):
        ctx.op("vector", lambda v, i=i: v.memset(vo_sb[i][:, :, 128:129], 1.0), "dve")
    vo_ones = ctx.c["dve"]

    psT = [ctx.ps(f"q_psT{i}", [128, 512], BF16) for i in range(2)]
    pso = [ctx.ps(f"q_pso{i}", [128, 512]) for i in range(2)]

    ctx.wait(G, "dc")
    ctx.op(G, lambda g: g.dma_start(out=id_sb[:], in_=ident[:]), "dc", 16)
    ctx.wait(G, "dc")
    ctx.op(G, lambda g: g.dma_start(
        out=cs_sb[:], in_=cs.rearrange("(tb p) c -> p tb c", p=128)), "dc", 16)
    consts = ctx.c["dc"]

    with ExitStack() as p1:
        c1 = sub(ctx, p1)
        xb = [c1.sb(f"q_xb{i}", [128, D], F32) for i in range(2)]
        xnb = [c1.sb(f"q_xnb{i}", [128, D], BF16) for i in range(2)]
        st = alloc_stats(c1, "q_st", sq=xnb[0])

        xb_ready = [None] * NTB
        for i in range(2):
            ctx.op(Y, lambda sy, i=i: sy.dma_start(
                out=xb[i][:], in_=xin[i * 128:(i + 1) * 128, :]), f"dx{i % 2}", 16)
            xb_ready[i] = (f"dx{i % 2}", ctx.c[f"dx{i % 2}"])
        xnb_free = [0, 0]
        psT_free = [0, 0]
        for i in range(NTB):
            par = i % 2
            if i >= 2:
                ctx.wait(S, "pe", xnb_free[par])
            emit_ln_stats(ctx, xb[par], st, sq=xnb[par], ready=xb_ready[i])
            ctx.wait(V, "dve")
            if i >= 2:
                ctx.wait(V, "pe", xnb_free[par])
            ln_apply = ctx.op(V, lambda v, par=par: v.tensor_scalar(
                out=xnb[par][:], in0=xb[par][:], scalar1=st["rstd"][:],
                scalar2=st["nmr"][:], op0=ALU.mult, op1=ALU.add), "dve")
            if i + 2 < NTB:
                ctx.wait(Y, "dve", ln_apply)
                ctx.op(Y, lambda sy, i=i, par=par: sy.dma_start(
                    out=xb[par][:], in_=xin[(i + 2) * 128:(i + 3) * 128, :]), f"dx{par}", 16)
                xb_ready[i + 2] = (f"dx{par}", ctx.c[f"dx{par}"])
            ctx.wait(T, "dc", consts)
            ctx.wait(T, "dve", ln_apply)
            for f4 in range(8):
                tp = f4 % 2
                ctx.wait(T, "dve", psT_free[tp])
                for ff in range(4):
                    f = f4 * 4 + ff
                    pe_n = ctx.op(T, lambda t, par=par, tp=tp, f=f, ff=ff: t.transpose(
                        psT[tp][:, ff * 128:(ff + 1) * 128],
                        xnb[par][:, f * 128:(f + 1) * 128], id_sb[:]),
                        "pe" if ff == 3 else None)
                ctx.wait(V, "pe", pe_n)
                psT_free[tp] = ctx.op(V, lambda v, i=i, tp=tp, f4=f4: v.tensor_copy(
                    xnt[:, f4 * 4:(f4 + 1) * 4, i * 128:(i + 1) * 128], psT[tp][:]), "dve")
            xnb_free[par] = ctx.c["pe"]
        p1_pe = ctx.c["pe"]
        p1_dve = ctx.c["dve"]
        p1_act = ctx.c["act"]

    wsl_ready = [None, None]
    wsl_free = [0, 0]
    pso_free = [0, 0]
    ro_free = [0, 0]
    ro_free_sem = [None, None]
    vo_free = [0, 0]
    vo_free_sem = [None, None]
    tt_free = [0, 0]
    psT2_free = [0, 0]

    ctx.wait(G, "pe", p1_pe)
    ctx.wait(G, "dve", p1_dve)
    ctx.wait(G, "act", p1_act)

    sl = 0
    for wi, (wsrc, dst) in enumerate(((wqs, qt), (wks, kt), (wvs, None))):
        is_v = wi == 2
        csel = (0, 3, None)[wi]
        for oc in range(8):
            par = sl % 2
            if sl == 0:
                ctx.wait(G, "dc", consts)
                ctx.op(G, lambda g: g.dma_start(out=wsl[0][:], in_=wqs[0]), "dw0", 16)
                wsl_ready[0] = ("dw0", ctx.c["dw0"])
            if sl + 1 < 24:
                npar = (sl + 1) % 2
                nwi, noc = divmod(sl + 1, 8)
                nsrc = (wqs, wks, wvs)[nwi]
                ctx.wait(G, "pe", wsl_free[npar])
                ctx.op(G, lambda g, nsrc=nsrc, noc=noc, npar=npar: g.dma_start(
                    out=wsl[npar][:], in_=nsrc[noc]), f"dw{npar}", 16)
                wsl_ready[npar] = (f"dw{npar}", ctx.c[f"dw{npar}"])

            for tb in range(NTB):
                pp = (sl * NTB + tb) % 2
                ctx.wait(T, *wsl_ready[par])
                ctx.wait(T, "dve", pso_free[pp])
                for k in range(32):
                    mm = ctx.op(T, lambda t, tb=tb, k=k, par=par, pp=pp: t.matmul(
                        pso[pp][:], xnt[:, k, tb * 128:(tb + 1) * 128],
                        wsl[par][:, k * 512:(k + 1) * 512],
                        start=(k == 0), stop=(k == 31)), "pe" if k == 31 else None)
                if tb == NTB - 1:
                    wsl_free[par] = mm

                if is_v:
                    ctx.wait(V, "pe", mm)
                    rpar = pp
                    ctx.wait(V, "dve", vo_ones)
                    if vo_free_sem[rpar]:
                        ctx.wait(V, vo_free_sem[rpar], vo_free[rpar])
                    ev = ctx.op(V, lambda v, rpar=rpar, pp=pp: v.tensor_copy(
                        vo_sb[rpar][:, :, 0:128], pso[pp][:]), "dve")
                    pso_free[pp] = ev
                    ctx.wait(Y, "dve", ev)
                    for hh in range(4):
                        row = (oc * 4 + hh) * 8 + tb
                        ctx.op(Y, lambda sy, row=row, hh=hh, rpar=rpar: sy.dma_start(
                            out=vhm[row], in_=vo_sb[rpar][:, hh, :]),
                            f"dst{rpar}", 16)
                    vo_free[rpar] = ctx.c[f"dst{rpar}"]
                    vo_free_sem[rpar] = f"dst{rpar}"
                else:
                    rpar = pp
                    ctx.wait(V, "pe", mm)
                    if ro_free_sem[rpar]:
                        ctx.wait(V, ro_free_sem[rpar], ro_free[rpar])
                    cc = cs_sb[:, tb, csel:csel + 1]
                    ss_ = cs_sb[:, tb, csel + 1:csel + 2]
                    ns_ = cs_sb[:, tb, csel + 2:csel + 3]
                    ctx.wait(V, "dc", consts)
                    ctx.op(V, lambda v, rpar=rpar, pp=pp, cc=cc: v.tensor_scalar(
                        out=ro_sb[rpar][:, 0:512:2], in0=pso[pp][:, 0:512:2],
                        scalar1=cc, scalar2=None, op0=ALU.mult), "dve")
                    ctx.op(V, lambda v, rpar=rpar, pp=pp, cc=cc: v.tensor_scalar(
                        out=ro_sb[rpar][:, 1:512:2], in0=pso[pp][:, 1:512:2],
                        scalar1=cc, scalar2=None, op0=ALU.mult), "dve")
                    ctx.wait(V, "dve")
                    ctx.op(V, lambda v, rpar=rpar, pp=pp, ss_=ss_: v.scalar_tensor_tensor(
                        out=ro_sb[rpar][:, 1:512:2], in0=pso[pp][:, 0:512:2],
                        scalar=ss_, in1=ro_sb[rpar][:, 1:512:2],
                        op0=ALU.mult, op1=ALU.add), "dve")
                    ctx.wait(V, "dve")
                    ev = ctx.op(V, lambda v, rpar=rpar, pp=pp, ns_=ns_: v.scalar_tensor_tensor(
                        out=ro_sb[rpar][:, 0:512:2], in0=pso[pp][:, 1:512:2],
                        scalar=ns_, in1=ro_sb[rpar][:, 0:512:2],
                        op0=ALU.mult, op1=ALU.add), "dve")
                    pso_free[pp] = ev
                    tp = (sl * NTB + tb) % 2
                    ctx.wait(T, "dve", max(ev, psT2_free[tp]))
                    for hh in range(4):
                        pe_n = ctx.op(T, lambda t, rpar=rpar, hh=hh, tp=tp: t.transpose(
                            psT[tp][:, hh * 128:(hh + 1) * 128],
                            ro_sb[rpar][:, hh * 128:(hh + 1) * 128],
                            id_sb[:]), "pe" if hh == 3 else None)
                    ctx.wait(V, "pe", pe_n)
                    if tt_free[tp]:
                        ctx.wait(V, f"dst{tp}", tt_free[tp])
                    psT2_free[tp] = ctx.op(V, lambda v, tp=tp: v.tensor_copy(
                        tt_sb[tp][:], psT[tp][:]), "dve")
                    ctx.wait(Y, "dve", psT2_free[tp])
                    tt_free[tp] = ctx.op(Y, lambda sy, dst=dst, oc=oc,
                                         tb=tb, tp=tp: sy.dma_start(
                        out=dst[0:128, oc * 4:(oc + 1) * 4, tb * 128:(tb + 1) * 128],
                        in_=tt_sb[tp][:]), f"dst{tp}", 16)
                    ro_free[rpar] = ctx.c["pe"]
                    ro_free_sem[rpar] = "pe"
            sl += 1

    if agk is not None:
        ctx.sems("cc")
        rg = [[0, 1, 2, 3], [4, 5, 6, 7]]
        ctx.wait(G, "dst0")
        ctx.wait(G, "dst1")
        ctx.op(G, lambda g: g.collective_compute(
            "AllGather", ALU.bypass, replica_groups=rg,
            ins=[kt2d], outs=[agk]), "cc", 1)
        ctx.op(G, lambda g: g.collective_compute(
            "AllGather", ALU.bypass, replica_groups=rg,
            ins=[vhm2d], outs=[agv]), "cc", 1)


def stage_l1attn_a_weights(inp):
    wq = np.asarray(inp["L1_attn_wq"], np.float32)
    wk = np.asarray(inp["L1_attn_wk"], np.float32)
    wv = np.asarray(inp["L1_attn_wv"], np.float32)
    return {
        "wq_s": pack_slabs(np.ascontiguousarray(wq.T), 512),
        "wk_s": pack_slabs(np.ascontiguousarray(wk.T), 512),
        "wv_s": pack_slabs(np.ascontiguousarray(wv.T), 512),
        "ident": np.eye(128, dtype=np.float32),
    }


def cs_cols_for(core):
    b, (a0, a1), (b0, b1) = chunk_ranges(core)
    pos = np.concatenate([np.arange(a0, a1), np.arange(b0, b1)]).astype(np.float32)
    c = np.cos(pos)
    s = np.sin(pos)
    sc = 1.0 / np.sqrt(128.0)
    return np.stack([c * sc, s * sc, -s * sc, c, s, -s], axis=1).astype(np.float32)


NKA, NKB = 12, 28


def unit_owner(u):
    o = u if u < 4 else 7 - u
    return o, (0 if u < 4 else 512), (0 if u < 4 else 4)


def emit_l1attn_b(ctx, xin, qt, ktloc, vhmloc, agk, agv, wos,
                  ident, trimask, vmka, vmkb, dmask, xout):
    ctx.sems("dc", "kv0", "kv1", "dxs0", "dxs1", "dxs2", "dxs3",
             "dw0", "dw1", "pe", "act", "dve", "dst0", "dst1")
    T, V, S, G, Y = "tensor", "vector", "scalar", "gpsimd", "sync"

    id_sb = ctx.sb("b_id", [128, 128], BF16)
    tri_sb = ctx.sb("b_tri", [128, 128], BF16)
    ot_sb = ctx.sb("b_ot", [128, 32, TLOC], BF16)

    agk3 = agk.rearrange("r (h t) -> r h t", h=32)
    agv3 = agv.rearrange("r (p e) -> r p e", p=128)

    ctx.wait(G, "dc")
    ctx.op(G, lambda g: g.dma_start(out=id_sb[:], in_=ident[:]), "dc", 16)
    ctx.wait(G, "dc")
    ctx.op(G, lambda g: g.dma_start(out=tri_sb[:], in_=trimask[:]), "dc", 16)

    ot_ready = [0] * NTB

    with ExitStack() as p1:
        c1 = sub(ctx, p1)
        qt_sb = c1.sb("b_qt", [128, 32, TLOC], BF16)
        mka_sb = c1.sb("b_mka", [128, NKA * 129], BF16)
        mkb_sb = c1.sb("b_mkb", [128, NKB * 129], BF16)
        ctx.wait(G, "dc")
        ctx.op(G, lambda g: g.dma_start(out=mka_sb[:], in_=vmka.broadcast_to([128, NKA * 129])), "dc", 16)
        ctx.wait(G, "dc")
        ctx.op(G, lambda g: g.dma_start(out=mkb_sb[:], in_=vmkb.broadcast_to([128, NKB * 129])), "dc", 16)
        consts = ctx.c["dc"]
        ctx.op(Y, lambda sy: sy.dma_start(out=qt_sb[:], in_=qt[:]), "dst0", 16)
        qt_ready = ("dst0", ctx.c["dst0"])
        ka = [c1.sb(f"b_ka{i}", [128, NKA * 128], BF16) for i in range(2)]
        kb_ = [c1.sb(f"b_kb{i}", [128, NKB * 128], BF16) for i in range(2)]
        kd = [c1.sb(f"b_kd{i}", [128, TLOC], BF16) for i in range(2)]
        va = [c1.sb(f"b_va{i}", [128, NKA, 129], BF16) for i in range(2)]
        vb = [c1.sb(f"b_vb{i}", [128, NKB, 129], BF16) for i in range(2)]
        vd = [c1.sb(f"b_vd{i}", [128, 8, 129], BF16) for i in range(2)]
        probs = [c1.sb(f"b_pr{i}", [128, 512], BF16) for i in range(2)]
        dmask_sb = c1.sb("b_dmask", [128, 4, 512], BF16)
        rz = [c1.sb(f"b_rz{i}", [128, 1], F32) for i in range(2)]
        oh = [c1.sb(f"b_oh{i}", [128, 128], BF16) for i in range(2)]

        pss = [c1.ps(f"b_pss{i}", [128, 512]) for i in range(2)]
        pv = [c1.ps(f"b_pv{i}", [128, 129]) for i in range(4)]
        psT = [c1.ps(f"b_psT{i}", [128, 128], BF16) for i in range(2)]

        ctx.wait(G, "dc")
        ctx.op(G, lambda g: g.dma_start(out=dmask_sb[:], in_=dmask[:]), "dc", 16)
        consts2 = ctx.c["dc"]
        kv_ready = [None, None]
        kv_free = [0, 0]
        pss_free = [0, 0]
        probs_free = [0, 0]
        pv_free = [0, 0, 0, 0]
        rz_free = [0, 0]
        oh_free = [0, 0]
        psT_free = [0, 0]

        vmask_done = [0, 0]

        def load_head(h, par):
            ctx.wait(Y, "pe", kv_free[par])
            ctx.wait(Y, "dve", vmask_done[par])
            for u in range(NKA // 4):
                o, loff, _ = unit_owner(u)
                ctx.op(Y, lambda sy, o=o, loff=loff, u=u, h=h, par=par: sy.dma_start(
                    out=ka[par][:, u * 512:(u + 1) * 512],
                    in_=agk3[o * 128:(o + 1) * 128, h, loff:loff + 512]), f"kv{par}", 16)
            for u in range(NKB // 4):
                o, loff, _ = unit_owner(u)
                ctx.op(Y, lambda sy, o=o, loff=loff, u=u, h=h, par=par: sy.dma_start(
                    out=kb_[par][:, u * 512:(u + 1) * 512],
                    in_=agk3[o * 128:(o + 1) * 128, h, loff:loff + 512]), f"kv{par}", 16)
            ctx.op(Y, lambda sy, h=h, par=par: sy.dma_start(
                out=kd[par][:], in_=ktloc[:, h, :]), f"kv{par}", 16)
            for u in range(NKA // 4):
                o, _, lb0 = unit_owner(u)
                r0 = o * 256 + h * 8 + lb0
                ctx.op(Y, lambda sy, r0=r0, u=u, par=par: sy.dma_start(
                    out=va[par][:, u * 4:(u + 1) * 4, :],
                    in_=agv3[r0:r0 + 4].rearrange("r p e -> p r e")), f"kv{par}", 16)
            for u in range(NKB // 4):
                o, _, lb0 = unit_owner(u)
                r0 = o * 256 + h * 8 + lb0
                ctx.op(Y, lambda sy, r0=r0, u=u, par=par: sy.dma_start(
                    out=vb[par][:, u * 4:(u + 1) * 4, :],
                    in_=agv3[r0:r0 + 4].rearrange("r p e -> p r e")), f"kv{par}", 16)
            ctx.op(Y, lambda sy, h=h, par=par: sy.dma_start(
                out=vd[par][:],
                in_=vhmloc[h * 8:(h + 1) * 8].rearrange("r p e -> p r e")), f"kv{par}", 16)
            kv_ready[par] = (f"kv{par}", ctx.c[f"kv{par}"])
            ctx.wait(V, *kv_ready[par])
            ctx.wait(V, "dc", consts)
            ctx.op(V, lambda v, par=par: v.tensor_tensor(
                out=va[par][:], in0=va[par][:], in1=mka_sb[:], op=ALU.mult), "dve")
            vmask_done[par] = ctx.op(V, lambda v, par=par: v.tensor_tensor(
                out=vb[par][:], in0=vb[par][:], in1=mkb_sb[:], op=ALU.mult), "dve")

        load_head(0, 0)
        for h in range(32):
            par = h % 2
            if h + 1 < 32:
                load_head(h + 1, (h + 1) % 2)
            ctx.wait(T, *kv_ready[par])
            ctx.wait(T, "dve", vmask_done[par])
            ctx.wait(T, *qt_ready)
            ctx.wait(T, "dc", consts2)
            for chunk in range(2):
                nkb = (NKA, NKB)[chunk]
                ksrc = (ka, kb_)[chunk][par]
                vsrc = (va, vb)[chunk][par]
                qc0 = chunk * 512
                nke = nkb + 4
                for kbi in range(nke):
                    dj = kbi - nkb
                    gp = kbi % 2
                    ctx.wait(T, "act", pss_free[gp])
                    if dj < 0:
                        kap = lambda t, kbi=kbi, ksrc=ksrc: ksrc[:, kbi * 128:(kbi + 1) * 128]
                    else:
                        kap = lambda t, lk=chunk * 4 + dj, par=par: kd[par][:, lk * 128:(lk + 1) * 128]
                    ctx.op(T, lambda t, kap=kap, gp=gp, qc0=qc0, h=h:
                           t.matmul(pss[gp][:], kap(t),
                                    qt_sb[:, h, qc0:qc0 + 512],
                                    start=True, stop=True), "pe")
                    smm = ctx.c["pe"]
                    ctx.wait(S, "pe", max(smm, probs_free[gp]))
                    e_n = ctx.op(S, lambda s, gp=gp: s.activation(
                        out=probs[gp][:], in_=pss[gp][:], func=AF.Exp), "act")
                    pss_free[gp] = e_n
                    if dj >= 0:
                        ctx.wait(V, "act", e_n)
                        m_n = ctx.op(V, lambda v, gp=gp, dj=dj: v.tensor_tensor(
                            out=probs[gp][:], in0=probs[gp][:],
                            in1=dmask_sb[:, dj, :], op=ALU.mult), "dve")
                        ctx.wait(T, "dve", m_n)
                    else:
                        ctx.wait(T, "act", e_n)
                    if kbi == 0:
                        ctx.wait(T, "act", max(pv_free[0], pv_free[1],
                                               pv_free[2], pv_free[3]))
                    for qb in range(4):
                        if dj < 0:
                            vap = lambda t, kbi=kbi, vsrc=vsrc: vsrc[:, kbi, :]
                        else:
                            vap = lambda t, lk=chunk * 4 + dj, par=par: vd[par][:, lk, :]
                        ctx.op(T, lambda t, vap=vap, qb=qb, gp=gp, kbi=kbi, nke=nke:
                               t.matmul(pv[qb][:],
                                        probs[gp][:, qb * 128:(qb + 1) * 128],
                                        vap(t),
                                        start=(kbi == 0), stop=(kbi == nke - 1)),
                               "pe" if qb == 3 else None)
                    probs_free[gp] = ctx.c["pe"]

                pv_mm = ctx.c["pe"]
                for qb in range(4):
                    qp = qb % 2
                    ctx.wait(V, "pe", pv_mm)
                    ctx.wait(V, "act", rz_free[qp])
                    r_n = ctx.op(V, lambda v, qb=qb, qp=qp: v.reciprocal(
                        rz[qp][:], pv[qb][:, 128:129]), "dve")
                    ctx.wait(S, "dve", r_n)
                    ctx.wait(S, "pe", oh_free[qp])
                    ev_n = ctx.op(S, lambda s, qb=qb, qp=qp: s.activation(
                        out=oh[qp][:], in_=pv[qb][:, 0:128],
                        func=AF.Copy, scale=rz[qp][:]), "act")
                    rz_free[qp] = ev_n
                    pv_free[qb] = ev_n
                    ctx.wait(T, "act", ev_n)
                    ctx.wait(T, "dve", psT_free[qp])
                    tp_n = ctx.op(T, lambda t, qp=qp: t.transpose(
                        psT[qp][:], oh[qp][:], id_sb[:]), "pe")
                    oh_free[qp] = tp_n
                    ctx.wait(V, "pe", tp_n)
                    psT_free[qp] = ctx.op(V, lambda v, h=h, qc0=qc0, qb=qb, qp=qp: v.tensor_copy(
                        ot_sb[:, h, qc0 + qb * 128:qc0 + (qb + 1) * 128], psT[qp][:]), "dve")
            kv_free[par] = ctx.c["pe"]
        p1_pe = ctx.c["pe"]
        p1_dve = ctx.c["dve"]
        p1_act = ctx.c["act"]
        ot_done = ctx.c["dve"]

    with ExitStack() as p2:
        c2 = sub(ctx, p2)
        wsl = [c2.sb(f"b_wsl{i}", [128, 32 * 512], BF16) for i in range(2)]
        xsl = [c2.sb(f"b_xsl{i}", [128, 512], F32) for i in range(4)]
        xo_sb = [c2.sb(f"b_xo{i}", [128, 512], F32) for i in range(2)]
        psp = [c2.ps(f"b_psp{i}", [128, 512]) for i in range(2)]

        wsl_ready = [None, None]
        wsl_free = [0, 0]
        ctx.wait(G, "pe", p1_pe)
        ctx.wait(G, "dve", p1_dve)
        ctx.wait(G, "act", p1_act)
        ctx.wait(T, "act", p1_act)
        ctx.wait(Y, "pe", p1_pe)
        ctx.wait(Y, "dve", p1_dve)
        ctx.wait(Y, "act", p1_act)
        ctx.wait(T, "dve", p1_dve)
        ctx.op(G, lambda g: g.dma_start(out=wsl[0][:], in_=wos[0]), "dw0", 16)
        wsl_ready[0] = ("dw0", ctx.c["dw0"])

        psp_free = [0, 0]
        xsl_free = [0] * 4
        xo_free = [0, 0]
        it = 0
        for oc in range(8):
            wp = oc % 2
            if oc + 1 < 8:
                npar = (oc + 1) % 2
                ctx.wait(G, "pe", wsl_free[npar])
                ctx.op(G, lambda g, oc=oc, npar=npar: g.dma_start(
                    out=wsl[npar][:], in_=wos[oc + 1]), f"dw{npar}", 16)
                wsl_ready[npar] = (f"dw{npar}", ctx.c[f"dw{npar}"])
            for tb in range(NTB):
                par = it % 2
                xp = it % 4
                ctx.wait(Y, "dve", xsl_free[xp])
                ctx.op(Y, lambda sy, tb=tb, oc=oc, xp=xp: sy.dma_start(
                    out=xsl[xp][:], in_=xin[tb * 128:(tb + 1) * 128, oc * 512:(oc + 1) * 512]),
                    f"dxs{xp}", 16)
                xsl_n = (f"dxs{xp}", ctx.c[f"dxs{xp}"])

                ctx.wait(T, *wsl_ready[wp])
                ctx.wait(T, "dve", max(ot_done, psp_free[par]))
                for k in range(32):
                    mm_n = ctx.op(T, lambda t, tb=tb, k=k, par=par, wp=wp: t.matmul(
                        psp[par][:], ot_sb[:, k, tb * 128:(tb + 1) * 128],
                        wsl[wp][:, k * 512:(k + 1) * 512],
                        start=(k == 0), stop=(k == 31)), "pe" if k == 31 else None)
                if tb == NTB - 1:
                    wsl_free[wp] = mm_n
                ctx.wait(V, "pe", mm_n)
                ctx.wait(V, *xsl_n)
                ctx.wait(V, f"dst{par}", xo_free[par])
                ep_n = ctx.op(V, lambda v, par=par, xp=xp: v.tensor_tensor(
                    out=xo_sb[par][:], in0=psp[par][:], in1=xsl[xp][:],
                    op=ALU.add), "dve")
                psp_free[par] = ep_n
                xsl_free[xp] = ep_n
                ctx.wait(Y, "dve", ep_n)
                xo_free[par] = ctx.op(Y, lambda sy, tb=tb, oc=oc, par=par: sy.dma_start(
                    out=xout[tb * 128:(tb + 1) * 128, oc * 512:(oc + 1) * 512],
                    in_=xo_sb[par][:]), f"dst{par}", 16)
                it += 1


def stage_l1attn_b(inp, s5a_outs):
    import ml_dtypes
    bf = ml_dtypes.bfloat16
    kt0 = np.asarray(s5a_outs[0]["kt"])
    fast = kt0.dtype.itemsize == 2
    if fast:
        conv = lambda a: np.asarray(a).view(np.uint16)
        out = lambda a: np.ascontiguousarray(a).view(bf)
        wt = np.uint16
        ONE = np.float32(1.0).astype(bf).view(np.uint16)
    else:
        conv = lambda a: np.asarray(a)
        out = lambda a: np.ascontiguousarray(a)
        wt = kt0.dtype
        ONE = wt.type(1.0) if hasattr(wt, "type") else 1.0
    wo = np.asarray(inp["L1_attn_wo"], np.float32)
    k_idx, q_idx = np.meshgrid(np.arange(128), np.arange(128), indexing="ij")
    shared = {
        "wo_s": pack_slabs(np.ascontiguousarray(wo.T), 512),
        "ident": np.eye(128, dtype=np.float32),
        "trimask": (k_idx <= q_idx).astype(np.float32),
    }
    percore = [None] * 8
    for bb in range(2):
        cores = [bb * 4 + c for c in range(4)]
        ktg = np.zeros((128, 32, 4096), wt)
        vg = np.zeros((4096, 32, 129), wt)
        for c in range(4):
            co = cores[c]
            kt_c = conv(s5a_outs[co]["kt"])
            vt_c = conv(s5a_outs[co]["vt"]).reshape(1024, 32, 128)
            for half, u in ((0, c), (1, 7 - c)):
                ktg[:, :, u * 512:(u + 1) * 512] = kt_c[:, :, half * 512:(half + 1) * 512]
                vg[u * 512:(u + 1) * 512, :, :128] = vt_c[half * 512:(half + 1) * 512]
                vg[u * 512:(u + 1) * 512, :, 128] = ONE
        for c in range(4):
            co = cores[c]
            kta = ktg[:, :, :NKA * 128].copy()
            kta[:, :, 512 * c:] = 0
            ktb = ktg[:, :, :NKB * 128].copy()
            ktb[:, :, 512 * (7 - c):] = 0
            va = vg[:NKA * 128].copy()
            va[512 * c:] = 0
            vb = vg[:NKB * 128].copy()
            vb[512 * (7 - c):] = 0
            vd = np.zeros((1024, 32, 129), wt)
            vd[:, :, :128] = conv(s5a_outs[co]["vt"]).reshape(1024, 32, 128)
            vd[:, :, 128] = ONE

            def vhead(vv, nkb):
                a = vv.reshape(nkb, 128, 32, 129).transpose(2, 1, 0, 3)
                return out(a.reshape(32, 128, nkb * 129))

            percore[co] = {
                "qt": out(conv(s5a_outs[co]["qt"])),
                "kta": out(kta),
                "ktb": out(ktb),
                "ktd": out(conv(s5a_outs[co]["kt"])),
                "vta": vhead(va, NKA),
                "vtb": vhead(vb, NKB),
                "vtd": vhead(vd, 8),
            }
    return shared, percore


def ctx_join(ctx):
    for e in ENGINES:
        for s in ("pe", "act", "dve", "dst0", "dst1"):
            if s in ctx.c:
                ctx.wait(e, s)


def build_megakernel(upto=7):
    nc = bass.Bass("TRN2", target_bir_lowering=False)
    EI, EO = "ExternalInput", "ExternalOutput"
    xin_h = nc.dram_tensor("xin_h", [TIN, D], F32, kind=EI).ap()
    wup = nc.dram_tensor("wup_s", [8, 128, 32 * 512], F32, kind=EI).ap()
    gate_t = nc.dram_tensor("gate_t", [D, 4], F32, kind=EI).ap()
    ebias = nc.dram_tensor("ebias", [1, 4], F32, kind=EI).ap()
    lora_at = nc.dram_tensor("lora_at", [D, 32], F32, kind=EI).ap()
    lora_bs = nc.dram_tensor("lora_bs", [32, D], F32, kind=EI).ap()
    t8 = nc.dram_tensor("t8", [4, 32], F32, kind=EI).ap()
    ident = nc.dram_tensor("ident", [128, 128], F32, kind=EI).ap()
    convw = nc.dram_tensor("convw_s", [3, 6, 128, 32 * 512], F32, kind=EI).ap()
    convb = nc.dram_tensor("convb", [1, 3072], F32, kind=EI).ap()
    wq0 = nc.dram_tensor("wq_r", [1024, 1024], F32, kind=EI).ap()
    wk0 = nc.dram_tensor("wk_r", [1024, 1024], F32, kind=EI).ap()
    wv0 = nc.dram_tensor("wv_r", [1024, 1024], F32, kind=EI).ap()
    wo0 = nc.dram_tensor("wo_r", [1024, 1024], F32, kind=EI).ap()
    bqc = nc.dram_tensor("bq_c", [128, 8], F32, kind=EI).ap()
    bkc = nc.dram_tensor("bk_c", [128, 8], F32, kind=EI).ap()
    bvr = nc.dram_tensor("bv_r", [1, 1024], F32, kind=EI).ap()
    bor = nc.dram_tensor("bo_r", [1, 1024], F32, kind=EI).ap()
    projw = nc.dram_tensor("projw_s", [8, 128, 8 * 512], F32, kind=EI).ap()
    projb = nc.dram_tensor("projb", [1, D], F32, kind=EI).ap()
    trimask = nc.dram_tensor("trimask", [128, 128], F32, kind=EI).ap()
    a2 = nc.dram_tensor("a2", [128, 64], F32, kind=EI).ap()
    b2 = nc.dram_tensor("b2", [128, 64], F32, kind=EI).ap()
    betar = nc.dram_tensor("betar", [64, 64], F32, kind=EI).ap()
    gcols = nc.dram_tensor("gcols", [128, 32], F32, kind=EI).ap()
    biasr = nc.dram_tensor("biasr", [1, D], F32, kind=EI).ap()
    wqs = nc.dram_tensor("wq_s", [8, 128, 32 * 512], F32, kind=EI).ap()
    wks = nc.dram_tensor("wk_s", [8, 128, 32 * 512], F32, kind=EI).ap()
    wvs = nc.dram_tensor("wv_s", [8, 128, 32 * 512], F32, kind=EI).ap()
    wos = nc.dram_tensor("wo_s", [8, 128, 32 * 512], F32, kind=EI).ap()
    cs = nc.dram_tensor("cs_cols", [TLOC, 6], F32, kind=EI).ap()
    vmka = nc.dram_tensor("vmka", [1, NKA * 129], F32, kind=EI).ap()
    vmkb = nc.dram_tensor("vmkb", [1, NKB * 129], F32, kind=EI).ap()
    dmask = nc.dram_tensor("dmask", [128, 4, 512], F32, kind=EI).ap()
    xout = nc.dram_tensor("xout", [TLOC, D], F32, kind=EO).ap()
    x1h = nc.dram_tensor("x1h", [TIN, D], F32).ap()
    x2 = nc.dram_tensor("x2", [TLOC, D], F32).ap()
    x3 = nc.dram_tensor("x3", [TLOC, D], F32).ap()
    x4 = nc.dram_tensor("x4", [TLOC, D], F32).ap()
    x5 = nc.dram_tensor("x5", [TLOC, D], F32).ap()
    qt_t = nc.dram_tensor("qt_i", [128, 32, TLOC], BF16)
    kt_t = nc.dram_tensor("kt_i", [128, 32, TLOC], BF16)
    vhm_t = nc.dram_tensor("vhm_i", [256, 128, 129], BF16)
    agk_t = nc.dram_tensor("agk_i", [512, 32 * TLOC], BF16)
    agv_t = nc.dram_tensor("agv_i", [1024, 128 * 129], BF16)
    cdram = nc.dram_tensor("c_scratch", [64, 64], F32).ap()
    qt, kt, vhm = qt_t.ap(), kt_t.ap(), vhm_t.ap()
    agk, agv = agk_t.ap(), agv_t.ap()

    tiles9 = [(i * 128, 0, 128) for i in range(8)] + [(TIN - 128, 124, 128)]

    with ExitStack() as es:
        ctx = Ctx(nc, es)
        ctx.sems("dc", "dcy", "dcy2", "dx0", "dx1", "dxs0", "dxs1", "dxs2", "dxs3",
                 "dw0", "dw1", "kv0", "kv1", "pe", "act", "dve", "dst0", "dst1", "cc")

        stages = []

        def stage(fn):
            stages.append(fn)

        @stage
        def s1():
            with ExitStack() as s:
                emit_l0ffn(sub(ctx, s), xin_h, x1h, wup, gate_t, ebias, lora_at,
                           lora_bs, t8, ident, pre="s1_", tiles=tiles9, ncol=256)

        @stage
        def s2():
            with ExitStack() as s:
                emit_l0attn(sub(ctx, s), x1h, x2, convw, convb, wq0, wk0, wv0, wo0,
                            bqc, bkc, bvr, bor, projw, projb, ident, trimask)

        @stage
        def s3():
            with ExitStack() as s:
                emit_l0ffn(sub(ctx, s), x2, x3, wup, gate_t, ebias, lora_at,
                           lora_bs, t8, ident, pre="s3_")

        @stage
        def s4():
            with ExitStack() as s:
                emit_l1ffn(sub(ctx, s), x3, x4, a2, b2, betar, gcols, biasr,
                           ident, cdram, pre="s4_")

        @stage
        def s5a():
            with ExitStack() as s:
                emit_l1attn_a(sub(ctx, s), x4, wqs, wks, wvs, cs, ident, qt, kt, vhm,
                              kt2d=kt.rearrange("p h t -> p (h t)"),
                              vhm2d=vhm.rearrange("r p e -> r (p e)"),
                              agk=agk, agv=agv)

        @stage
        def s5b():
            ctx.wait("sync", "cc")
            with ExitStack() as s:
                emit_l1attn_b(sub(ctx, s), x4, qt, kt, vhm, agk, agv, wos,
                              ident, trimask, vmka, vmkb, dmask, x5)

        @stage
        def s6():
            with ExitStack() as s:
                emit_l1ffn(sub(ctx, s), x5, xout, a2, b2, betar, gcols, biasr,
                           ident, cdram, pre="s6_")

        for i, fn in enumerate(stages[:upto]):
            if i:
                ctx_join(ctx)
            fn()

        if upto < 7:
            ctx_join(ctx)
            srcx = [None, x1h, x2, x3, x4, None, x5][upto]
            if upto == 1:
                ctx.op("sync", lambda sy: sy.dma_start(out=xout[0:512, :], in_=x1h[1:513, :]), "dst0", 16)
                ctx.op("sync", lambda sy: sy.dma_start(out=xout[512:1024, :], in_=x1h[515:1027, :]), "dst1", 16)
            elif srcx is not None:
                ctx.op("sync", lambda sy, srcx=srcx: sy.dma_start(out=xout[:], in_=srcx[:]), "dst0", 16)

        ctx.wait("sync", "dst0")
        ctx.wait("sync", "dst1")
        ctx.flush()
    return nc


def stage_megakernel_inputs(inputs):
    shared = {}
    shared.update(stage_l0ffn_weights(inputs))
    shared.update(stage_l0attn_weights(inputs))
    shared.update(stage_l1ffn_weights(inputs))
    shared.update(stage_l1attn_a_weights(inputs))
    wo = np.asarray(inputs["L1_attn_wo"], np.float32)
    shared["wo_s"] = pack_slabs(np.ascontiguousarray(wo.T), 512)
    dm = np.zeros((128, 4, 512), np.float32)
    kk, qq = np.meshgrid(np.arange(128), np.arange(128), indexing="ij")
    tri = (kk <= qq).astype(np.float32)
    for j in range(4):
        for qb in range(4):
            dm[:, j, qb * 128:(qb + 1) * 128] = 0.0 if qb < j else (tri if qb == j else 1.0)
    shared["dmask"] = dm
    percore = []
    for core in range(NCORES):
        cpr = core % 4
        mka = np.zeros((1, NKA * 129), np.float32)
        mkb = np.zeros((1, NKB * 129), np.float32)
        mka[0, :4 * cpr * 129] = 1.0
        mkb[0, :4 * (7 - cpr) * 129] = 1.0
        percore.append({
            "cs_cols": cs_cols_for(core),
            "vmka": mka,
            "vmkb": mkb,
        })
    return shared, percore


_PROGS = {}


def _prog(name, builder):
    if name not in _PROGS:
        _PROGS[name] = builder()
    return _PROGS[name]


def kernel(**inputs):
    from concourse.bass_utils import run_bass_kernel_spmd
    x = np.ascontiguousarray(np.asarray(inputs["L0_attn_inputs"], np.float32))
    nc = _prog("mega", build_megakernel)
    shared, percore = stage_megakernel_inputs(inputs)
    in_maps = []
    for core in range(NCORES):
        m = dict(shared)
        m.update(percore[core])
        m["xin_h"] = slice_haloed(x, core)
        in_maps.append(m)
    res = run_bass_kernel_spmd(nc, in_maps, core_ids=list(range(NCORES)))
    sh = [res.results[c]["xout"] for c in range(NCORES)]
    out = np.empty(x.shape, sh[0].dtype)
    unslice_owned(sh, out)
    return out


# version 05
# speedup vs baseline: 1.0120x; 1.0120x over previous
from contextlib import ExitStack

import numpy as np

import concourse.bass as bass
import concourse.mybir as mybir

F32 = mybir.dt.float32
BF16 = mybir.dt.bfloat16
AF = mybir.ActivationFunctionType
ALU = mybir.AluOpType
AX = mybir.AxisListType

D = 4096
NCORES = 8
TLOC = 1024
NTB = TLOC // 128
LN_EPS = 1e-5

ENGINES = ("tensor", "vector", "scalar", "gpsimd", "sync")


def chunk_ranges(core):
    b, c = divmod(core, 4)
    a0 = 512 * c
    b0 = 512 * (7 - c)
    return b, (a0, a0 + 512), (b0, b0 + 512)


def slice_owned(x_full, core):
    b, (a0, a1), (b0, b1) = chunk_ranges(core)
    return np.ascontiguousarray(
        np.concatenate([x_full[b, a0:a1], x_full[b, b0:b1]], axis=0))


def unslice_owned(shards, out):
    for core in range(NCORES):
        b, (a0, a1), (b0, b1) = chunk_ranges(core)
        out[b, a0:a1] = shards[core][:512]
        out[b, b0:b1] = shards[core][512:]
    return out


def pack_slabs(w_in_out, slab_cols):
    IN, OUT = w_in_out.shape
    nk = IN // 128
    ns = OUT // slab_cols
    w = w_in_out.reshape(nk, 128, ns, slab_cols).transpose(2, 1, 0, 3)
    return np.ascontiguousarray(w.reshape(ns, 128, nk * slab_cols))


class Ctx:

    def __init__(self, nc, es):
        self.nc = nc
        self.es = es
        self.s = {}
        self.c = {}
        self.q = {e: [] for e in ENGINES}

    def sems(self, *names):
        for n in names:
            if n not in self.s:
                self.s[n] = self.es.enter_context(self.nc.semaphore(n))
                self.c[n] = 0

    def sb(self, name, shape, dtype):
        return self.es.enter_context(self.nc.sbuf_tensor(name, shape, dtype))

    def ps(self, name, shape, dtype=F32):
        return self.es.enter_context(self.nc.psum_tensor(name, shape, dtype))

    def wait(self, eng, sem, val=None):
        if val is None:
            val = self.c[sem]
        if val <= 0:
            return
        h = self.s[sem]
        self.q[eng].append(lambda e: e.wait_ge(h, val))

    def op(self, eng, fn, sem=None, n=1):
        if sem is None:
            self.q[eng].append(fn)
            return None
        h = self.s[sem]
        self.c[sem] += n
        self.q[eng].append(lambda e: fn(e).then_inc(h, n))
        return self.c[sem]

    def flush(self):
        with self.nc.Block() as block:
            for name in ENGINES:
                thunks = self.q[name]
                if not thunks:
                    continue
                getattr(block, name)(lambda e, th=tuple(thunks): [f(e) for f in th])
        self.q = {e: [] for e in ENGINES}


def alloc_stats(ctx, tag, sq=None):
    return {
        "sq": sq if sq is not None else ctx.sb(f"{tag}_sq", [128, D], BF16),
        "ss": ctx.sb(f"{tag}_ss", [128, 1], F32),
        "sm": ctx.sb(f"{tag}_sm", [128, 1], F32),
        "mu": ctx.sb(f"{tag}_mu", [128, 1], F32),
        "var": ctx.sb(f"{tag}_var", [128, 1], F32),
        "rvar": ctx.sb(f"{tag}_rvar", [128, 1], F32),
        "rstd": ctx.sb(f"{tag}_rstd", [128, 1], F32),
        "nmr": ctx.sb(f"{tag}_nmr", [128, 1], F32),
    }


def emit_ln_stats(ctx, xb, st, sq=None, ready=None):
    V, S = "vector", "scalar"
    sq = st["sq"] if sq is None else sq
    if ready is not None:
        ctx.wait(S, *ready)
        ctx.wait(V, *ready)
    ss_n = ctx.op(S, lambda s: s.activation(out=sq[:], in_=xb[:], func=AF.Square,
                                            accum_out=st["ss"][:]), "act")
    ctx.op(V, lambda v: v.reduce_sum(st["sm"][:], xb[:], axis=AX.X), "dve")
    ctx.wait(V, "dve")
    ctx.op(V, lambda v: v.tensor_scalar(out=st["mu"][:], in0=st["sm"][:],
                                        scalar1=1.0 / D, scalar2=None, op0=ALU.mult), "dve")
    ctx.wait(V, "dve")
    ctx.op(V, lambda v: v.tensor_tensor(out=st["var"][:], in0=st["mu"][:],
                                        in1=st["mu"][:], op=ALU.mult), "dve")
    ctx.wait(V, "dve")
    ctx.wait(V, "act", ss_n)
    ctx.op(V, lambda v: v.scalar_tensor_tensor(
        out=st["var"][:], in0=st["ss"][:], scalar=1.0 / D, in1=st["var"][:],
        op0=ALU.mult, op1=ALU.subtract), "dve")
    ctx.wait(V, "dve")
    ctx.op(V, lambda v: v.tensor_scalar(out=st["var"][:], in0=st["var"][:],
                                        scalar1=LN_EPS, scalar2=None, op0=ALU.add), "dve")
    ctx.wait(V, "dve")
    ctx.op(V, lambda v: v.reciprocal(st["rvar"][:], st["var"][:]), "dve")
    ctx.wait(S, "dve")
    ctx.op(S, lambda s: s.activation(out=st["rstd"][:], in_=st["rvar"][:],
                                     func=AF.Sqrt), "act")
    ctx.wait(V, "act")
    return ctx.op(V, lambda v: v.scalar_tensor_tensor(
        out=st["nmr"][:], in0=st["mu"][:], scalar=-1.0, in1=st["rstd"][:],
        op0=ALU.mult, op1=ALU.mult), "dve")


def build_l0ffn_program():
    nc = bass.Bass("TRN2", target_bir_lowering=False)
    xin = nc.dram_tensor("xin", [TLOC, D], F32, kind="ExternalInput").ap()
    wup = nc.dram_tensor("wup_s", [8, 128, 32 * 512], F32, kind="ExternalInput").ap()
    gate_t = nc.dram_tensor("gate_t", [D, 4], F32, kind="ExternalInput").ap()
    ebias = nc.dram_tensor("ebias", [1, 4], F32, kind="ExternalInput").ap()
    lora_at = nc.dram_tensor("lora_at", [D, 32], F32, kind="ExternalInput").ap()
    lora_bs = nc.dram_tensor("lora_bs", [32, D], F32, kind="ExternalInput").ap()
    t8 = nc.dram_tensor("t8", [4, 32], F32, kind="ExternalInput").ap()
    ident = nc.dram_tensor("ident", [128, 128], F32, kind="ExternalInput").ap()
    xout = nc.dram_tensor("xout", [TLOC, D], F32, kind="ExternalOutput").ap()

    with ExitStack() as es:
        ctx = Ctx(nc, es)
        emit_l0ffn(ctx, xin, xout, wup, gate_t, ebias, lora_at, lora_bs, t8, ident)
        ctx.wait("sync", "dst0")
        ctx.wait("sync", "dst1")
        ctx.flush()
    return nc


def emit_l0ffn(ctx, xin, xout, wup, gate_t, ebias, lora_at, lora_bs, t8, ident,
               pre="", tiles=None, ncol=512):
    if tiles is None:
        tiles = [(i * 128, 0, 128) for i in range(NTB)]
    nt = len(tiles)
    noc = D // ncol
    nsub = 512 // ncol
    ctx.sems("dc", "dx0", "dx1", "dxs0", "dxs1", "dxs2", "dxs3", "dw0", "dw1",
             "pe", "act", "dve", "dst0", "dst1")
    T, V, S, G, Y = "tensor", "vector", "scalar", "gpsimd", "sync"

    gate_sb = ctx.sb(pre + "gate_sb", [128, 32, 4], BF16)
    ebias_sb = ctx.sb(pre + "ebias_sb", [128, 4], F32)
    la_sb = ctx.sb(pre + "la_sb", [128, 32, 32], BF16)
    lb_sb = ctx.sb(pre + "lb_sb", [32, D], BF16)
    t8_sb = ctx.sb(pre + "t8_sb", [4, 32], BF16)
    id_sb = ctx.sb(pre + "id_sb", [128, 128], BF16)
    xb = [ctx.sb(pre + f"xb{i}", [128, D], F32) for i in range(2)]
    xnb = [ctx.sb(pre + f"xnb{i}", [128, D], BF16) for i in range(2)]
    xnt = ctx.sb(pre + "xnt", [128, nt * D], BF16)
    wsl = [ctx.sb(pre + f"wsl{i}", [128, 32, ncol], BF16) for i in range(2)]
    zsel = ctx.sb(pre + "zsel", [32, nt * 128], BF16)
    logits = ctx.sb(pre + "logits", [128, 4], F32)
    gmax = ctx.sb(pre + "gmax", [128, 1], F32)
    onehot = ctx.sb(pre + "onehot", [128, 4], BF16)
    onehT = ctx.sb(pre + "onehT", [4, 128], BF16)
    mask_sb = ctx.sb(pre + "mask_sb", [32, 128], BF16)
    xsl = [ctx.sb(pre + f"xsl{i}", [128, ncol], F32) for i in range(4)]
    xo_sb = [ctx.sb(pre + f"xo{i}", [128, ncol], F32) for i in range(2)]
    st = alloc_stats(ctx, pre + "st", sq=xnb[0])

    psT = [ctx.ps(pre + f"psT{i}", [128, 512], BF16) for i in range(2)]
    psg = ctx.ps(pre + "psg", [128, 4])
    psoh = ctx.ps(pre + "psoh", [4, 128], BF16)
    psm = ctx.ps(pre + "psm", [32, 128])
    psz = ctx.ps(pre + "psz", [32, 128])
    pso = [ctx.ps(pre + f"pso{i}", [128, ncol]) for i in range(2)]

    for fn in (
        lambda g: g.dma_start(out=gate_sb[:], in_=gate_t.rearrange("(k p) e -> p k e", p=128)),
        lambda g: g.dma_start(out=la_sb[:], in_=lora_at.rearrange("(k p) e -> p k e", p=128)),
        lambda g: g.dma_start(out=lb_sb[:], in_=lora_bs[:]),
        lambda g: g.dma_start(out=t8_sb[:], in_=t8[:]),
        lambda g: g.dma_start(out=id_sb[:], in_=ident[:]),
        lambda g: g.dma_start(out=ebias_sb[:], in_=ebias.broadcast_to([128, 4])),
    ):
        ctx.wait(G, "dc")
        ctx.op(G, fn, "dc", 16)
    consts = ctx.c["dc"]

    wsl_ready = [None, None]
    wsl_free = [0, 0]
    def _slab_src(oc):
        if nsub == 1:
            return wup[oc].rearrange("p (k j) -> p k j", j=ncol)
        return wup[oc // nsub].rearrange("p (k j) -> p k j", j=512)[
            :, :, (oc % nsub) * ncol:(oc % nsub + 1) * ncol]

    ctx.wait(G, "dc", consts)
    ctx.op(G, lambda g: g.dma_start(out=wsl[0][:], in_=_slab_src(0)), "dw0", 16)
    wsl_ready[0] = ("dw0", ctx.c["dw0"])

    xb_ready = [None] * nt
    for i in range(min(2, nt)):
        r0 = tiles[i][0]
        ctx.op(Y, lambda sy, i=i, r0=r0: sy.dma_start(
            out=xb[i][:], in_=xin[r0:r0 + 128, :]), f"dx{i}", 16)
        xb_ready[i] = (f"dx{i}", ctx.c[f"dx{i}"])

    tp_done = [0] * nt
    gate_done = [0] * nt
    xnb_free = [0, 0]
    psT_free = [0, 0]
    psg_free = 0
    psz_free = 0
    psoh_free = 0
    psm_free = 0
    onehot_free = 0
    onehT_free = 0
    mask_free = 0

    for tb in range(nt):
        par = tb % 2
        r0 = tiles[tb][0]

        if tb >= 2:
            ctx.wait("scalar", "pe", xnb_free[par])
        emit_ln_stats(ctx, xb[par], st, sq=xnb[par], ready=xb_ready[tb])
        ctx.wait(V, "dve")
        if tb >= 2:
            ctx.wait(V, "pe", xnb_free[par])
        ln_apply = ctx.op(V, lambda v, par=par: v.tensor_scalar(
            out=xnb[par][:], in0=xb[par][:], scalar1=st["rstd"][:],
            scalar2=st["nmr"][:], op0=ALU.mult, op1=ALU.add), "dve")

        if tb + 2 < nt:
            ctx.wait(Y, "dve", ln_apply)
            nr0 = tiles[tb + 2][0]
            ctx.op(Y, lambda sy, nr0=nr0, par=par: sy.dma_start(
                out=xb[par][:], in_=xin[nr0:nr0 + 128, :]), f"dx{par}", 16)
            xb_ready[tb + 2] = (f"dx{par}", ctx.c[f"dx{par}"])

        ctx.wait(T, "dc", consts)
        ctx.wait(T, "dve", ln_apply)
        for f4 in range(8):
            tpar = f4 % 2
            ctx.wait(T, "dve", psT_free[tpar])
            for ff in range(4):
                f = f4 * 4 + ff
                pe_n = ctx.op(T, lambda t, par=par, tpar=tpar, f=f, ff=ff: t.transpose(
                    psT[tpar][:, ff * 128:(ff + 1) * 128],
                    xnb[par][:, f * 128:(f + 1) * 128], id_sb[:]),
                    "pe" if ff == 3 else None)
            ctx.wait(V, "pe", pe_n)
            psT_free[tpar] = ctx.op(V, lambda v, tb=tb, tpar=tpar, f4=f4: v.tensor_copy(
                xnt[:, tb * D + f4 * 512: tb * D + (f4 + 1) * 512], psT[tpar][:]), "dve")
        xnb_free[par] = ctx.c["pe"]
        tp_done[tb] = ctx.c["dve"]

        ctx.wait(T, "dve", max(psg_free, tp_done[tb]))
        for k in range(32):
            ctx.op(T, lambda t, tb=tb, k=k: t.matmul(
                psg[:], xnt[:, tb * D + k * 128:tb * D + (k + 1) * 128],
                gate_sb[:, k, :],
                start=(k == 0), stop=(k == 31)), "pe" if k == 31 else None)
        ctx.wait(V, "pe")
        ctx.wait(V, "dc", consts)
        psg_free = ctx.op(V, lambda v: v.tensor_tensor(out=logits[:], in0=psg[:],
                                            in1=ebias_sb[:], op=ALU.add), "dve")
        ctx.wait(V, "dve")
        ctx.op(V, lambda v: v.reduce_max(gmax[:], logits[:], axis=AX.X), "dve")
        ctx.wait(V, "dve")
        ctx.wait(V, "pe", onehot_free)
        oh_n = ctx.op(V, lambda v: v.tensor_scalar(
            out=onehot[:], in0=logits[:], scalar1=gmax[:], scalar2=None,
            op0=ALU.is_equal), "dve")
        ctx.wait(T, "dve", max(oh_n, psoh_free))
        ohT_pe = ctx.op(T, lambda t: t.transpose(psoh[:], onehot[:], id_sb[:]), "pe")
        onehot_free = ohT_pe
        ctx.wait(V, "pe", max(ohT_pe, onehT_free))
        ohT_n = ctx.op(V, lambda v: v.tensor_copy(onehT[:], psoh[:]), "dve")
        psoh_free = ohT_n
        ctx.wait(T, "dve", ohT_n)
        ctx.wait(T, "act", psm_free)
        m_pe = ctx.op(T, lambda t: t.matmul(psm[:], t8_sb[:], onehT[:],
                                            start=True, stop=True), "pe")
        onehT_free = m_pe
        ctx.wait(S, "pe", m_pe)
        ctx.wait(S, "dve", mask_free)
        msk_n = ctx.op(S, lambda s: s.activation(out=mask_sb[:], in_=psm[:],
                                                 func=AF.Copy), "act")
        psm_free = msk_n
        ctx.wait(T, "dve", psz_free)
        for k in range(32):
            ctx.op(T, lambda t, tb=tb, k=k: t.matmul(
                psz[:], la_sb[:, k, :],
                xnt[:, tb * D + k * 128:tb * D + (k + 1) * 128],
                start=(k == 0), stop=(k == 31)), "pe" if k == 31 else None)
        ctx.wait(V, "pe")
        ctx.wait(V, "act", msk_n)
        gate_done[tb] = ctx.op(V, lambda v, tb=tb: v.tensor_tensor(
            out=zsel[:, tb * 128:(tb + 1) * 128], in0=psz[:], in1=mask_sb[:],
            op=ALU.mult), "dve")
        psz_free = gate_done[tb]
        mask_free = gate_done[tb]

    pso_free = [0, 0]
    xsl_free = [0] * 4
    xo_free = [0, 0]
    it = 0
    for oc in range(noc):
        wpar = oc % 2
        if oc + 1 < noc:
            npar = (oc + 1) % 2
            ctx.wait(G, "pe", wsl_free[npar])
            ctx.op(G, lambda g, oc=oc, npar=npar: g.dma_start(
                out=wsl[npar][:], in_=_slab_src(oc + 1)), f"dw{npar}", 16)
            wsl_ready[npar] = (f"dw{npar}", ctx.c[f"dw{npar}"])

        for tb in range(nt):
            par = it % 2
            xp = it % 4
            r0 = tiles[tb][0]
            ctx.wait(Y, "dve", xsl_free[xp])
            ctx.op(Y, lambda sy, r0=r0, oc=oc, xp=xp: sy.dma_start(
                out=xsl[xp][:], in_=xin[r0:r0 + 128, oc * ncol:(oc + 1) * ncol]),
                f"dxs{xp}", 16)
            xsl_n = (f"dxs{xp}", ctx.c[f"dxs{xp}"])

            ctx.wait(T, *wsl_ready[wpar])
            ctx.wait(T, "dve", max(tp_done[tb], pso_free[par]))
            for k in range(32):
                ctx.op(T, lambda t, tb=tb, k=k, par=par, wpar=wpar: t.matmul(
                    pso[par][:], xnt[:, tb * D + k * 128:tb * D + (k + 1) * 128],
                    wsl[wpar][:, k, :],
                    start=(k == 0), stop=False))
            ctx.wait(T, "dve", gate_done[tb])
            mm_n = ctx.op(T, lambda t, tb=tb, oc=oc, par=par: t.matmul(
                pso[par][:], zsel[:, tb * 128:(tb + 1) * 128],
                lb_sb[0:32, oc * ncol:(oc + 1) * ncol],
                start=False, stop=True), "pe")
            if tb == nt - 1:
                wsl_free[wpar] = mm_n

            ctx.wait(V, "pe", mm_n)
            ctx.wait(V, *xsl_n)
            ctx.wait(V, f"dst{par}", xo_free[par])
            ep_n = ctx.op(V, lambda v, par=par, xp=xp: v.scalar_tensor_tensor(
                out=xo_sb[par][:], in0=pso[par][:], scalar=0.5, in1=xsl[xp][:],
                op0=ALU.mult, op1=ALU.add), "dve")
            pso_free[par] = ep_n
            xsl_free[xp] = ep_n
            lo, hi = tiles[tb][1], tiles[tb][2]
            ctx.wait(Y, "dve", ep_n)
            xo_free[par] = ctx.op(Y, lambda sy, r0=r0, oc=oc, par=par, lo=lo, hi=hi:
                                  sy.dma_start(
                out=xout[r0 + lo:r0 + hi, oc * ncol:(oc + 1) * ncol],
                in_=xo_sb[par][lo:hi, :]), f"dst{par}", 16)
            it += 1


def stage_l0ffn_weights(inp):
    W_up = np.asarray(inp["L0_ffn_W_up"], np.float32)
    gate = np.asarray(inp["L0_ffn_gate_w"], np.float32)
    eb = np.asarray(inp["L0_ffn_expert_bias"], np.float32)
    A = np.asarray(inp["L0_ffn_lora_A"], np.float32)
    B = np.asarray(inp["L0_ffn_lora_B"], np.float32)
    t8 = np.zeros((4, 32), np.float32)
    for e in range(4):
        t8[e, e * 8:(e + 1) * 8] = 1.0
    return {
        "wup_s": pack_slabs(np.ascontiguousarray(W_up.T), 512),
        "gate_t": np.ascontiguousarray(gate.T),
        "ebias": eb.reshape(1, 4),
        "lora_at": np.ascontiguousarray(A.transpose(2, 0, 1).reshape(D, 32)),
        "lora_bs": np.ascontiguousarray(B.transpose(0, 2, 1).reshape(32, D)),
        "t8": t8,
        "ident": np.eye(128, dtype=np.float32),
    }



TIN = 1028


def xin_row(owned_idx):
    return 1 + owned_idx if owned_idx < 512 else 3 + owned_idx


def sub(ctx, es):
    c = Ctx.__new__(Ctx)
    c.__dict__.update(ctx.__dict__)
    c.es = es
    return c


def build_l0attn_program():
    nc = bass.Bass("TRN2", target_bir_lowering=False)
    xin = nc.dram_tensor("xin_h", [TIN, D], F32, kind="ExternalInput").ap()
    convw = nc.dram_tensor("convw_s", [3, 6, 128, 32 * 512], F32, kind="ExternalInput").ap()
    convb = nc.dram_tensor("convb", [1, 3072], F32, kind="ExternalInput").ap()
    wq = nc.dram_tensor("wq_r", [1024, 1024], F32, kind="ExternalInput").ap()
    wk = nc.dram_tensor("wk_r", [1024, 1024], F32, kind="ExternalInput").ap()
    wv = nc.dram_tensor("wv_r", [1024, 1024], F32, kind="ExternalInput").ap()
    wo = nc.dram_tensor("wo_r", [1024, 1024], F32, kind="ExternalInput").ap()
    bqc = nc.dram_tensor("bq_c", [128, 8], F32, kind="ExternalInput").ap()
    bkc = nc.dram_tensor("bk_c", [128, 8], F32, kind="ExternalInput").ap()
    bvr = nc.dram_tensor("bv_r", [1, 1024], F32, kind="ExternalInput").ap()
    bor = nc.dram_tensor("bo_r", [1, 1024], F32, kind="ExternalInput").ap()
    projw = nc.dram_tensor("projw_s", [8, 128, 8 * 512], F32, kind="ExternalInput").ap()
    projb = nc.dram_tensor("projb", [1, D], F32, kind="ExternalInput").ap()
    ident = nc.dram_tensor("ident", [128, 128], F32, kind="ExternalInput").ap()
    trimask = nc.dram_tensor("trimask", [128, 128], F32, kind="ExternalInput").ap()
    xout = nc.dram_tensor("xout", [TLOC, D], F32, kind="ExternalOutput").ap()

    with ExitStack() as es:
        ctx = Ctx(nc, es)
        emit_l0attn(ctx, xin, xout, convw, convb, wq, wk, wv, wo, bqc, bkc,
                    bvr, bor, projw, projb, ident, trimask)
        ctx.wait("sync", "dst0")
        ctx.wait("sync", "dst1")
        ctx.flush()
    return nc


def emit_l0attn(ctx, xin, xout, convw, convb, wq, wk, wv, wo, bqc, bkc,
                bvr, bor, projw, projb, ident, trimask):
    ctx.sems("dc", "dx0", "dx1", "dxs0", "dxs1", "dxs2", "dxs3",
             "dw0", "dw1", "pe", "act", "dve", "dst0", "dst1")
    T, V, S, G, Y = "tensor", "vector", "scalar", "gpsimd", "sync"

    id_sb = ctx.sb("id_sb", [128, 128], BF16)
    tri_sb = ctx.sb("tri_sb", [128, 128], BF16)
    ones_sb = ctx.sb("ones_sb", [1, 128], BF16)
    convb_sb = ctx.sb("convb_sb", [1, 3072], BF16)
    bq_sb = ctx.sb("bq_sb", [128, 8], F32)
    bk_sb = ctx.sb("bk_sb", [128, 8], F32)
    bv_sb = ctx.sb("bv_sb", [1, 1024], BF16)
    bo_sb = ctx.sb("bo_sb", [1, 1024], BF16)
    projb_sb = ctx.sb("projb_sb", [1, D], BF16)
    qkvc = ctx.sb("qkvc", [128, NTB, 3072], BF16)

    for fn in (
        lambda g: g.dma_start(out=id_sb[:], in_=ident[:]),
        lambda g: g.dma_start(out=tri_sb[:], in_=trimask[:]),
        lambda g: g.dma_start(out=convb_sb[:], in_=convb[:]),
        lambda g: g.dma_start(out=bv_sb[:], in_=bvr[:]),
        lambda g: g.dma_start(out=bo_sb[:], in_=bor[:]),
        lambda g: g.dma_start(out=projb_sb[:], in_=projb[:]),
    ):
        ctx.wait(G, "dc")
        ctx.op(G, fn, "dc", 16)
    ctx.wait(G, "dc")
    ctx.op(G, lambda g: g.dma_start(out=bq_sb[:], in_=bqc[:]), "dc", 16)
    ctx.wait(G, "dc")
    ctx.op(G, lambda g: g.dma_start(out=bk_sb[:], in_=bkc[:]), "dc", 16)
    consts = ctx.c["dc"]
    ctx.op(V, lambda v: v.memset(ones_sb[:], 1.0), "dve")
    ctx.wait(V, "dc", consts)
    ctx.op(V, lambda v: v.tensor_scalar(out=bq_sb[:], in0=bq_sb[:], scalar1=0.125,
                                        scalar2=None, op0=ALU.mult), "dve")
    bq_ready = ctx.c["dve"]
    ones_ready = ctx.c["dve"]

    qkvc_ready = [0] * NTB
    ab0 = [xin_row(128 * i) for i in range(NTB)]

    with ExitStack() as p12:
        c12 = sub(ctx, p12)
        xnt = c12.sb("xnt2", [128, 32, TIN], BF16)

        with ExitStack() as p1:
            c1 = sub(ctx, p1)
            xb = [c1.sb(f"a_xb{i}", [128, D], F32) for i in range(2)]
            xnb = [c1.sb(f"a_xnb{i}", [128, D], BF16) for i in range(2)]
            st = alloc_stats(c1, "a_st", sq=xnb[0])
            psT = [c1.ps(f"a_psT{i}", [128, 512], BF16) for i in range(2)]

            t0s = [min(i * 128, TIN - 128) for i in range(9)]
            xb_ready = [None] * 9
            for i in range(2):
                ctx.op(Y, lambda sy, i=i: sy.dma_start(
                    out=xb[i][:], in_=xin[t0s[i]:t0s[i] + 128, :]), f"dx{i % 2}", 16)
                xb_ready[i] = (f"dx{i % 2}", ctx.c[f"dx{i % 2}"])

            xnb_free = [0, 0]
            psT_free = [0, 0]
            for i in range(9):
                par = i % 2
                if i >= 2:
                    ctx.wait(S, "pe", xnb_free[par])
                emit_ln_stats(ctx, xb[par], st, sq=xnb[par], ready=xb_ready[i])
                ctx.wait(V, "dve")
                if i >= 2:
                    ctx.wait(V, "pe", xnb_free[par])
                ln_apply = ctx.op(V, lambda v, par=par: v.tensor_scalar(
                    out=xnb[par][:], in0=xb[par][:], scalar1=st["rstd"][:],
                    scalar2=st["nmr"][:], op0=ALU.mult, op1=ALU.add), "dve")
                if i + 2 < 9:
                    ctx.wait(Y, "dve", ln_apply)
                    ctx.op(Y, lambda sy, i=i, par=par: sy.dma_start(
                        out=xb[par][:], in_=xin[t0s[i + 2]:t0s[i + 2] + 128, :]),
                        f"dx{par}", 16)
                    xb_ready[i + 2] = (f"dx{par}", ctx.c[f"dx{par}"])
                ctx.wait(T, "dc", consts)
                ctx.wait(T, "dve", ln_apply)
                for f4 in range(8):
                    tp = f4 % 2
                    ctx.wait(T, "dve", psT_free[tp])
                    for ff in range(4):
                        f = f4 * 4 + ff
                        pe_n = ctx.op(T, lambda t, par=par, tp=tp, f=f, ff=ff: t.transpose(
                            psT[tp][:, ff * 128:(ff + 1) * 128],
                            xnb[par][:, f * 128:(f + 1) * 128], id_sb[:]),
                            "pe" if ff == 3 else None)
                    ctx.wait(V, "pe", pe_n)
                    psT_free[tp] = ctx.op(V, lambda v, i=i, tp=tp, f4=f4: v.tensor_copy(
                        xnt[:, f4 * 4:(f4 + 1) * 4, t0s[i]:t0s[i] + 128], psT[tp][:]), "dve")
                xnb_free[par] = ctx.c["pe"]
            p1_pe = ctx.c["pe"]
            p1_dve = ctx.c["dve"]
            p1_act = ctx.c["act"]

        with ExitStack() as p2:
            c2 = sub(ctx, p2)
            csl = [c2.sb(f"csl{i}", [128, 32 * 512], BF16) for i in range(2)]
            psc = [c2.ps(f"psc{i}", [128, 512]) for i in range(8)]

            csl_ready = [None, None]
            csl_free = [0, 0]
            ctx.wait(G, "pe", p1_pe)
            ctx.wait(G, "dve", p1_dve)
            ctx.wait(G, "act", p1_act)
            ctx.wait(G, "dc", consts)
            ctx.op(G, lambda g: g.dma_start(out=csl[0][:], in_=convw[0, 0]), "dw0", 16)
            csl_ready[0] = ("dw0", ctx.c["dw0"])

            sl = 0
            for oc6 in range(6):
                for dt in range(3):
                    par = sl % 2
                    if sl + 1 < 18:
                        npar = (sl + 1) % 2
                        noc, ndt = divmod(sl + 1, 3)
                        ctx.wait(G, "pe", csl_free[npar])
                        ctx.op(G, lambda g, noc=noc, ndt=ndt, npar=npar: g.dma_start(
                            out=csl[npar][:], in_=convw[ndt, noc]), f"dw{npar}", 16)
                        csl_ready[npar] = (f"dw{npar}", ctx.c[f"dw{npar}"])
                    ctx.wait(T, *csl_ready[par])
                    if sl == 0:
                        ctx.wait(T, "dve", p1_dve)
                    for ab in range(NTB):
                        if dt == 0:
                            ctx.wait(T, "dve", qkvc_ready[ab])
                        for k in range(32):
                            is_last_of_slab = (ab == NTB - 1 and k == 31)
                            ctx.op(T, lambda t, ab=ab, k=k, dt=dt, par=par:
                                   t.matmul(psc[ab][:],
                                            xnt[:, k, ab0[ab] + dt - 1: ab0[ab] + dt + 127],
                                            csl[par][:, k * 512:(k + 1) * 512],
                                            start=(dt == 0 and k == 0), stop=False),
                                   "pe" if is_last_of_slab else None)
                    csl_free[par] = ctx.c["pe"]
                    sl += 1
                ctx.wait(T, "dve", ones_ready)
                for ab in range(NTB):
                    mm_n = ctx.op(T, lambda t, ab=ab, oc6=oc6: t.matmul(
                        psc[ab][:], ones_sb[:],
                        convb_sb[:, oc6 * 512:(oc6 + 1) * 512],
                        start=False, stop=True), "pe")
                    ctx.wait(V, "pe", mm_n)
                    qkvc_ready[ab] = ctx.op(V, lambda v, ab=ab, oc6=oc6: v.tensor_copy(
                        qkvc[:, ab, oc6 * 512:(oc6 + 1) * 512], psc[ab][:]), "dve")
            p2_pe = ctx.c["pe"]
            p2_dve = ctx.c["dve"]

    o2t_ready = [0] * NTB
    with ExitStack() as p3:
        c3 = sub(ctx, p3)
        o2t = c3.sb("o2t", [128, 8, TLOC], BF16)
        wq_sb = c3.sb("wq_sb", [128, 8, 1024], BF16)
        wk_sb = c3.sb("wk_sb", [128, 8, 1024], BF16)
        wv_sb = c3.sb("wv_sb", [128, 8, 1024], BF16)
        wo_sb = c3.sb("wo_sb", [128, 8, 1024], BF16)
        qkvT = [c3.sb(f"qkvT{i}", [128, 24, 128], BF16) for i in range(2)]
        qhT = [c3.sb(f"qhT{i}", [128, 8, 128], BF16) for i in range(2)]
        khT = [c3.sb(f"khT{i}", [128, 8, 128], BF16) for i in range(2)]
        vh = [c3.sb(f"vh{i}", [128, 16, 65], BF16) for i in range(2)]
        probs = [c3.sb(f"probs{i}", [128, 128], BF16) for i in range(2)]
        rz = [c3.sb(f"rz{i}", [128, 1], F32) for i in range(2)]
        oh = [c3.sb(f"oh{i}", [128, 16 * 64], BF16) for i in range(2)]
        o2 = [c3.sb(f"o2{i}", [128, 1024], BF16) for i in range(2)]
        psT3 = [c3.ps(f"psT3_{i}", [128, 512], BF16) for i in range(2)]
        psqk = [c3.ps(f"psqk{i}", [128, 128]) for i in range(2)]
        psv = c3.ps("psv", [128, 512])
        pss = [c3.ps(f"pss{i}", [128, 512]) for i in range(2)]
        pso3 = psqk

        for fn in (
            lambda g: g.dma_start(out=wq_sb[:], in_=wq.rearrange("(k p) o -> p k o", p=128)),
            lambda g: g.dma_start(out=wk_sb[:], in_=wk.rearrange("(k p) o -> p k o", p=128)),
            lambda g: g.dma_start(out=wv_sb[:], in_=wv.rearrange("(k p) o -> p k o", p=128)),
            lambda g: g.dma_start(out=wo_sb[:], in_=wo.rearrange("(k p) o -> p k o", p=128)),
        ):
            ctx.wait(G, "pe", p2_pe)
            ctx.wait(G, "dve", p2_dve)
            ctx.wait(G, "dc")
            ctx.op(G, fn, "dc", 16)
        w3_ready = ctx.c["dc"]
        for i in range(2):
            ctx.op(V, lambda v, i=i: v.memset(vh[i][:, :, 64:65], 1.0), "dve")
        vh_ones_dve = ctx.c["dve"]

        qkvT_free = [0, 0]
        psT3_free = [0, 0]
        qhT_free = [0, 0]
        khT_free = [0, 0]
        vh_free = [0, 0]
        oh_free = [0, 0]
        o2_free = [0, 0]
        psqk_free = [0, 0]
        psv_free = 0
        pss_free = [0, 0]
        pso3_free = [0, 0]
        pswo_free = 0
        probs_free = [0, 0]
        rz_free = [0, 0]

        ctx.wait(T, "dve", p2_dve)
        for ab in range(NTB):
            par = ab % 2
            ctx.wait(T, "dve", max(qkvc_ready[ab], qkvT_free[par]))
            for f4 in range(6):
                tp = f4 % 2
                ctx.wait(T, "dve", psT3_free[tp])
                for ff in range(4):
                    f = f4 * 4 + ff
                    pe_n = ctx.op(T, lambda t, ab=ab, f=f, ff=ff, tp=tp: t.transpose(
                        psT3[tp][:, ff * 128:(ff + 1) * 128],
                        qkvc[:, ab, f * 128:(f + 1) * 128], id_sb[:]),
                        "pe" if ff == 3 else None)
                ctx.wait(V, "pe", pe_n)
                psT3_free[tp] = ctx.op(V, lambda v, par=par, f4=f4, tp=tp: v.tensor_copy(
                    qkvT[par][:, f4 * 4:(f4 + 1) * 4, :], psT3[tp][:]), "dve")
            qkvT_ready = ctx.c["dve"]

            ctx.wait(T, "dc", w3_ready)
            for which, wsb, dsttile, dstfree, bias, sc in (
                    ("q", wq_sb, qhT, qhT_free, bq_sb, 0.125),
                    ("k", wk_sb, khT, khT_free, bk_sb, 1.0)):
                base = 0 if which == "q" else 8
                for hc in range(8):
                    pq = hc % 2
                    ctx.wait(T, "dve", qkvT_ready)
                    ctx.wait(T, "dve", psqk_free[pq])
                    ctx.wait(T, "act", pso3_free[pq])
                    for k in range(8):
                        ctx.op(T, lambda t, k=k, hc=hc, par=par, pq=pq,
                               wsb=wsb, base=base: t.matmul(
                            psqk[pq][:], wsb[:, k, hc * 128:(hc + 1) * 128],
                            qkvT[par][:, base + k, :],
                            start=(k == 0), stop=(k == 7)), "pe" if k == 7 else None)
                    ctx.wait(V, "pe", ctx.c["pe"])
                    if which == "q":
                        ctx.wait(V, "dve", bq_ready)
                    if ab >= 2:
                        ctx.wait(V, "pe", dstfree[par])
                    psqk_free[pq] = ctx.op(V, lambda v, hc=hc, par=par, pq=pq,
                                           dsttile=dsttile, bias=bias, sc=sc:
                                           v.tensor_scalar(
                        out=dsttile[par][:, hc, :], in0=psqk[pq][:],
                        scalar1=sc, scalar2=bias[:, hc:hc + 1],
                        op0=ALU.mult, op1=ALU.add), "dve")
            qk_ready = ctx.c["dve"]

            for vc2 in range(2):
                ctx.wait(T, "dve", max(qkvT_ready, ones_ready))
                ctx.wait(T, "act", max(psv_free, pswo_free))
                for k in range(8):
                    ctx.op(T, lambda t, k=k, vc2=vc2, par=par: t.matmul(
                        psv[:], qkvT[par][:, 16 + k, :],
                        wv_sb[:, k, vc2 * 512:(vc2 + 1) * 512],
                        start=(k == 0), stop=False))
                mm = ctx.op(T, lambda t, vc2=vc2: t.matmul(
                    psv[:], ones_sb[:], bv_sb[:, vc2 * 512:(vc2 + 1) * 512],
                    start=False, stop=True), "pe")
                ctx.wait(S, "pe", mm)
                ctx.wait(S, "dve", vh_ones_dve)
                if ab >= 2:
                    ctx.wait(S, "pe", vh_free[par])
                psv_free = ctx.op(S, lambda s, vc2=vc2, par=par: s.activation(
                    out=vh[par][:, vc2 * 8:(vc2 + 1) * 8, 0:64],
                    in_=psv[:], func=AF.Copy), "act")
            v_ready = ctx.c["act"]

            for h in range(16):
                hp = h % 2
                p0 = 64 * (h % 2)
                kt = h // 2
                ctx.wait(T, "dve", qk_ready)
                ctx.wait(T, "act", pss_free[hp])
                s_pe = ctx.op(T, lambda t, par=par, p0=p0, kt=kt, hp=hp: t.matmul(
                    pss[hp][:, 0:128], khT[par][p0:p0 + 64, kt, :], qhT[par][p0:p0 + 64, kt, :],
                    start=True, stop=True), "pe")
                ctx.wait(S, "pe", max(s_pe, probs_free[hp]))
                e_n = ctx.op(S, lambda s, hp=hp: s.activation(
                    out=probs[hp][:], in_=pss[hp][:, 0:128], func=AF.Exp), "act")
                pss_free[hp] = e_n
                ctx.wait(V, "act", e_n)
                m_n = ctx.op(V, lambda v, hp=hp: v.tensor_tensor(
                    out=probs[hp][:], in0=probs[hp][:], in1=tri_sb[:], op=ALU.mult), "dve")
                ctx.wait(T, "dve", m_n)
                ctx.wait(T, "act", max(v_ready, pso3_free[hp]))
                pv_pe = ctx.op(T, lambda t, h=h, par=par, hp=hp: t.matmul(
                    pso3[hp][:, 0:65], probs[hp][:], vh[par][:, h, :],
                    start=True, stop=True), "pe")
                probs_free[hp] = pv_pe
                ctx.wait(V, "pe", pv_pe)
                ctx.wait(V, "act", rz_free[hp])
                r_n = ctx.op(V, lambda v, hp=hp: v.reciprocal(rz[hp][:], pso3[hp][:, 64:65]), "dve")
                ctx.wait(S, "dve", r_n)
                if ab >= 2:
                    ctx.wait(S, "pe", oh_free[par])
                pso3_free[hp] = ctx.op(S, lambda s, h=h, par=par, hp=hp: s.activation(
                    out=oh[par][:, h * 64:(h + 1) * 64], in_=pso3[hp][:, 0:64],
                    func=AF.Copy, scale=rz[hp][:]), "act")
                rz_free[hp] = pso3_free[hp]
            oh_ready = ctx.c["act"]

            ctx.wait(T, "act", oh_ready)
            for f4 in range(2):
                tp = f4 % 2
                ctx.wait(T, "dve", psT3_free[tp])
                for ff in range(4):
                    f = f4 * 4 + ff
                    pe_n = ctx.op(T, lambda t, par=par, f=f, ff=ff, tp=tp: t.transpose(
                        psT3[tp][:, ff * 128:(ff + 1) * 128],
                        oh[par][:, f * 128:(f + 1) * 128], id_sb[:]),
                        "pe" if ff == 3 else None)
                ctx.wait(V, "pe", pe_n)
                psT3_free[tp] = ctx.op(V, lambda v, par=par, f4=f4, tp=tp: v.tensor_copy(
                    qkvT[par][:, f4 * 4:(f4 + 1) * 4, :], psT3[tp][:]), "dve")
            ohT_ready = ctx.c["dve"]
            oh_free[par] = ctx.c["pe"]

            for wc in range(2):
                ctx.wait(T, "dve", max(ohT_ready, ones_ready))
                ctx.wait(T, "act", max(pswo_free, v_ready))
                for k in range(8):
                    ctx.op(T, lambda t, k=k, wc=wc, par=par: t.matmul(
                        psv[:], qkvT[par][:, k, :],
                        wo_sb[:, k, wc * 512:(wc + 1) * 512],
                        start=(k == 0), stop=False))
                mm = ctx.op(T, lambda t, wc=wc: t.matmul(
                    psv[:], ones_sb[:], bo_sb[:, wc * 512:(wc + 1) * 512],
                    start=False, stop=True), "pe")
                ctx.wait(S, "pe", mm)
                if ab >= 2:
                    ctx.wait(S, "pe", o2_free[par])
                pswo_free = ctx.op(S, lambda s, wc=wc, par=par: s.activation(
                    out=o2[par][:, wc * 512:(wc + 1) * 512], in_=psv[:],
                    func=AF.Copy), "act")
            o2_ready = ctx.c["act"]

            ctx.wait(T, "act", o2_ready)
            for f4 in range(2):
                tp = f4 % 2
                ctx.wait(T, "dve", psT3_free[tp])
                for ff in range(4):
                    f = f4 * 4 + ff
                    pe_n = ctx.op(T, lambda t, par=par, f=f, ff=ff, tp=tp: t.transpose(
                        psT3[tp][:, ff * 128:(ff + 1) * 128],
                        o2[par][:, f * 128:(f + 1) * 128], id_sb[:]),
                        "pe" if ff == 3 else None)
                ctx.wait(V, "pe", pe_n)
                psT3_free[tp] = ctx.op(V, lambda v, ab=ab, f4=f4, tp=tp: v.tensor_copy(
                    o2t[:, f4 * 4:(f4 + 1) * 4, ab * 128:(ab + 1) * 128], psT3[tp][:]), "dve")
            o2t_ready[ab] = ctx.c["dve"]
            o2_free[par] = ctx.c["pe"]
            qkvT_free[par] = ctx.c["pe"]
            qhT_free[par] = ctx.c["pe"]
            khT_free[par] = ctx.c["pe"]
            vh_free[par] = ctx.c["pe"]
        p3_pe = ctx.c["pe"]
        p3_dve = ctx.c["dve"]
        p3_act = ctx.c["act"]

        with ExitStack() as p4:
            c4 = sub(ctx, p4)
            pjs = [c4.sb(f"pjs{i}", [128, 8 * 512], BF16) for i in range(2)]
            xsl = [c4.sb(f"p_xsl{i}", [128, 512], F32) for i in range(4)]
            xo_sb = [c4.sb(f"p_xo{i}", [128, 512], F32) for i in range(2)]
            psp = [psv, pss[0]]

            pjs_ready = [None, None]
            pjs_free = [0, 0]
            ctx.wait(T, "act", p3_act)
            ctx.wait(T, "dve", p3_dve)
            ctx.wait(G, "pe", p3_pe)
            ctx.wait(G, "dve", p3_dve)
            ctx.wait(G, "act", p3_act)
            ctx.wait(G, "dw0")
            ctx.wait(G, "dw1")
            ctx.op(G, lambda g: g.dma_start(out=pjs[0][:], in_=projw[0]), "dw0", 16)
            pjs_ready[0] = ("dw0", ctx.c["dw0"])

            psp_free = [0, 0]
            xsl_free = [0] * 4
            xo_free = [0, 0]
            it = 0
            for oc in range(8):
                wp = oc % 2
                if oc + 1 < 8:
                    npar = (oc + 1) % 2
                    ctx.wait(G, "pe", pjs_free[npar])
                    ctx.op(G, lambda g, oc=oc, npar=npar: g.dma_start(
                        out=pjs[npar][:], in_=projw[oc + 1]), f"dw{npar}", 16)
                    pjs_ready[npar] = (f"dw{npar}", ctx.c[f"dw{npar}"])
                for tb in range(NTB):
                    par = it % 2
                    xp = it % 4
                    ctx.wait(Y, "dve", xsl_free[xp])
                    r0 = xin_row(tb * 128)
                    ctx.op(Y, lambda sy, oc=oc, xp=xp, r0=r0: sy.dma_start(
                        out=xsl[xp][:], in_=xin[r0:r0 + 128, oc * 512:(oc + 1) * 512]),
                        f"dxs{xp}", 16)
                    xsl_n = (f"dxs{xp}", ctx.c[f"dxs{xp}"])

                    ctx.wait(T, *pjs_ready[wp])
                    ctx.wait(T, "dve", max(o2t_ready[tb], psp_free[par], ones_ready))
                    for k in range(8):
                        ctx.op(T, lambda t, tb=tb, k=k, par=par, wp=wp: t.matmul(
                            psp[par][:], o2t[:, k, tb * 128:(tb + 1) * 128],
                            pjs[wp][:, k * 512:(k + 1) * 512],
                            start=(k == 0), stop=False))
                    mm_n = ctx.op(T, lambda t, oc=oc, par=par: t.matmul(
                        psp[par][:], ones_sb[:], projb_sb[:, oc * 512:(oc + 1) * 512],
                        start=False, stop=True), "pe")
                    if tb == NTB - 1:
                        pjs_free[wp] = mm_n

                    ctx.wait(V, "pe", mm_n)
                    ctx.wait(V, *xsl_n)
                    ctx.wait(V, f"dst{par}", xo_free[par])
                    ep_n = ctx.op(V, lambda v, par=par, xp=xp: v.tensor_tensor(
                        out=xo_sb[par][:], in0=psp[par][:], in1=xsl[xp][:],
                        op=ALU.add), "dve")
                    psp_free[par] = ep_n
                    xsl_free[xp] = ep_n
                    ctx.wait(Y, "dve", ep_n)
                    xo_free[par] = ctx.op(Y, lambda sy, tb=tb, oc=oc, par=par: sy.dma_start(
                        out=xout[tb * 128:(tb + 1) * 128, oc * 512:(oc + 1) * 512],
                        in_=xo_sb[par][:]), f"dst{par}", 16)
                    it += 1


def stage_l0attn_weights(inp):
    cw = np.asarray(inp["L0_attn_conv_w"], np.float32)
    convw = np.stack([pack_slabs(np.ascontiguousarray(cw[dt]), 512) for dt in range(3)])
    wq = np.ascontiguousarray(np.asarray(inp["L0_attn_wq"], np.float32).reshape(1024, 1024))
    wk = np.ascontiguousarray(np.asarray(inp["L0_attn_wk"], np.float32).reshape(1024, 1024))
    wv = np.ascontiguousarray(np.asarray(inp["L0_attn_wv"], np.float32).reshape(1024, 1024))
    wo = np.ascontiguousarray(np.asarray(inp["L0_attn_wo"], np.float32).reshape(1024, 1024))
    bq = np.asarray(inp["L0_attn_bq"], np.float32).reshape(1024)
    bk = np.asarray(inp["L0_attn_bk"], np.float32).reshape(1024)
    bv = np.asarray(inp["L0_attn_bv"], np.float32).reshape(1024)
    bo = np.asarray(inp["L0_attn_bo"], np.float32)
    pw = np.asarray(inp["L0_attn_proj_w"], np.float32)
    pb = np.asarray(inp["L0_attn_proj_b"], np.float32)
    cb = np.asarray(inp["L0_attn_conv_b"], np.float32)
    k_idx, q_idx = np.meshgrid(np.arange(128), np.arange(128), indexing="ij")
    tri_kq = (k_idx <= q_idx).astype(np.float32)
    return {
        "convw_s": convw,
        "convb": cb.reshape(1, 3072),
        "wq_r": wq, "wk_r": wk, "wv_r": wv, "wo_r": wo,
        "bq_c": np.ascontiguousarray(bq.reshape(8, 128).T),
        "bk_c": np.ascontiguousarray(bk.reshape(8, 128).T),
        "bv_r": bv.reshape(1, 1024),
        "bo_r": bo.reshape(1, 1024),
        "projw_s": pack_slabs(pw, 512),
        "projb": pb.reshape(1, D),
        "ident": np.eye(128, dtype=np.float32),
        "trimask": tri_kq,
    }


def slice_haloed(x_full, core):
    b, (a0, a1), (b0, b1) = chunk_ranges(core)
    out = np.zeros((TIN, D), x_full.dtype)
    out[1:513] = x_full[b, a0:a1]
    out[515:1027] = x_full[b, b0:b1]
    if a0 > 0:
        out[0] = x_full[b, a0 - 1]
    if a1 < 4096:
        out[513] = x_full[b, a1]
    if b0 > 0:
        out[514] = x_full[b, b0 - 1]
    if b1 < 4096:
        out[1027] = x_full[b, b1]
    return out


def build_l1ffn_program():
    nc = bass.Bass("TRN2", target_bir_lowering=False)
    xin = nc.dram_tensor("xin", [TLOC, D], F32, kind="ExternalInput").ap()
    a2 = nc.dram_tensor("a2", [128, 64], F32, kind="ExternalInput").ap()
    b2 = nc.dram_tensor("b2", [128, 64], F32, kind="ExternalInput").ap()
    betar = nc.dram_tensor("betar", [64, 64], F32, kind="ExternalInput").ap()
    gcols = nc.dram_tensor("gcols", [128, 32], F32, kind="ExternalInput").ap()
    biasr = nc.dram_tensor("biasr", [1, D], F32, kind="ExternalInput").ap()
    ident = nc.dram_tensor("ident", [128, 128], F32, kind="ExternalInput").ap()
    xout = nc.dram_tensor("xout", [TLOC, D], F32, kind="ExternalOutput").ap()
    cdram = nc.dram_tensor("c_scratch", [64, 64], F32).ap()

    with ExitStack() as es:
        ctx = Ctx(nc, es)
        emit_l1ffn(ctx, xin, xout, a2, b2, betar, gcols, biasr, ident, cdram)
        ctx.wait("sync", "dst0")
        ctx.wait("sync", "dst1")
        ctx.flush()
    return nc


def emit_l1ffn(ctx, xin, xout, a2, b2, betar, gcols, biasr, ident, cdram, pre="f_"):
    ctx.sems("dc", "dcy", "dcy2", "dx0", "dx1", "dw0", "pe", "act", "dve", "dst0", "dst1")
    T, V, S, G, Y = "tensor", "vector", "scalar", "gpsimd", "sync"

    id_sb = ctx.sb(pre + "id", [128, 128], BF16)
    a2_sb = ctx.sb(pre + "a2", [128, 64], BF16)
    b2_sb = ctx.sb(pre + "b2", [128, 64], BF16)
    betar_sb = ctx.sb(pre + "betar", [64, 64], BF16)
    gc_sb = ctx.sb(pre + "gc", [128, 32], F32)
    bias_bc = ctx.sb(pre + "biasbc", [128, D], F32)
    cb2 = ctx.sb(pre + "cb2", [128, D], BF16)
    m1_sb = ctx.sb(pre + "m1", [64, 64], BF16)
    m1t_sb = ctx.sb(pre + "m1t", [64, 64], BF16)
    c_sb = ctx.sb(pre + "c", [64, 64], F32)

    xb = [ctx.sb(pre + f"xb{i}", [128, D], F32) for i in range(2)]
    xn2 = [ctx.sb(pre + f"xn2{i}", [128, D], BF16) for i in range(2)]
    hT = [ctx.sb(pre + f"hT{i}", [128, 32, 128], BF16) for i in range(2)]
    u_sb = [ctx.sb(pre + f"u{i}", [128, D], BF16) for i in range(2)]
    uT = [ctx.sb(pre + f"uT{i}", [128, 32, 128], BF16) for i in range(2)]
    y_sb = [ctx.sb(pre + f"y{i}", [128, D], F32) for i in range(2)]
    g_sb = ctx.sb(pre + "g", [128, D], F32)
    e1_sb = ctx.sb(pre + "e1", [128, D], F32)
    xo_sb = y_sb
    st = alloc_stats(ctx, pre + "st", sq=xn2[0])
    r1c = ctx.sb(pre + "r1", [128, 1], F32)
    v2e = ctx.sb(pre + "v2e", [128, 1], F32)
    r2c = ctx.sb(pre + "r2", [128, 1], F32)
    rstd2 = ctx.sb(pre + "rstd2", [128, 1], F32)
    s2col = ctx.sb(pre + "s2col", [128, 1], F32)
    Rc = ctx.sb(pre + "R", [128, 1], F32)
    nmR = ctx.sb(pre + "nmR", [128, 1], F32)

    psT = [ctx.ps(pre + f"psT{i}", [128, 512], BF16) for i in range(2)]
    psu = [ctx.ps(pre + f"psu{i}", [128, 64]) for i in range(2)]
    psy = [ctx.ps(pre + f"psy{i}", [128, 64]) for i in range(2)]
    psc1 = ctx.ps(pre + "psc1", [64, 64])
    psc2 = ctx.ps(pre + "psc2", [64, 64], BF16)

    for fn in (
        lambda g: g.dma_start(out=id_sb[:], in_=ident[:]),
        lambda g: g.dma_start(out=a2_sb[:], in_=a2[:]),
        lambda g: g.dma_start(out=b2_sb[:], in_=b2[:]),
        lambda g: g.dma_start(out=betar_sb[:], in_=betar[:]),
        lambda g: g.dma_start(out=gc_sb[:], in_=gcols[:]),
    ):
        ctx.wait(G, "dc")
        ctx.op(G, fn, "dc", 16)
    consts = ctx.c["dc"]
    ctx.op(Y, lambda sy: sy.dma_start(out=bias_bc[:], in_=biasr.broadcast_to([128, D])), "dcy", 16)
    bias_ready = ("dcy", ctx.c["dcy"])

    ctx.wait(T, "dc", consts)
    n = ctx.op(T, lambda t: t.matmul(psc1[:], b2_sb[0:64, :], betar_sb[:],
                                     start=True, stop=True), "pe")
    ctx.wait(V, "pe", n)
    n = ctx.op(V, lambda v: v.tensor_copy(m1_sb[:], psc1[:]), "dve")
    ctx.wait(T, "dve", n)
    n = ctx.op(T, lambda t: t.transpose(psc2[:], m1_sb[:], id_sb[0:64, 0:64]), "pe")
    ctx.wait(V, "pe", n)
    n = ctx.op(V, lambda v: v.tensor_copy(m1t_sb[:], psc2[:]), "dve")
    ctx.wait(T, "dve", n)
    n = ctx.op(T, lambda t: t.matmul(psc1[:], m1t_sb[:], a2_sb[0:64, :],
                                     start=True, stop=True), "pe")
    ctx.wait(V, "pe", n)
    n = ctx.op(V, lambda v: v.tensor_copy(c_sb[:], psc1[:]), "dve")
    ctx.wait(Y, "dve", n)
    ctx.op(Y, lambda sy: sy.dma_start(out=cdram[:], in_=c_sb[:]), "dcy2", 16)
    ctx.wait(G, "dcy2")
    ctx.wait(G, "dc", consts)
    ctx.op(G, lambda g: g.dma_start(
        out=cb2[:], in_=cdram.rearrange("q b -> (q b)").unsqueeze(0).broadcast_to([128, D])), "dc", 16)
    cb2_ready = ("dc", ctx.c["dc"])

    xb_ready = [None] * NTB
    for i in range(2):
        ctx.op(Y, lambda sy, i=i: sy.dma_start(
            out=xb[i][:], in_=xin[i * 128:(i + 1) * 128, :]), f"dx{i % 2}", 16)
        xb_ready[i] = (f"dx{i % 2}", ctx.c[f"dx{i % 2}"])

    xn2_free = [0, 0]
    hT_free = [0, 0]
    uT_free = [0, 0]
    u_free = [0, 0]
    psT_free = [0, 0]
    psT_free_sem = [None, None]
    psu_free = [0, 0]
    psy_free = [0, 0]
    y_free = [0, 0]
    xo_free = [0, 0]
    xb_free = [0, 0]
    xn2_freed_dve = [0, 0]

    for tb in range(NTB):
        par = tb % 2
        if tb >= 2:
            ctx.wait(S, "pe", xn2_free[par])
            ctx.wait(S, "dve", xn2_freed_dve[par])
        emit_ln_stats(ctx, xb[par], st, sq=xn2[par], ready=xb_ready[tb])
        ctx.wait(V, "dve")
        ctx.op(V, lambda v: v.tensor_scalar(out=v2e[:], in0=st["rvar"][:],
                                            scalar1=-LN_EPS, scalar2=1.0 + LN_EPS,
                                            op0=ALU.mult, op1=ALU.add), "dve")
        ctx.wait(V, "dve")
        ctx.op(V, lambda v: v.reciprocal(r2c[:], v2e[:]), "dve")
        ctx.wait(S, "dve")
        ctx.op(S, lambda s: s.activation(out=rstd2[:], in_=r2c[:], func=AF.Sqrt), "act")
        ctx.op(S, lambda s: s.activation(out=s2col[:], in_=v2e[:], func=AF.Sqrt), "act")
        ctx.wait(V, "act")
        ctx.op(V, lambda v: v.tensor_tensor(out=Rc[:], in0=st["rstd"][:],
                                            in1=rstd2[:], op=ALU.mult), "dve")
        ctx.wait(V, "dve")
        ctx.op(V, lambda v: v.scalar_tensor_tensor(
            out=nmR[:], in0=st["mu"][:], scalar=-1.0, in1=Rc[:],
            op0=ALU.mult, op1=ALU.mult), "dve")
        ctx.wait(V, "dve")
        if tb >= 2:
            ctx.wait(V, "pe", xn2_free[par])
        xn2_n = ctx.op(V, lambda v, par=par: v.tensor_scalar(
            out=xn2[par][:], in0=xb[par][:], scalar1=Rc[:], scalar2=nmR[:],
            op0=ALU.mult, op1=ALU.add), "dve")

        ctx.wait(T, "dc", consts)
        ctx.wait(T, "dve", xn2_n)
        for f4 in range(8):
            tp = f4 % 2
            if psT_free_sem[tp]:
                ctx.wait(T, psT_free_sem[tp], psT_free[tp])
            for ff in range(4):
                f = f4 * 4 + ff
                pe_n = ctx.op(T, lambda t, par=par, tp=tp, f=f, ff=ff: t.transpose(
                    psT[tp][:, ff * 128:(ff + 1) * 128],
                    xn2[par][:, f * 128:(f + 1) * 128], id_sb[:]),
                    "pe" if ff == 3 else None)
            ctx.wait(V, "pe", pe_n)
            if tb >= 2:
                ctx.wait(V, "pe", hT_free[par])
            for ff in range(4):
                f = f4 * 4 + ff
                n = ctx.op(V, lambda v, par=par, tp=tp, f=f, ff=ff: v.tensor_scalar(
                    out=hT[par][:, f, :], in0=psT[tp][:, ff * 128:(ff + 1) * 128],
                    scalar1=gc_sb[:, f:f + 1], scalar2=None, op0=ALU.mult), "dve")
            psT_free[tp] = n
            psT_free_sem[tp] = "dve"
        xn2_free[par] = ctx.c["pe"]
        hT_ready = ctx.c["dve"]

        ctx.wait(T, "dve", hT_ready)
        for p in range(64):
            pp = p % 2
            base = 64 * (p % 2)
            f = p // 2
            ctx.wait(T, "dve", psu_free[pp])
            mm = ctx.op(T, lambda t, par=par, f=f, base=base, pp=pp: t.matmul(
                psu[pp][:], hT[par][base:base + 64, f, :], a2_sb[base:base + 64, :],
                start=True, stop=True), "pe")
            ctx.wait(V, "pe", mm)
            psu_free[pp] = ctx.op(V, lambda v, par=par, p=p, pp=pp: v.tensor_copy(
                u_sb[par][:, p::64], psu[pp][:]), "dve")
        u_ready = ctx.c["dve"]

        ctx.wait(T, "dve", u_ready)
        for f4 in range(8):
            tp = f4 % 2
            ctx.wait(T, psT_free_sem[tp], psT_free[tp])
            for ff in range(4):
                f = f4 * 4 + ff
                pe_n = ctx.op(T, lambda t, par=par, tp=tp, f=f, ff=ff: t.transpose(
                    psT[tp][:, ff * 128:(ff + 1) * 128],
                    u_sb[par][:, f * 128:(f + 1) * 128], id_sb[:]),
                    "pe" if ff == 3 else None)
            ctx.wait(V, "pe", pe_n)
            if tb >= 2:
                ctx.wait(V, "pe", uT_free[par])
            psT_free[tp] = ctx.op(V, lambda v, par=par, tp=tp, f4=f4: v.tensor_copy(
                uT[par][:, f4 * 4:(f4 + 1) * 4, :], psT[tp][:]), "dve")
            psT_free_sem[tp] = "dve"
        uT_ready = ctx.c["dve"]
        u_freed_dve_cur = ctx.c["pe"]

        ctx.wait(T, "dve", uT_ready)
        ctx.wait(V, *cb2_ready)
        for b in range(64):
            pp = b % 2
            base = 64 * (b % 2)
            f = b // 2
            ctx.wait(T, "dve", psy_free[pp])
            mm = ctx.op(T, lambda t, par=par, f=f, base=base, pp=pp: t.matmul(
                psy[pp][:], uT[par][base:base + 64, f, :], b2_sb[base:base + 64, :],
                start=True, stop=True), "pe")
            ctx.wait(V, "pe", mm)
            if b < 2 and tb >= 2:
                ctx.wait(V, "act", y_free[par])
                ctx.wait(V, f"dst{par}", xo_free[par])
            psy_free[pp] = ctx.op(V, lambda v, par=par, b=b, pp=pp: v.scalar_tensor_tensor(
                out=y_sb[par][:, b::64], in0=psy[pp][:], scalar=1.0,
                in1=cb2[:, b::64], op0=ALU.mult, op1=ALU.add), "dve")
        y_ready = ctx.c["dve"]

        ctx.wait(S, "dve", y_ready)
        gelu_n = ctx.op(S, lambda s, par=par: s.activation(
            out=g_sb[:], in_=y_sb[par][:], func=AF.Gelu), "act")
        y_free[par] = gelu_n
        ctx.wait(V, "act", gelu_n)
        ctx.wait(V, *bias_ready)
        e1_n = ctx.op(V, lambda v: v.tensor_tensor(
            out=e1_sb[:], in0=g_sb[:], in1=bias_bc[:], op=ALU.add), "dve")
        ctx.wait(V, "dve", e1_n)
        e2_n = ctx.op(V, lambda v, par=par: v.scalar_tensor_tensor(
            out=e1_sb[:], in0=xn2[par][:], scalar=s2col[:], in1=e1_sb[:],
            op0=ALU.mult, op1=ALU.add), "dve")
        ctx.wait(V, "dve", e2_n)
        e3_n = ctx.op(V, lambda v, par=par: v.scalar_tensor_tensor(
            out=xo_sb[par][:], in0=e1_sb[:], scalar=0.5, in1=xb[par][:],
            op0=ALU.mult, op1=ALU.add), "dve")
        xb_free[par] = e3_n
        ctx.wait(Y, "dve", e3_n)
        xo_free[par] = ctx.op(Y, lambda sy, tb=tb, par=par: sy.dma_start(
            out=xout[tb * 128:(tb + 1) * 128, :], in_=xo_sb[par][:]), f"dst{par}", 16)

        if tb + 2 < NTB:
            ctx.wait(Y, "dve", e3_n)
            ctx.op(Y, lambda sy, tb=tb, par=par: sy.dma_start(
                out=xb[par][:], in_=xin[(tb + 2) * 128:(tb + 3) * 128, :]), f"dx{par}", 16)
            xb_ready[tb + 2] = (f"dx{par}", ctx.c[f"dx{par}"])

        hT_free[par] = ctx.c["pe"]
        uT_free[par] = ctx.c["pe"]
        xn2_freed_dve[par] = e2_n


def stage_l1ffn_weights(inp):
    A = np.asarray(inp["L1_ffn_A"], np.float32)
    B = np.asarray(inp["L1_ffn_B"], np.float32)
    beta = np.asarray(inp["L1_ffn_ln_beta"], np.float32)
    gamma = np.asarray(inp["L1_ffn_ln_gamma"], np.float32)
    bias = np.asarray(inp["L1_ffn_bias"], np.float32)
    return {
        "a2": np.ascontiguousarray(np.concatenate([A, A], axis=0)),
        "b2": np.ascontiguousarray(np.concatenate([B, B], axis=0)),
        "betar": np.ascontiguousarray(beta.reshape(64, 64)),
        "gcols": np.ascontiguousarray(gamma.reshape(32, 128).T),
        "biasr": bias.reshape(1, D),
        "ident": np.eye(128, dtype=np.float32),
    }


def emit_l1attn_a(ctx, xin, wqs, wks, wvs, cs, ident, qt, kt, vhm,
                  kt2d=None, vhm2d=None, agk=None, agv=None):
    ctx.sems("dc", "dx0", "dx1", "dw0", "dw1", "pe", "act", "dve", "dst0", "dst1")
    T, V, S, G, Y = "tensor", "vector", "scalar", "gpsimd", "sync"

    id_sb = ctx.sb("q_id", [128, 128], BF16)
    cs_sb = ctx.sb("q_cs", [128, NTB, 6], F32)
    xnt = ctx.sb("q_xnt", [128, 32, TLOC], BF16)
    wsl = [ctx.sb(f"q_wsl{i}", [128, 32 * 512], BF16) for i in range(2)]
    ro_sb = [ctx.sb(f"q_ro{i}", [128, 512], BF16) for i in range(2)]
    vo_sb = [ctx.sb(f"q_vo{i}", [128, 4, 129], BF16) for i in range(2)]
    tt_sb = [ctx.sb(f"q_tt{i}", [128, 4, 128], BF16) for i in range(2)]
    for i in range(2):
        ctx.op("vector", lambda v, i=i: v.memset(vo_sb[i][:, :, 128:129], 1.0), "dve")
    vo_ones = ctx.c["dve"]

    psT = [ctx.ps(f"q_psT{i}", [128, 512], BF16) for i in range(2)]
    pso = [ctx.ps(f"q_pso{i}", [128, 512]) for i in range(2)]

    ctx.wait(G, "dc")
    ctx.op(G, lambda g: g.dma_start(out=id_sb[:], in_=ident[:]), "dc", 16)
    ctx.wait(G, "dc")
    ctx.op(G, lambda g: g.dma_start(
        out=cs_sb[:], in_=cs.rearrange("(tb p) c -> p tb c", p=128)), "dc", 16)
    consts = ctx.c["dc"]

    with ExitStack() as p1:
        c1 = sub(ctx, p1)
        xb = [c1.sb(f"q_xb{i}", [128, D], F32) for i in range(2)]
        xnb = [c1.sb(f"q_xnb{i}", [128, D], BF16) for i in range(2)]
        st = alloc_stats(c1, "q_st", sq=xnb[0])

        xb_ready = [None] * NTB
        for i in range(2):
            ctx.op(Y, lambda sy, i=i: sy.dma_start(
                out=xb[i][:], in_=xin[i * 128:(i + 1) * 128, :]), f"dx{i % 2}", 16)
            xb_ready[i] = (f"dx{i % 2}", ctx.c[f"dx{i % 2}"])
        xnb_free = [0, 0]
        psT_free = [0, 0]
        for i in range(NTB):
            par = i % 2
            if i >= 2:
                ctx.wait(S, "pe", xnb_free[par])
            emit_ln_stats(ctx, xb[par], st, sq=xnb[par], ready=xb_ready[i])
            ctx.wait(V, "dve")
            if i >= 2:
                ctx.wait(V, "pe", xnb_free[par])
            ln_apply = ctx.op(V, lambda v, par=par: v.tensor_scalar(
                out=xnb[par][:], in0=xb[par][:], scalar1=st["rstd"][:],
                scalar2=st["nmr"][:], op0=ALU.mult, op1=ALU.add), "dve")
            if i + 2 < NTB:
                ctx.wait(Y, "dve", ln_apply)
                ctx.op(Y, lambda sy, i=i, par=par: sy.dma_start(
                    out=xb[par][:], in_=xin[(i + 2) * 128:(i + 3) * 128, :]), f"dx{par}", 16)
                xb_ready[i + 2] = (f"dx{par}", ctx.c[f"dx{par}"])
            ctx.wait(T, "dc", consts)
            ctx.wait(T, "dve", ln_apply)
            for f4 in range(8):
                tp = f4 % 2
                ctx.wait(T, "dve", psT_free[tp])
                for ff in range(4):
                    f = f4 * 4 + ff
                    pe_n = ctx.op(T, lambda t, par=par, tp=tp, f=f, ff=ff: t.transpose(
                        psT[tp][:, ff * 128:(ff + 1) * 128],
                        xnb[par][:, f * 128:(f + 1) * 128], id_sb[:]),
                        "pe" if ff == 3 else None)
                ctx.wait(V, "pe", pe_n)
                psT_free[tp] = ctx.op(V, lambda v, i=i, tp=tp, f4=f4: v.tensor_copy(
                    xnt[:, f4 * 4:(f4 + 1) * 4, i * 128:(i + 1) * 128], psT[tp][:]), "dve")
            xnb_free[par] = ctx.c["pe"]
        p1_pe = ctx.c["pe"]
        p1_dve = ctx.c["dve"]
        p1_act = ctx.c["act"]

    wsl_ready = [None, None]
    wsl_free = [0, 0]
    pso_free = [0, 0]
    ro_free = [0, 0]
    ro_free_sem = [None, None]
    vo_free = [0, 0]
    vo_free_sem = [None, None]
    tt_free = [0, 0]
    psT2_free = [0, 0]

    ctx.wait(G, "pe", p1_pe)
    ctx.wait(G, "dve", p1_dve)
    ctx.wait(G, "act", p1_act)

    sl = 0
    for wi, (wsrc, dst) in enumerate(((wqs, qt), (wks, kt), (wvs, None))):
        is_v = wi == 2
        csel = (0, 3, None)[wi]
        for oc in range(8):
            par = sl % 2
            if sl == 0:
                ctx.wait(G, "dc", consts)
                ctx.op(G, lambda g: g.dma_start(out=wsl[0][:], in_=wqs[0]), "dw0", 16)
                wsl_ready[0] = ("dw0", ctx.c["dw0"])
            if sl + 1 < 24:
                npar = (sl + 1) % 2
                nwi, noc = divmod(sl + 1, 8)
                nsrc = (wqs, wks, wvs)[nwi]
                ctx.wait(G, "pe", wsl_free[npar])
                ctx.op(G, lambda g, nsrc=nsrc, noc=noc, npar=npar: g.dma_start(
                    out=wsl[npar][:], in_=nsrc[noc]), f"dw{npar}", 16)
                wsl_ready[npar] = (f"dw{npar}", ctx.c[f"dw{npar}"])

            for tb in range(NTB):
                pp = (sl * NTB + tb) % 2
                ctx.wait(T, *wsl_ready[par])
                ctx.wait(T, "dve", pso_free[pp])
                for k in range(32):
                    mm = ctx.op(T, lambda t, tb=tb, k=k, par=par, pp=pp: t.matmul(
                        pso[pp][:], xnt[:, k, tb * 128:(tb + 1) * 128],
                        wsl[par][:, k * 512:(k + 1) * 512],
                        start=(k == 0), stop=(k == 31)), "pe" if k == 31 else None)
                if tb == NTB - 1:
                    wsl_free[par] = mm

                if is_v:
                    ctx.wait(V, "pe", mm)
                    rpar = pp
                    ctx.wait(V, "dve", vo_ones)
                    if vo_free_sem[rpar]:
                        ctx.wait(V, vo_free_sem[rpar], vo_free[rpar])
                    ev = ctx.op(V, lambda v, rpar=rpar, pp=pp: v.tensor_copy(
                        vo_sb[rpar][:, :, 0:128], pso[pp][:]), "dve")
                    pso_free[pp] = ev
                    ctx.wait(Y, "dve", ev)
                    for hh in range(4):
                        row = (oc * 4 + hh) * 8 + tb
                        ctx.op(Y, lambda sy, row=row, hh=hh, rpar=rpar: sy.dma_start(
                            out=vhm[row], in_=vo_sb[rpar][:, hh, :]),
                            f"dst{rpar}", 16)
                    vo_free[rpar] = ctx.c[f"dst{rpar}"]
                    vo_free_sem[rpar] = f"dst{rpar}"
                else:
                    rpar = pp
                    ctx.wait(V, "pe", mm)
                    if ro_free_sem[rpar]:
                        ctx.wait(V, ro_free_sem[rpar], ro_free[rpar])
                    cc = cs_sb[:, tb, csel:csel + 1]
                    ss_ = cs_sb[:, tb, csel + 1:csel + 2]
                    ns_ = cs_sb[:, tb, csel + 2:csel + 3]
                    ctx.wait(V, "dc", consts)
                    ctx.op(V, lambda v, rpar=rpar, pp=pp, cc=cc: v.tensor_scalar(
                        out=ro_sb[rpar][:, 0:512:2], in0=pso[pp][:, 0:512:2],
                        scalar1=cc, scalar2=None, op0=ALU.mult), "dve")
                    ctx.op(V, lambda v, rpar=rpar, pp=pp, cc=cc: v.tensor_scalar(
                        out=ro_sb[rpar][:, 1:512:2], in0=pso[pp][:, 1:512:2],
                        scalar1=cc, scalar2=None, op0=ALU.mult), "dve")
                    ctx.wait(V, "dve")
                    ctx.op(V, lambda v, rpar=rpar, pp=pp, ss_=ss_: v.scalar_tensor_tensor(
                        out=ro_sb[rpar][:, 1:512:2], in0=pso[pp][:, 0:512:2],
                        scalar=ss_, in1=ro_sb[rpar][:, 1:512:2],
                        op0=ALU.mult, op1=ALU.add), "dve")
                    ctx.wait(V, "dve")
                    ev = ctx.op(V, lambda v, rpar=rpar, pp=pp, ns_=ns_: v.scalar_tensor_tensor(
                        out=ro_sb[rpar][:, 0:512:2], in0=pso[pp][:, 1:512:2],
                        scalar=ns_, in1=ro_sb[rpar][:, 0:512:2],
                        op0=ALU.mult, op1=ALU.add), "dve")
                    pso_free[pp] = ev
                    tp = (sl * NTB + tb) % 2
                    ctx.wait(T, "dve", max(ev, psT2_free[tp]))
                    for hh in range(4):
                        pe_n = ctx.op(T, lambda t, rpar=rpar, hh=hh, tp=tp: t.transpose(
                            psT[tp][:, hh * 128:(hh + 1) * 128],
                            ro_sb[rpar][:, hh * 128:(hh + 1) * 128],
                            id_sb[:]), "pe" if hh == 3 else None)
                    ctx.wait(V, "pe", pe_n)
                    if tt_free[tp]:
                        ctx.wait(V, f"dst{tp}", tt_free[tp])
                    psT2_free[tp] = ctx.op(V, lambda v, tp=tp: v.tensor_copy(
                        tt_sb[tp][:], psT[tp][:]), "dve")
                    ctx.wait(Y, "dve", psT2_free[tp])
                    tt_free[tp] = ctx.op(Y, lambda sy, dst=dst, oc=oc,
                                         tb=tb, tp=tp: sy.dma_start(
                        out=dst[0:128, oc * 4:(oc + 1) * 4, tb * 128:(tb + 1) * 128],
                        in_=tt_sb[tp][:]), f"dst{tp}", 16)
                    ro_free[rpar] = ctx.c["pe"]
                    ro_free_sem[rpar] = "pe"
            sl += 1

    if agk is not None:
        ctx.sems("cc")
        rg = [[0, 1, 2, 3], [4, 5, 6, 7]]
        ctx.wait(G, "dst0")
        ctx.wait(G, "dst1")
        ctx.op(G, lambda g: g.collective_compute(
            "AllGather", ALU.bypass, replica_groups=rg,
            ins=[kt2d], outs=[agk]), "cc", 1)
        ctx.op(G, lambda g: g.collective_compute(
            "AllGather", ALU.bypass, replica_groups=rg,
            ins=[vhm2d], outs=[agv]), "cc", 1)


def stage_l1attn_a_weights(inp):
    wq = np.asarray(inp["L1_attn_wq"], np.float32)
    wk = np.asarray(inp["L1_attn_wk"], np.float32)
    wv = np.asarray(inp["L1_attn_wv"], np.float32)
    return {
        "wq_s": pack_slabs(np.ascontiguousarray(wq.T), 512),
        "wk_s": pack_slabs(np.ascontiguousarray(wk.T), 512),
        "wv_s": pack_slabs(np.ascontiguousarray(wv.T), 512),
        "ident": np.eye(128, dtype=np.float32),
    }


def cs_cols_for(core):
    b, (a0, a1), (b0, b1) = chunk_ranges(core)
    pos = np.concatenate([np.arange(a0, a1), np.arange(b0, b1)]).astype(np.float32)
    c = np.cos(pos)
    s = np.sin(pos)
    sc = 1.0 / np.sqrt(128.0)
    return np.stack([c * sc, s * sc, -s * sc, c, s, -s], axis=1).astype(np.float32)


NKA, NKB = 12, 28


def unit_owner(u):
    o = u if u < 4 else 7 - u
    return o, (0 if u < 4 else 512), (0 if u < 4 else 4)


def emit_l1attn_b(ctx, xin, qt, ktloc, vhmloc, agk, agv, wos,
                  ident, trimask, vmka, vmkb, dmask, xout):
    ctx.sems("dc", "kv0", "kv1", "dxs0", "dxs1", "dxs2", "dxs3",
             "dw0", "dw1", "pe", "act", "dve", "dst0", "dst1")
    T, V, S, G, Y = "tensor", "vector", "scalar", "gpsimd", "sync"

    id_sb = ctx.sb("b_id", [128, 128], BF16)
    tri_sb = ctx.sb("b_tri", [128, 128], BF16)
    ot_sb = ctx.sb("b_ot", [128, 32, TLOC], BF16)

    agk3 = agk.rearrange("r (h t) -> r h t", h=32)
    agv3 = agv.rearrange("r (p e) -> r p e", p=128)

    ctx.wait(G, "dc")
    ctx.op(G, lambda g: g.dma_start(out=id_sb[:], in_=ident[:]), "dc", 16)
    ctx.wait(G, "dc")
    ctx.op(G, lambda g: g.dma_start(out=tri_sb[:], in_=trimask[:]), "dc", 16)

    ot_ready = [0] * NTB

    with ExitStack() as p1:
        c1 = sub(ctx, p1)
        qt_sb = c1.sb("b_qt", [128, 32, TLOC], BF16)
        mka_sb = c1.sb("b_mka", [128, NKA * 129], BF16)
        mkb_sb = c1.sb("b_mkb", [128, NKB * 129], BF16)
        ctx.wait(G, "dc")
        ctx.op(G, lambda g: g.dma_start(out=mka_sb[:], in_=vmka.broadcast_to([128, NKA * 129])), "dc", 16)
        ctx.wait(G, "dc")
        ctx.op(G, lambda g: g.dma_start(out=mkb_sb[:], in_=vmkb.broadcast_to([128, NKB * 129])), "dc", 16)
        consts = ctx.c["dc"]
        ctx.op(Y, lambda sy: sy.dma_start(out=qt_sb[:], in_=qt[:]), "dst0", 16)
        qt_ready = ("dst0", ctx.c["dst0"])
        ka = [c1.sb(f"b_ka{i}", [128, NKA * 128], BF16) for i in range(2)]
        kb_ = [c1.sb(f"b_kb{i}", [128, NKB * 128], BF16) for i in range(2)]
        kd = [c1.sb(f"b_kd{i}", [128, TLOC], BF16) for i in range(2)]
        va = [c1.sb(f"b_va{i}", [128, NKA, 129], BF16) for i in range(2)]
        vb = [c1.sb(f"b_vb{i}", [128, NKB, 129], BF16) for i in range(2)]
        vd = [c1.sb(f"b_vd{i}", [128, 8, 129], BF16) for i in range(2)]
        probs = [c1.sb(f"b_pr{i}", [128, 512], BF16) for i in range(3)]
        dmask_sb = c1.sb("b_dmask", [128, 4, 512], BF16)
        rz = [c1.sb(f"b_rz{i}", [128, 1], F32) for i in range(2)]
        oh = [c1.sb(f"b_oh{i}", [128, 128], BF16) for i in range(2)]

        pss = [c1.ps(f"b_pss{i}", [128, 512]) for i in range(3)]
        pvt = [c1.ps(f"b_pv{i}", [128, 129]) for i in range(4)]
        pv = [t[:] for t in pvt]
        psT = [c1.ps("b_psT0", [128, 128], BF16)] * 2

        ctx.wait(G, "dc")
        ctx.op(G, lambda g: g.dma_start(out=dmask_sb[:], in_=dmask[:]), "dc", 16)
        consts2 = ctx.c["dc"]
        kv_ready = [None, None]
        kv_free = [0, 0]
        pss_free = [0, 0, 0]
        probs_free = [0, 0, 0]
        pv_free = [0, 0, 0, 0]
        rz_free = [0, 0]
        oh_free = [0, 0]
        psT_free = [0, 0]

        vmask_done = [0, 0]

        def load_head(h, par):
            ctx.wait(Y, "pe", kv_free[par])
            ctx.wait(Y, "dve", vmask_done[par])
            for u in range(NKA // 4):
                o, loff, _ = unit_owner(u)
                ctx.op(Y, lambda sy, o=o, loff=loff, u=u, h=h, par=par: sy.dma_start(
                    out=ka[par][:, u * 512:(u + 1) * 512],
                    in_=agk3[o * 128:(o + 1) * 128, h, loff:loff + 512]), f"kv{par}", 16)
            for u in range(NKB // 4):
                o, loff, _ = unit_owner(u)
                ctx.op(Y, lambda sy, o=o, loff=loff, u=u, h=h, par=par: sy.dma_start(
                    out=kb_[par][:, u * 512:(u + 1) * 512],
                    in_=agk3[o * 128:(o + 1) * 128, h, loff:loff + 512]), f"kv{par}", 16)
            ctx.op(Y, lambda sy, h=h, par=par: sy.dma_start(
                out=kd[par][:], in_=ktloc[:, h, :]), f"kv{par}", 16)
            for u in range(NKA // 4):
                o, _, lb0 = unit_owner(u)
                r0 = o * 256 + h * 8 + lb0
                ctx.op(Y, lambda sy, r0=r0, u=u, par=par: sy.dma_start(
                    out=va[par][:, u * 4:(u + 1) * 4, :],
                    in_=agv3[r0:r0 + 4].rearrange("r p e -> p r e")), f"kv{par}", 16)
            for u in range(NKB // 4):
                o, _, lb0 = unit_owner(u)
                r0 = o * 256 + h * 8 + lb0
                ctx.op(Y, lambda sy, r0=r0, u=u, par=par: sy.dma_start(
                    out=vb[par][:, u * 4:(u + 1) * 4, :],
                    in_=agv3[r0:r0 + 4].rearrange("r p e -> p r e")), f"kv{par}", 16)
            ctx.op(Y, lambda sy, h=h, par=par: sy.dma_start(
                out=vd[par][:],
                in_=vhmloc[h * 8:(h + 1) * 8].rearrange("r p e -> p r e")), f"kv{par}", 16)
            kv_ready[par] = (f"kv{par}", ctx.c[f"kv{par}"])
            ctx.wait(V, *kv_ready[par])
            ctx.wait(V, "dc", consts)
            ctx.op(V, lambda v, par=par: v.tensor_tensor(
                out=va[par][:], in0=va[par][:], in1=mka_sb[:], op=ALU.mult), "dve")
            vmask_done[par] = ctx.op(V, lambda v, par=par: v.tensor_tensor(
                out=vb[par][:], in0=vb[par][:], in1=mkb_sb[:], op=ALU.mult), "dve")

        load_head(0, 0)
        for h in range(32):
            par = h % 2
            if h + 1 < 32:
                load_head(h + 1, (h + 1) % 2)
            ctx.wait(T, *kv_ready[par])
            ctx.wait(T, "dve", vmask_done[par])
            ctx.wait(T, *qt_ready)
            ctx.wait(T, "dc", consts2)
            for chunk in range(2):
                nkb = (NKA, NKB)[chunk]
                ksrc = (ka, kb_)[chunk][par]
                vsrc = (va, vb)[chunk][par]
                qc0 = chunk * 512
                nke = nkb + 4
                for kbi in range(nke):
                    dj = kbi - nkb
                    gp = kbi % 3
                    ctx.wait(T, "act", pss_free[gp])
                    if dj < 0:
                        kap = lambda t, kbi=kbi, ksrc=ksrc: ksrc[:, kbi * 128:(kbi + 1) * 128]
                    else:
                        kap = lambda t, lk=chunk * 4 + dj, par=par: kd[par][:, lk * 128:(lk + 1) * 128]
                    ctx.op(T, lambda t, kap=kap, gp=gp, qc0=qc0, h=h:
                           t.matmul(pss[gp][:], kap(t),
                                    qt_sb[:, h, qc0:qc0 + 512],
                                    start=True, stop=True), "pe")
                    smm = ctx.c["pe"]
                    ctx.wait(S, "pe", max(smm, probs_free[gp]))
                    e_n = ctx.op(S, lambda s, gp=gp: s.activation(
                        out=probs[gp][:], in_=pss[gp][:], func=AF.Exp), "act")
                    pss_free[gp] = e_n
                    if dj >= 0:
                        ctx.wait(V, "act", e_n)
                        m_n = ctx.op(V, lambda v, gp=gp, dj=dj: v.tensor_tensor(
                            out=probs[gp][:], in0=probs[gp][:],
                            in1=dmask_sb[:, dj, :], op=ALU.mult), "dve")
                        ctx.wait(T, "dve", m_n)
                    else:
                        ctx.wait(T, "act", e_n)
                    if kbi == 0:
                        ctx.wait(T, "act", max(pv_free[0], pv_free[1],
                                               pv_free[2], pv_free[3]))
                    for qb in range(4):
                        if dj < 0:
                            vap = lambda t, kbi=kbi, vsrc=vsrc: vsrc[:, kbi, :]
                        else:
                            vap = lambda t, lk=chunk * 4 + dj, par=par: vd[par][:, lk, :]
                        ctx.op(T, lambda t, vap=vap, qb=qb, gp=gp, kbi=kbi, nke=nke:
                               t.matmul(pv[qb],
                                        probs[gp][:, qb * 128:(qb + 1) * 128],
                                        vap(t),
                                        start=(kbi == 0), stop=(kbi == nke - 1)),
                               "pe" if qb == 3 else None)
                    probs_free[gp] = ctx.c["pe"]

                pv_mm = ctx.c["pe"]
                for qb in range(4):
                    qp = qb % 2
                    ctx.wait(V, "pe", pv_mm)
                    ctx.wait(V, "act", rz_free[qp])
                    r_n = ctx.op(V, lambda v, qb=qb, qp=qp: v.reciprocal(
                        rz[qp][:], pv[qb][:, 128:129]), "dve")
                    ctx.wait(S, "dve", r_n)
                    ctx.wait(S, "pe", oh_free[qp])
                    ev_n = ctx.op(S, lambda s, qb=qb, qp=qp: s.activation(
                        out=oh[qp][:], in_=pv[qb][:, 0:128],
                        func=AF.Copy, scale=rz[qp][:]), "act")
                    rz_free[qp] = ev_n
                    pv_free[qb] = ev_n
                    ctx.wait(T, "act", ev_n)
                    ctx.wait(T, "dve", max(psT_free[0], psT_free[1]))
                    tp_n = ctx.op(T, lambda t, qp=qp: t.transpose(
                        psT[qp][:], oh[qp][:], id_sb[:]), "pe")
                    oh_free[qp] = tp_n
                    ctx.wait(V, "pe", tp_n)
                    psT_free[qp] = ctx.op(V, lambda v, h=h, qc0=qc0, qb=qb, qp=qp: v.tensor_copy(
                        ot_sb[:, h, qc0 + qb * 128:qc0 + (qb + 1) * 128], psT[qp][:]), "dve")
            kv_free[par] = ctx.c["pe"]
        p1_pe = ctx.c["pe"]
        p1_dve = ctx.c["dve"]
        p1_act = ctx.c["act"]
        ot_done = ctx.c["dve"]

    with ExitStack() as p2:
        c2 = sub(ctx, p2)
        wsl = [c2.sb(f"b_wsl{i}", [128, 32 * 512], BF16) for i in range(2)]
        xsl = [c2.sb(f"b_xsl{i}", [128, 512], F32) for i in range(4)]
        xo_sb = [c2.sb(f"b_xo{i}", [128, 512], F32) for i in range(2)]
        psp = [c2.ps(f"b_psp{i}", [128, 512]) for i in range(2)]

        wsl_ready = [None, None]
        wsl_free = [0, 0]
        ctx.wait(G, "pe", p1_pe)
        ctx.wait(G, "dve", p1_dve)
        ctx.wait(G, "act", p1_act)
        ctx.wait(T, "act", p1_act)
        ctx.wait(Y, "pe", p1_pe)
        ctx.wait(Y, "dve", p1_dve)
        ctx.wait(Y, "act", p1_act)
        ctx.wait(T, "dve", p1_dve)
        ctx.op(G, lambda g: g.dma_start(out=wsl[0][:], in_=wos[0]), "dw0", 16)
        wsl_ready[0] = ("dw0", ctx.c["dw0"])

        psp_free = [0, 0]
        xsl_free = [0] * 4
        xo_free = [0, 0]
        it = 0
        for oc in range(8):
            wp = oc % 2
            if oc + 1 < 8:
                npar = (oc + 1) % 2
                ctx.wait(G, "pe", wsl_free[npar])
                ctx.op(G, lambda g, oc=oc, npar=npar: g.dma_start(
                    out=wsl[npar][:], in_=wos[oc + 1]), f"dw{npar}", 16)
                wsl_ready[npar] = (f"dw{npar}", ctx.c[f"dw{npar}"])
            for tb in range(NTB):
                par = it % 2
                xp = it % 4
                ctx.wait(Y, "dve", xsl_free[xp])
                ctx.op(Y, lambda sy, tb=tb, oc=oc, xp=xp: sy.dma_start(
                    out=xsl[xp][:], in_=xin[tb * 128:(tb + 1) * 128, oc * 512:(oc + 1) * 512]),
                    f"dxs{xp}", 16)
                xsl_n = (f"dxs{xp}", ctx.c[f"dxs{xp}"])

                ctx.wait(T, *wsl_ready[wp])
                ctx.wait(T, "dve", max(ot_done, psp_free[par]))
                for k in range(32):
                    mm_n = ctx.op(T, lambda t, tb=tb, k=k, par=par, wp=wp: t.matmul(
                        psp[par][:], ot_sb[:, k, tb * 128:(tb + 1) * 128],
                        wsl[wp][:, k * 512:(k + 1) * 512],
                        start=(k == 0), stop=(k == 31)), "pe" if k == 31 else None)
                if tb == NTB - 1:
                    wsl_free[wp] = mm_n
                ctx.wait(V, "pe", mm_n)
                ctx.wait(V, *xsl_n)
                ctx.wait(V, f"dst{par}", xo_free[par])
                ep_n = ctx.op(V, lambda v, par=par, xp=xp: v.tensor_tensor(
                    out=xo_sb[par][:], in0=psp[par][:], in1=xsl[xp][:],
                    op=ALU.add), "dve")
                psp_free[par] = ep_n
                xsl_free[xp] = ep_n
                ctx.wait(Y, "dve", ep_n)
                xo_free[par] = ctx.op(Y, lambda sy, tb=tb, oc=oc, par=par: sy.dma_start(
                    out=xout[tb * 128:(tb + 1) * 128, oc * 512:(oc + 1) * 512],
                    in_=xo_sb[par][:]), f"dst{par}", 16)
                it += 1


def stage_l1attn_b(inp, s5a_outs):
    import ml_dtypes
    bf = ml_dtypes.bfloat16
    kt0 = np.asarray(s5a_outs[0]["kt"])
    fast = kt0.dtype.itemsize == 2
    if fast:
        conv = lambda a: np.asarray(a).view(np.uint16)
        out = lambda a: np.ascontiguousarray(a).view(bf)
        wt = np.uint16
        ONE = np.float32(1.0).astype(bf).view(np.uint16)
    else:
        conv = lambda a: np.asarray(a)
        out = lambda a: np.ascontiguousarray(a)
        wt = kt0.dtype
        ONE = wt.type(1.0) if hasattr(wt, "type") else 1.0
    wo = np.asarray(inp["L1_attn_wo"], np.float32)
    k_idx, q_idx = np.meshgrid(np.arange(128), np.arange(128), indexing="ij")
    shared = {
        "wo_s": pack_slabs(np.ascontiguousarray(wo.T), 512),
        "ident": np.eye(128, dtype=np.float32),
        "trimask": (k_idx <= q_idx).astype(np.float32),
    }
    percore = [None] * 8
    for bb in range(2):
        cores = [bb * 4 + c for c in range(4)]
        ktg = np.zeros((128, 32, 4096), wt)
        vg = np.zeros((4096, 32, 129), wt)
        for c in range(4):
            co = cores[c]
            kt_c = conv(s5a_outs[co]["kt"])
            vt_c = conv(s5a_outs[co]["vt"]).reshape(1024, 32, 128)
            for half, u in ((0, c), (1, 7 - c)):
                ktg[:, :, u * 512:(u + 1) * 512] = kt_c[:, :, half * 512:(half + 1) * 512]
                vg[u * 512:(u + 1) * 512, :, :128] = vt_c[half * 512:(half + 1) * 512]
                vg[u * 512:(u + 1) * 512, :, 128] = ONE
        for c in range(4):
            co = cores[c]
            kta = ktg[:, :, :NKA * 128].copy()
            kta[:, :, 512 * c:] = 0
            ktb = ktg[:, :, :NKB * 128].copy()
            ktb[:, :, 512 * (7 - c):] = 0
            va = vg[:NKA * 128].copy()
            va[512 * c:] = 0
            vb = vg[:NKB * 128].copy()
            vb[512 * (7 - c):] = 0
            vd = np.zeros((1024, 32, 129), wt)
            vd[:, :, :128] = conv(s5a_outs[co]["vt"]).reshape(1024, 32, 128)
            vd[:, :, 128] = ONE

            def vhead(vv, nkb):
                a = vv.reshape(nkb, 128, 32, 129).transpose(2, 1, 0, 3)
                return out(a.reshape(32, 128, nkb * 129))

            percore[co] = {
                "qt": out(conv(s5a_outs[co]["qt"])),
                "kta": out(kta),
                "ktb": out(ktb),
                "ktd": out(conv(s5a_outs[co]["kt"])),
                "vta": vhead(va, NKA),
                "vtb": vhead(vb, NKB),
                "vtd": vhead(vd, 8),
            }
    return shared, percore


def ctx_join(ctx):
    for e in ENGINES:
        for s in ("pe", "act", "dve", "dst0", "dst1"):
            if s in ctx.c:
                ctx.wait(e, s)


def build_megakernel(upto=7):
    nc = bass.Bass("TRN2", target_bir_lowering=False)
    EI, EO = "ExternalInput", "ExternalOutput"
    xin_h = nc.dram_tensor("xin_h", [TIN, D], F32, kind=EI).ap()
    wup = nc.dram_tensor("wup_s", [8, 128, 32 * 512], F32, kind=EI).ap()
    gate_t = nc.dram_tensor("gate_t", [D, 4], F32, kind=EI).ap()
    ebias = nc.dram_tensor("ebias", [1, 4], F32, kind=EI).ap()
    lora_at = nc.dram_tensor("lora_at", [D, 32], F32, kind=EI).ap()
    lora_bs = nc.dram_tensor("lora_bs", [32, D], F32, kind=EI).ap()
    t8 = nc.dram_tensor("t8", [4, 32], F32, kind=EI).ap()
    ident = nc.dram_tensor("ident", [128, 128], F32, kind=EI).ap()
    convw = nc.dram_tensor("convw_s", [3, 6, 128, 32 * 512], F32, kind=EI).ap()
    convb = nc.dram_tensor("convb", [1, 3072], F32, kind=EI).ap()
    wq0 = nc.dram_tensor("wq_r", [1024, 1024], F32, kind=EI).ap()
    wk0 = nc.dram_tensor("wk_r", [1024, 1024], F32, kind=EI).ap()
    wv0 = nc.dram_tensor("wv_r", [1024, 1024], F32, kind=EI).ap()
    wo0 = nc.dram_tensor("wo_r", [1024, 1024], F32, kind=EI).ap()
    bqc = nc.dram_tensor("bq_c", [128, 8], F32, kind=EI).ap()
    bkc = nc.dram_tensor("bk_c", [128, 8], F32, kind=EI).ap()
    bvr = nc.dram_tensor("bv_r", [1, 1024], F32, kind=EI).ap()
    bor = nc.dram_tensor("bo_r", [1, 1024], F32, kind=EI).ap()
    projw = nc.dram_tensor("projw_s", [8, 128, 8 * 512], F32, kind=EI).ap()
    projb = nc.dram_tensor("projb", [1, D], F32, kind=EI).ap()
    trimask = nc.dram_tensor("trimask", [128, 128], F32, kind=EI).ap()
    a2 = nc.dram_tensor("a2", [128, 64], F32, kind=EI).ap()
    b2 = nc.dram_tensor("b2", [128, 64], F32, kind=EI).ap()
    betar = nc.dram_tensor("betar", [64, 64], F32, kind=EI).ap()
    gcols = nc.dram_tensor("gcols", [128, 32], F32, kind=EI).ap()
    biasr = nc.dram_tensor("biasr", [1, D], F32, kind=EI).ap()
    wqs = nc.dram_tensor("wq_s", [8, 128, 32 * 512], F32, kind=EI).ap()
    wks = nc.dram_tensor("wk_s", [8, 128, 32 * 512], F32, kind=EI).ap()
    wvs = nc.dram_tensor("wv_s", [8, 128, 32 * 512], F32, kind=EI).ap()
    wos = nc.dram_tensor("wo_s", [8, 128, 32 * 512], F32, kind=EI).ap()
    cs = nc.dram_tensor("cs_cols", [TLOC, 6], F32, kind=EI).ap()
    vmka = nc.dram_tensor("vmka", [1, NKA * 129], F32, kind=EI).ap()
    vmkb = nc.dram_tensor("vmkb", [1, NKB * 129], F32, kind=EI).ap()
    dmask = nc.dram_tensor("dmask", [128, 4, 512], F32, kind=EI).ap()
    xout = nc.dram_tensor("xout", [TLOC, D], F32, kind=EO).ap()
    x1h = nc.dram_tensor("x1h", [TIN, D], F32).ap()
    x2 = nc.dram_tensor("x2", [TLOC, D], F32).ap()
    x3 = nc.dram_tensor("x3", [TLOC, D], F32).ap()
    x4 = nc.dram_tensor("x4", [TLOC, D], F32).ap()
    x5 = nc.dram_tensor("x5", [TLOC, D], F32).ap()
    qt_t = nc.dram_tensor("qt_i", [128, 32, TLOC], BF16)
    kt_t = nc.dram_tensor("kt_i", [128, 32, TLOC], BF16)
    vhm_t = nc.dram_tensor("vhm_i", [256, 128, 129], BF16)
    agk_t = nc.dram_tensor("agk_i", [512, 32 * TLOC], BF16)
    agv_t = nc.dram_tensor("agv_i", [1024, 128 * 129], BF16)
    cdram = nc.dram_tensor("c_scratch", [64, 64], F32).ap()
    qt, kt, vhm = qt_t.ap(), kt_t.ap(), vhm_t.ap()
    agk, agv = agk_t.ap(), agv_t.ap()

    tiles9 = [(i * 128, 0, 128) for i in range(8)] + [(TIN - 128, 124, 128)]

    with ExitStack() as es:
        ctx = Ctx(nc, es)
        ctx.sems("dc", "dcy", "dcy2", "dx0", "dx1", "dxs0", "dxs1", "dxs2", "dxs3",
                 "dw0", "dw1", "kv0", "kv1", "pe", "act", "dve", "dst0", "dst1", "cc")

        stages = []

        def stage(fn):
            stages.append(fn)

        @stage
        def s1():
            with ExitStack() as s:
                emit_l0ffn(sub(ctx, s), xin_h, x1h, wup, gate_t, ebias, lora_at,
                           lora_bs, t8, ident, pre="s1_", tiles=tiles9, ncol=256)

        @stage
        def s2():
            with ExitStack() as s:
                emit_l0attn(sub(ctx, s), x1h, x2, convw, convb, wq0, wk0, wv0, wo0,
                            bqc, bkc, bvr, bor, projw, projb, ident, trimask)

        @stage
        def s3():
            with ExitStack() as s:
                emit_l0ffn(sub(ctx, s), x2, x3, wup, gate_t, ebias, lora_at,
                           lora_bs, t8, ident, pre="s3_")

        @stage
        def s4():
            with ExitStack() as s:
                emit_l1ffn(sub(ctx, s), x3, x4, a2, b2, betar, gcols, biasr,
                           ident, cdram, pre="s4_")

        @stage
        def s5a():
            with ExitStack() as s:
                emit_l1attn_a(sub(ctx, s), x4, wqs, wks, wvs, cs, ident, qt, kt, vhm,
                              kt2d=kt.rearrange("p h t -> p (h t)"),
                              vhm2d=vhm.rearrange("r p e -> r (p e)"),
                              agk=agk, agv=agv)

        @stage
        def s5b():
            ctx.wait("sync", "cc")
            with ExitStack() as s:
                emit_l1attn_b(sub(ctx, s), x4, qt, kt, vhm, agk, agv, wos,
                              ident, trimask, vmka, vmkb, dmask, x5)

        @stage
        def s6():
            with ExitStack() as s:
                emit_l1ffn(sub(ctx, s), x5, xout, a2, b2, betar, gcols, biasr,
                           ident, cdram, pre="s6_")

        for i, fn in enumerate(stages[:upto]):
            if i:
                ctx_join(ctx)
            fn()

        if upto < 7:
            ctx_join(ctx)
            srcx = [None, x1h, x2, x3, x4, None, x5][upto]
            if upto == 1:
                ctx.op("sync", lambda sy: sy.dma_start(out=xout[0:512, :], in_=x1h[1:513, :]), "dst0", 16)
                ctx.op("sync", lambda sy: sy.dma_start(out=xout[512:1024, :], in_=x1h[515:1027, :]), "dst1", 16)
            elif srcx is not None:
                ctx.op("sync", lambda sy, srcx=srcx: sy.dma_start(out=xout[:], in_=srcx[:]), "dst0", 16)

        ctx.wait("sync", "dst0")
        ctx.wait("sync", "dst1")
        ctx.flush()
    return nc


def stage_megakernel_inputs(inputs):
    shared = {}
    shared.update(stage_l0ffn_weights(inputs))
    shared.update(stage_l0attn_weights(inputs))
    shared.update(stage_l1ffn_weights(inputs))
    shared.update(stage_l1attn_a_weights(inputs))
    wo = np.asarray(inputs["L1_attn_wo"], np.float32)
    shared["wo_s"] = pack_slabs(np.ascontiguousarray(wo.T), 512)
    dm = np.zeros((128, 4, 512), np.float32)
    kk, qq = np.meshgrid(np.arange(128), np.arange(128), indexing="ij")
    tri = (kk <= qq).astype(np.float32)
    for j in range(4):
        for qb in range(4):
            dm[:, j, qb * 128:(qb + 1) * 128] = 0.0 if qb < j else (tri if qb == j else 1.0)
    shared["dmask"] = dm
    percore = []
    for core in range(NCORES):
        cpr = core % 4
        mka = np.zeros((1, NKA * 129), np.float32)
        mkb = np.zeros((1, NKB * 129), np.float32)
        mka[0, :4 * cpr * 129] = 1.0
        mkb[0, :4 * (7 - cpr) * 129] = 1.0
        percore.append({
            "cs_cols": cs_cols_for(core),
            "vmka": mka,
            "vmkb": mkb,
        })
    return shared, percore


_PROGS = {}


def _prog(name, builder):
    if name not in _PROGS:
        _PROGS[name] = builder()
    return _PROGS[name]


def kernel(**inputs):
    from concourse.bass_utils import run_bass_kernel_spmd
    x = np.ascontiguousarray(np.asarray(inputs["L0_attn_inputs"], np.float32))
    nc = _prog("mega", build_megakernel)
    shared, percore = stage_megakernel_inputs(inputs)
    in_maps = []
    for core in range(NCORES):
        m = dict(shared)
        m.update(percore[core])
        m["xin_h"] = slice_haloed(x, core)
        in_maps.append(m)
    res = run_bass_kernel_spmd(nc, in_maps, core_ids=list(range(NCORES)))
    sh = [res.results[c]["xout"] for c in range(NCORES)]
    out = np.empty(x.shape, sh[0].dtype)
    unslice_owned(sh, out)
    return out


# version 07 (final)
# speedup vs baseline: 1.2131x; 1.1987x over previous
from contextlib import ExitStack

import numpy as np

import concourse.bass as bass
import concourse.mybir as mybir

F32 = mybir.dt.float32
BF16 = mybir.dt.bfloat16
AF = mybir.ActivationFunctionType
ALU = mybir.AluOpType
AX = mybir.AxisListType

D = 4096
NCORES = 8
TLOC = 1024
NTB = TLOC // 128
LN_EPS = 1e-5

ENGINES = ("tensor", "vector", "scalar", "gpsimd", "sync")


def chunk_ranges(core):
    b, c = divmod(core, 4)
    a0 = 512 * c
    b0 = 512 * (7 - c)
    return b, (a0, a0 + 512), (b0, b0 + 512)


def slice_owned(x_full, core):
    b, (a0, a1), (b0, b1) = chunk_ranges(core)
    return np.ascontiguousarray(
        np.concatenate([x_full[b, a0:a1], x_full[b, b0:b1]], axis=0))


def unslice_owned(shards, out):
    for core in range(NCORES):
        b, (a0, a1), (b0, b1) = chunk_ranges(core)
        out[b, a0:a1] = shards[core][:512]
        out[b, b0:b1] = shards[core][512:]
    return out


def pack_slabs(w_in_out, slab_cols):
    IN, OUT = w_in_out.shape
    nk = IN // 128
    ns = OUT // slab_cols
    w = w_in_out.reshape(nk, 128, ns, slab_cols).transpose(2, 1, 0, 3)
    return np.ascontiguousarray(w.reshape(ns, 128, nk * slab_cols))


class Ctx:

    def __init__(self, nc, es):
        self.nc = nc
        self.es = es
        self.s = {}
        self.c = {}
        self.q = {e: [] for e in ENGINES}

    def sems(self, *names):
        for n in names:
            if n not in self.s:
                self.s[n] = self.es.enter_context(self.nc.semaphore(n))
                self.c[n] = 0

    def sb(self, name, shape, dtype):
        return self.es.enter_context(self.nc.sbuf_tensor(name, shape, dtype))

    def ps(self, name, shape, dtype=F32):
        return self.es.enter_context(self.nc.psum_tensor(name, shape, dtype))

    def wait(self, eng, sem, val=None):
        if val is None:
            val = self.c[sem]
        if val <= 0:
            return
        h = self.s[sem]
        self.q[eng].append(lambda e: e.wait_ge(h, val))

    def op(self, eng, fn, sem=None, n=1):
        if sem is None:
            self.q[eng].append(fn)
            return None
        h = self.s[sem]
        self.c[sem] += n
        self.q[eng].append(lambda e: fn(e).then_inc(h, n))
        return self.c[sem]

    def bump(self, eng, sem, n=1):
        h = self.s[sem]
        self.c[sem] += n
        self.q[eng].append(lambda e: e.sem_inc(h, n))
        return self.c[sem]

    def flush(self):
        with self.nc.Block() as block:
            for name in ENGINES:
                thunks = self.q[name]
                if not thunks:
                    continue
                getattr(block, name)(lambda e, th=tuple(thunks): [f(e) for f in th])
        self.q = {e: [] for e in ENGINES}


def alloc_stats(ctx, tag, sq=None):
    return {
        "sq": sq if sq is not None else ctx.sb(f"{tag}_sq", [128, D], BF16),
        "ss": ctx.sb(f"{tag}_ss", [128, 1], F32),
        "sm": ctx.sb(f"{tag}_sm", [128, 1], F32),
        "mu": ctx.sb(f"{tag}_mu", [128, 1], F32),
        "var": ctx.sb(f"{tag}_var", [128, 1], F32),
        "rvar": ctx.sb(f"{tag}_rvar", [128, 1], F32),
        "rstd": ctx.sb(f"{tag}_rstd", [128, 1], F32),
        "nmr": ctx.sb(f"{tag}_nmr", [128, 1], F32),
    }


def emit_ln_stats(ctx, xb, st, sq=None, ready=None):
    V, S = "vector", "scalar"
    sq = st["sq"] if sq is None else sq
    if ready is not None:
        ctx.wait(S, *ready)
        ctx.wait(V, *ready)
    ss_n = ctx.op(S, lambda s: s.activation(out=sq[:], in_=xb[:], func=AF.Square,
                                            accum_out=st["ss"][:]), "act")
    ctx.op(V, lambda v: v.reduce_sum(st["sm"][:], xb[:], axis=AX.X), "dve")
    ctx.wait(V, "dve")
    ctx.op(V, lambda v: v.tensor_scalar(out=st["mu"][:], in0=st["sm"][:],
                                        scalar1=1.0 / D, scalar2=None, op0=ALU.mult), "dve")
    ctx.wait(V, "dve")
    ctx.op(V, lambda v: v.tensor_tensor(out=st["var"][:], in0=st["mu"][:],
                                        in1=st["mu"][:], op=ALU.mult), "dve")
    ctx.wait(V, "dve")
    ctx.wait(V, "act", ss_n)
    ctx.op(V, lambda v: v.scalar_tensor_tensor(
        out=st["var"][:], in0=st["ss"][:], scalar=1.0 / D, in1=st["var"][:],
        op0=ALU.mult, op1=ALU.subtract), "dve")
    ctx.wait(V, "dve")
    ctx.op(V, lambda v: v.tensor_scalar(out=st["var"][:], in0=st["var"][:],
                                        scalar1=LN_EPS, scalar2=None, op0=ALU.add), "dve")
    ctx.wait(V, "dve")
    ctx.op(V, lambda v: v.reciprocal(st["rvar"][:], st["var"][:]), "dve")
    ctx.wait(S, "dve")
    ctx.op(S, lambda s: s.activation(out=st["rstd"][:], in_=st["rvar"][:],
                                     func=AF.Sqrt), "act")
    ctx.wait(V, "act")
    return ctx.op(V, lambda v: v.scalar_tensor_tensor(
        out=st["nmr"][:], in0=st["mu"][:], scalar=-1.0, in1=st["rstd"][:],
        op0=ALU.mult, op1=ALU.mult), "dve")


def build_l0ffn_program():
    nc = bass.Bass("TRN2", target_bir_lowering=False)
    xin = nc.dram_tensor("xin", [TLOC, D], F32, kind="ExternalInput").ap()
    wup = nc.dram_tensor("wup_s", [8, 128, 32 * 512], F32, kind="ExternalInput").ap()
    gate_t = nc.dram_tensor("gate_t", [D, 4], F32, kind="ExternalInput").ap()
    ebias = nc.dram_tensor("ebias", [1, 4], F32, kind="ExternalInput").ap()
    lora_at = nc.dram_tensor("lora_at", [D, 32], F32, kind="ExternalInput").ap()
    lora_bs = nc.dram_tensor("lora_bs", [32, D], F32, kind="ExternalInput").ap()
    t8 = nc.dram_tensor("t8", [4, 32], F32, kind="ExternalInput").ap()
    ident = nc.dram_tensor("ident", [128, 128], F32, kind="ExternalInput").ap()
    xout = nc.dram_tensor("xout", [TLOC, D], F32, kind="ExternalOutput").ap()

    with ExitStack() as es:
        ctx = Ctx(nc, es)
        emit_l0ffn(ctx, xin, xout, wup, gate_t, ebias, lora_at, lora_bs, t8, ident)
        ctx.wait("sync", "dst0")
        ctx.wait("sync", "dst1")
        ctx.flush()
    return nc


def emit_l0ffn(ctx, xin, xout, wup, gate_t, ebias, lora_at, lora_bs, t8, ident,
               pre="", tiles=None, ncol=512):
    if tiles is None:
        tiles = [(i * 128, 0, 128) for i in range(NTB)]
    nt = len(tiles)
    noc = D // ncol
    nsub = 512 // ncol
    ctx.sems("dc", "dx0", "dx1", "dxs0", "dxs1", "dxs2", "dxs3", "dw0", "dw1",
             "pe", "act", "dve", "dst0", "dst1")
    T, V, S, G, Y = "tensor", "vector", "scalar", "gpsimd", "sync"

    gate_sb = ctx.sb(pre + "gate_sb", [128, 32, 4], BF16)
    ebias_sb = ctx.sb(pre + "ebias_sb", [128, 4], F32)
    la_sb = ctx.sb(pre + "la_sb", [128, 32, 32], BF16)
    lb_sb = ctx.sb(pre + "lb_sb", [32, D], BF16)
    t8_sb = ctx.sb(pre + "t8_sb", [4, 32], BF16)
    id_sb = ctx.sb(pre + "id_sb", [128, 128], BF16)
    xb = [ctx.sb(pre + f"xb{i}", [128, D], F32) for i in range(2)]
    xnb = [ctx.sb(pre + f"xnb{i}", [128, D], BF16) for i in range(2)]
    xnt = ctx.sb(pre + "xnt", [128, nt * D], BF16)
    wsl = [ctx.sb(pre + f"wsl{i}", [128, 32, ncol], BF16) for i in range(2)]
    zsel = ctx.sb(pre + "zsel", [32, nt * 128], BF16)
    logits = ctx.sb(pre + "logits", [128, 4], F32)
    gmax = ctx.sb(pre + "gmax", [128, 1], F32)
    onehot = ctx.sb(pre + "onehot", [128, 4], BF16)
    onehT = ctx.sb(pre + "onehT", [4, 128], BF16)
    mask_sb = ctx.sb(pre + "mask_sb", [32, 128], BF16)
    xsl = [ctx.sb(pre + f"xsl{i}", [128, ncol], F32) for i in range(4)]
    xo_sb = [ctx.sb(pre + f"xo{i}", [128, ncol], F32) for i in range(2)]
    st = alloc_stats(ctx, pre + "st", sq=xnb[0])

    psT = [ctx.ps(pre + f"psT{i}", [128, 512], BF16) for i in range(2)]
    psg = ctx.ps(pre + "psg", [128, 4])
    psoh = ctx.ps(pre + "psoh", [4, 128], BF16)
    psm = ctx.ps(pre + "psm", [32, 128])
    psz = ctx.ps(pre + "psz", [32, 128])
    pso = [ctx.ps(pre + f"pso{i}", [128, ncol]) for i in range(2)]

    for fn in (
        lambda g: g.dma_start(out=gate_sb[:], in_=gate_t.rearrange("(k p) e -> p k e", p=128)),
        lambda g: g.dma_start(out=la_sb[:], in_=lora_at.rearrange("(k p) e -> p k e", p=128)),
        lambda g: g.dma_start(out=lb_sb[:], in_=lora_bs[:]),
        lambda g: g.dma_start(out=t8_sb[:], in_=t8[:]),
        lambda g: g.dma_start(out=id_sb[:], in_=ident[:]),
        lambda g: g.dma_start(out=ebias_sb[:], in_=ebias.broadcast_to([128, 4])),
    ):
        ctx.wait(G, "dc")
        ctx.op(G, fn, "dc", 16)
    consts = ctx.c["dc"]

    wsl_ready = [None, None]
    wsl_free = [0, 0]
    def _slab_src(oc):
        if nsub == 1:
            return wup[oc].rearrange("p (k j) -> p k j", j=ncol)
        return wup[oc // nsub].rearrange("p (k j) -> p k j", j=512)[
            :, :, (oc % nsub) * ncol:(oc % nsub + 1) * ncol]

    ctx.wait(G, "dc", consts)
    ctx.op(G, lambda g: g.dma_start(out=wsl[0][:], in_=_slab_src(0)), "dw0", 16)
    wsl_ready[0] = ("dw0", ctx.c["dw0"])

    xb_ready = [None] * nt
    for i in range(min(2, nt)):
        r0 = tiles[i][0]
        ctx.op(Y, lambda sy, i=i, r0=r0: sy.dma_start(
            out=xb[i][:], in_=xin[r0:r0 + 128, :]), f"dx{i}", 16)
        xb_ready[i] = (f"dx{i}", ctx.c[f"dx{i}"])

    tp_done = [0] * nt
    gate_done = [0] * nt
    xnb_free = [0, 0]
    psT_free = [0, 0]
    psg_free = 0
    psz_free = 0
    psoh_free = 0
    psm_free = 0
    onehot_free = 0
    onehT_free = 0
    mask_free = 0

    for tb in range(nt):
        par = tb % 2
        r0 = tiles[tb][0]

        if tb >= 2:
            ctx.wait("scalar", "pe", xnb_free[par])
        emit_ln_stats(ctx, xb[par], st, sq=xnb[par], ready=xb_ready[tb])
        ctx.wait(V, "dve")
        if tb >= 2:
            ctx.wait(V, "pe", xnb_free[par])
        ln_apply = ctx.op(V, lambda v, par=par: v.tensor_scalar(
            out=xnb[par][:], in0=xb[par][:], scalar1=st["rstd"][:],
            scalar2=st["nmr"][:], op0=ALU.mult, op1=ALU.add), "dve")

        if tb + 2 < nt:
            ctx.wait(Y, "dve", ln_apply)
            nr0 = tiles[tb + 2][0]
            ctx.op(Y, lambda sy, nr0=nr0, par=par: sy.dma_start(
                out=xb[par][:], in_=xin[nr0:nr0 + 128, :]), f"dx{par}", 16)
            xb_ready[tb + 2] = (f"dx{par}", ctx.c[f"dx{par}"])

        ctx.wait(T, "dc", consts)
        ctx.wait(T, "dve", ln_apply)
        for f4 in range(8):
            tpar = f4 % 2
            ctx.wait(T, "dve", psT_free[tpar])
            for ff in range(4):
                f = f4 * 4 + ff
                pe_n = ctx.op(T, lambda t, par=par, tpar=tpar, f=f, ff=ff: t.transpose(
                    psT[tpar][:, ff * 128:(ff + 1) * 128],
                    xnb[par][:, f * 128:(f + 1) * 128], id_sb[:]),
                    "pe" if ff == 3 else None)
            ctx.wait(V, "pe", pe_n)
            psT_free[tpar] = ctx.op(V, lambda v, tb=tb, tpar=tpar, f4=f4: v.tensor_copy(
                xnt[:, tb * D + f4 * 512: tb * D + (f4 + 1) * 512], psT[tpar][:]), "dve")
        xnb_free[par] = ctx.c["pe"]
        tp_done[tb] = ctx.c["dve"]

        ctx.wait(T, "dve", max(psg_free, tp_done[tb]))
        for k in range(32):
            ctx.op(T, lambda t, tb=tb, k=k: t.matmul(
                psg[:], xnt[:, tb * D + k * 128:tb * D + (k + 1) * 128],
                gate_sb[:, k, :],
                start=(k == 0), stop=(k == 31)), "pe" if k == 31 else None)
        ctx.wait(V, "pe")
        ctx.wait(V, "dc", consts)
        psg_free = ctx.op(V, lambda v: v.tensor_tensor(out=logits[:], in0=psg[:],
                                            in1=ebias_sb[:], op=ALU.add), "dve")
        ctx.wait(V, "dve")
        ctx.op(V, lambda v: v.reduce_max(gmax[:], logits[:], axis=AX.X), "dve")
        ctx.wait(V, "dve")
        ctx.wait(V, "pe", onehot_free)
        oh_n = ctx.op(V, lambda v: v.tensor_scalar(
            out=onehot[:], in0=logits[:], scalar1=gmax[:], scalar2=None,
            op0=ALU.is_equal), "dve")
        ctx.wait(T, "dve", max(oh_n, psoh_free))
        ohT_pe = ctx.op(T, lambda t: t.transpose(psoh[:], onehot[:], id_sb[:]), "pe")
        onehot_free = ohT_pe
        ctx.wait(V, "pe", max(ohT_pe, onehT_free))
        ohT_n = ctx.op(V, lambda v: v.tensor_copy(onehT[:], psoh[:]), "dve")
        psoh_free = ohT_n
        ctx.wait(T, "dve", ohT_n)
        ctx.wait(T, "act", psm_free)
        m_pe = ctx.op(T, lambda t: t.matmul(psm[:], t8_sb[:], onehT[:],
                                            start=True, stop=True), "pe")
        onehT_free = m_pe
        ctx.wait(S, "pe", m_pe)
        ctx.wait(S, "dve", mask_free)
        msk_n = ctx.op(S, lambda s: s.activation(out=mask_sb[:], in_=psm[:],
                                                 func=AF.Copy), "act")
        psm_free = msk_n
        ctx.wait(T, "dve", psz_free)
        for k in range(32):
            ctx.op(T, lambda t, tb=tb, k=k: t.matmul(
                psz[:], la_sb[:, k, :],
                xnt[:, tb * D + k * 128:tb * D + (k + 1) * 128],
                start=(k == 0), stop=(k == 31)), "pe" if k == 31 else None)
        ctx.wait(V, "pe")
        ctx.wait(V, "act", msk_n)
        gate_done[tb] = ctx.op(V, lambda v, tb=tb: v.tensor_tensor(
            out=zsel[:, tb * 128:(tb + 1) * 128], in0=psz[:], in1=mask_sb[:],
            op=ALU.mult), "dve")
        psz_free = gate_done[tb]
        mask_free = gate_done[tb]

    pso_free = [0, 0]
    xsl_free = [0] * 4
    xo_free = [0, 0]
    it = 0
    for oc in range(noc):
        wpar = oc % 2
        if oc + 1 < noc:
            npar = (oc + 1) % 2
            ctx.wait(G, "pe", wsl_free[npar])
            ctx.op(G, lambda g, oc=oc, npar=npar: g.dma_start(
                out=wsl[npar][:], in_=_slab_src(oc + 1)), f"dw{npar}", 16)
            wsl_ready[npar] = (f"dw{npar}", ctx.c[f"dw{npar}"])

        for tb in range(nt):
            par = it % 2
            xp = it % 4
            r0 = tiles[tb][0]
            ctx.wait(Y, "dve", xsl_free[xp])
            ctx.op(Y, lambda sy, r0=r0, oc=oc, xp=xp: sy.dma_start(
                out=xsl[xp][:], in_=xin[r0:r0 + 128, oc * ncol:(oc + 1) * ncol]),
                f"dxs{xp}", 16)
            xsl_n = (f"dxs{xp}", ctx.c[f"dxs{xp}"])

            ctx.wait(T, *wsl_ready[wpar])
            ctx.wait(T, "dve", max(tp_done[tb], pso_free[par]))
            for k in range(32):
                ctx.op(T, lambda t, tb=tb, k=k, par=par, wpar=wpar: t.matmul(
                    pso[par][:], xnt[:, tb * D + k * 128:tb * D + (k + 1) * 128],
                    wsl[wpar][:, k, :],
                    start=(k == 0), stop=False))
            ctx.wait(T, "dve", gate_done[tb])
            mm_n = ctx.op(T, lambda t, tb=tb, oc=oc, par=par: t.matmul(
                pso[par][:], zsel[:, tb * 128:(tb + 1) * 128],
                lb_sb[0:32, oc * ncol:(oc + 1) * ncol],
                start=False, stop=True), "pe")
            if tb == nt - 1:
                wsl_free[wpar] = mm_n

            ctx.wait(V, "pe", mm_n)
            ctx.wait(V, *xsl_n)
            ctx.wait(V, f"dst{par}", xo_free[par])
            ep_n = ctx.op(V, lambda v, par=par, xp=xp: v.scalar_tensor_tensor(
                out=xo_sb[par][:], in0=pso[par][:], scalar=0.5, in1=xsl[xp][:],
                op0=ALU.mult, op1=ALU.add), "dve")
            pso_free[par] = ep_n
            xsl_free[xp] = ep_n
            lo, hi = tiles[tb][1], tiles[tb][2]
            ctx.wait(Y, "dve", ep_n)
            xo_free[par] = ctx.op(Y, lambda sy, r0=r0, oc=oc, par=par, lo=lo, hi=hi:
                                  sy.dma_start(
                out=xout[r0 + lo:r0 + hi, oc * ncol:(oc + 1) * ncol],
                in_=xo_sb[par][lo:hi, :]), f"dst{par}", 16)
            it += 1


def stage_l0ffn_weights(inp):
    W_up = np.asarray(inp["L0_ffn_W_up"], np.float32)
    gate = np.asarray(inp["L0_ffn_gate_w"], np.float32)
    eb = np.asarray(inp["L0_ffn_expert_bias"], np.float32)
    A = np.asarray(inp["L0_ffn_lora_A"], np.float32)
    B = np.asarray(inp["L0_ffn_lora_B"], np.float32)
    t8 = np.zeros((4, 32), np.float32)
    for e in range(4):
        t8[e, e * 8:(e + 1) * 8] = 1.0
    return {
        "wup_s": pack_slabs(np.ascontiguousarray(W_up.T), 512),
        "gate_t": np.ascontiguousarray(gate.T),
        "ebias": eb.reshape(1, 4),
        "lora_at": np.ascontiguousarray(A.transpose(2, 0, 1).reshape(D, 32)),
        "lora_bs": np.ascontiguousarray(B.transpose(0, 2, 1).reshape(32, D)),
        "t8": t8,
        "ident": np.eye(128, dtype=np.float32),
    }



TIN = 1028


def xin_row(owned_idx):
    return 1 + owned_idx if owned_idx < 512 else 3 + owned_idx


def sub(ctx, es):
    c = Ctx.__new__(Ctx)
    c.__dict__.update(ctx.__dict__)
    c.es = es
    return c


def build_l0attn_program():
    nc = bass.Bass("TRN2", target_bir_lowering=False)
    xin = nc.dram_tensor("xin_h", [TIN, D], F32, kind="ExternalInput").ap()
    convw = nc.dram_tensor("convw_s", [3, 6, 128, 32 * 512], F32, kind="ExternalInput").ap()
    convb = nc.dram_tensor("convb", [1, 3072], F32, kind="ExternalInput").ap()
    wq = nc.dram_tensor("wq_r", [1024, 1024], F32, kind="ExternalInput").ap()
    wk = nc.dram_tensor("wk_r", [1024, 1024], F32, kind="ExternalInput").ap()
    wv = nc.dram_tensor("wv_r", [1024, 1024], F32, kind="ExternalInput").ap()
    wo = nc.dram_tensor("wo_r", [1024, 1024], F32, kind="ExternalInput").ap()
    bqc = nc.dram_tensor("bq_c", [128, 8], F32, kind="ExternalInput").ap()
    bkc = nc.dram_tensor("bk_c", [128, 8], F32, kind="ExternalInput").ap()
    bvr = nc.dram_tensor("bv_r", [1, 1024], F32, kind="ExternalInput").ap()
    bor = nc.dram_tensor("bo_r", [1, 1024], F32, kind="ExternalInput").ap()
    projw = nc.dram_tensor("projw_s", [8, 128, 8 * 512], F32, kind="ExternalInput").ap()
    projb = nc.dram_tensor("projb", [1, D], F32, kind="ExternalInput").ap()
    ident = nc.dram_tensor("ident", [128, 128], F32, kind="ExternalInput").ap()
    trimask = nc.dram_tensor("trimask", [128, 128], F32, kind="ExternalInput").ap()
    xout = nc.dram_tensor("xout", [TLOC, D], F32, kind="ExternalOutput").ap()

    with ExitStack() as es:
        ctx = Ctx(nc, es)
        emit_l0attn(ctx, xin, xout, convw, convb, wq, wk, wv, wo, bqc, bkc,
                    bvr, bor, projw, projb, ident, trimask)
        ctx.wait("sync", "dst0")
        ctx.wait("sync", "dst1")
        ctx.flush()
    return nc


def emit_l0attn(ctx, xin, xout, convw, convb, wq, wk, wv, wo, bqc, bkc,
                bvr, bor, projw, projb, ident, trimask):
    ctx.sems("dc", "dx0", "dx1", "dxs0", "dxs1", "dxs2", "dxs3",
             "dw0", "dw1", "pe", "act", "dve", "dst0", "dst1")
    T, V, S, G, Y = "tensor", "vector", "scalar", "gpsimd", "sync"

    id_sb = ctx.sb("id_sb", [128, 128], BF16)
    tri_sb = ctx.sb("tri_sb", [128, 128], BF16)
    ones_sb = ctx.sb("ones_sb", [1, 128], BF16)
    convb_sb = ctx.sb("convb_sb", [1, 3072], BF16)
    bq_sb = ctx.sb("bq_sb", [128, 8], F32)
    bk_sb = ctx.sb("bk_sb", [128, 8], F32)
    bv_sb = ctx.sb("bv_sb", [1, 1024], BF16)
    bo_sb = ctx.sb("bo_sb", [1, 1024], BF16)
    projb_sb = ctx.sb("projb_sb", [1, D], BF16)
    qkvc = ctx.sb("qkvc", [128, NTB, 3072], BF16)

    for fn in (
        lambda g: g.dma_start(out=id_sb[:], in_=ident[:]),
        lambda g: g.dma_start(out=tri_sb[:], in_=trimask[:]),
        lambda g: g.dma_start(out=convb_sb[:], in_=convb[:]),
        lambda g: g.dma_start(out=bv_sb[:], in_=bvr[:]),
        lambda g: g.dma_start(out=bo_sb[:], in_=bor[:]),
        lambda g: g.dma_start(out=projb_sb[:], in_=projb[:]),
    ):
        ctx.wait(G, "dc")
        ctx.op(G, fn, "dc", 16)
    ctx.wait(G, "dc")
    ctx.op(G, lambda g: g.dma_start(out=bq_sb[:], in_=bqc[:]), "dc", 16)
    ctx.wait(G, "dc")
    ctx.op(G, lambda g: g.dma_start(out=bk_sb[:], in_=bkc[:]), "dc", 16)
    consts = ctx.c["dc"]
    ctx.op(V, lambda v: v.memset(ones_sb[:], 1.0), "dve")
    ctx.wait(V, "dc", consts)
    ctx.op(V, lambda v: v.tensor_scalar(out=bq_sb[:], in0=bq_sb[:], scalar1=0.125,
                                        scalar2=None, op0=ALU.mult), "dve")
    bq_ready = ctx.c["dve"]
    ones_ready = ctx.c["dve"]

    qkvc_ready = [0] * NTB
    ab0 = [xin_row(128 * i) for i in range(NTB)]

    with ExitStack() as p12:
        c12 = sub(ctx, p12)
        xnt = c12.sb("xnt2", [128, 32, TIN], BF16)

        with ExitStack() as p1:
            c1 = sub(ctx, p1)
            xb = [c1.sb(f"a_xb{i}", [128, D], F32) for i in range(2)]
            xnb = [c1.sb(f"a_xnb{i}", [128, D], BF16) for i in range(2)]
            st = alloc_stats(c1, "a_st", sq=xnb[0])
            psT = [c1.ps(f"a_psT{i}", [128, 512], BF16) for i in range(2)]

            t0s = [min(i * 128, TIN - 128) for i in range(9)]
            xb_ready = [None] * 9
            for i in range(2):
                ctx.op(Y, lambda sy, i=i: sy.dma_start(
                    out=xb[i][:], in_=xin[t0s[i]:t0s[i] + 128, :]), f"dx{i % 2}", 16)
                xb_ready[i] = (f"dx{i % 2}", ctx.c[f"dx{i % 2}"])

            xnb_free = [0, 0]
            psT_free = [0, 0]
            for i in range(9):
                par = i % 2
                if i >= 2:
                    ctx.wait(S, "pe", xnb_free[par])
                emit_ln_stats(ctx, xb[par], st, sq=xnb[par], ready=xb_ready[i])
                ctx.wait(V, "dve")
                if i >= 2:
                    ctx.wait(V, "pe", xnb_free[par])
                ln_apply = ctx.op(V, lambda v, par=par: v.tensor_scalar(
                    out=xnb[par][:], in0=xb[par][:], scalar1=st["rstd"][:],
                    scalar2=st["nmr"][:], op0=ALU.mult, op1=ALU.add), "dve")
                if i + 2 < 9:
                    ctx.wait(Y, "dve", ln_apply)
                    ctx.op(Y, lambda sy, i=i, par=par: sy.dma_start(
                        out=xb[par][:], in_=xin[t0s[i + 2]:t0s[i + 2] + 128, :]),
                        f"dx{par}", 16)
                    xb_ready[i + 2] = (f"dx{par}", ctx.c[f"dx{par}"])
                ctx.wait(T, "dc", consts)
                ctx.wait(T, "dve", ln_apply)
                for f4 in range(8):
                    tp = f4 % 2
                    ctx.wait(T, "dve", psT_free[tp])
                    for ff in range(4):
                        f = f4 * 4 + ff
                        pe_n = ctx.op(T, lambda t, par=par, tp=tp, f=f, ff=ff: t.transpose(
                            psT[tp][:, ff * 128:(ff + 1) * 128],
                            xnb[par][:, f * 128:(f + 1) * 128], id_sb[:]),
                            "pe" if ff == 3 else None)
                    ctx.wait(V, "pe", pe_n)
                    psT_free[tp] = ctx.op(V, lambda v, i=i, tp=tp, f4=f4: v.tensor_copy(
                        xnt[:, f4 * 4:(f4 + 1) * 4, t0s[i]:t0s[i] + 128], psT[tp][:]), "dve")
                xnb_free[par] = ctx.c["pe"]
            p1_pe = ctx.c["pe"]
            p1_dve = ctx.c["dve"]
            p1_act = ctx.c["act"]

        with ExitStack() as p2:
            c2 = sub(ctx, p2)
            csl = [c2.sb(f"csl{i}", [128, 32 * 512], BF16) for i in range(2)]
            psc = [c2.ps(f"psc{i}", [128, 512]) for i in range(8)]

            csl_ready = [None, None]
            csl_free = [0, 0]
            ctx.wait(G, "pe", p1_pe)
            ctx.wait(G, "dve", p1_dve)
            ctx.wait(G, "act", p1_act)
            ctx.wait(G, "dc", consts)
            ctx.op(G, lambda g: g.dma_start(out=csl[0][:], in_=convw[0, 0]), "dw0", 16)
            csl_ready[0] = ("dw0", ctx.c["dw0"])

            sl = 0
            for oc6 in range(6):
                for dt in range(3):
                    par = sl % 2
                    if sl + 1 < 18:
                        npar = (sl + 1) % 2
                        noc, ndt = divmod(sl + 1, 3)
                        ctx.wait(G, "pe", csl_free[npar])
                        ctx.op(G, lambda g, noc=noc, ndt=ndt, npar=npar: g.dma_start(
                            out=csl[npar][:], in_=convw[ndt, noc]), f"dw{npar}", 16)
                        csl_ready[npar] = (f"dw{npar}", ctx.c[f"dw{npar}"])
                    ctx.wait(T, *csl_ready[par])
                    if sl == 0:
                        ctx.wait(T, "dve", p1_dve)
                    for ab in range(NTB):
                        if dt == 0:
                            ctx.wait(T, "dve", qkvc_ready[ab])
                        for k in range(32):
                            is_last_of_slab = (ab == NTB - 1 and k == 31)
                            ctx.op(T, lambda t, ab=ab, k=k, dt=dt, par=par:
                                   t.matmul(psc[ab][:],
                                            xnt[:, k, ab0[ab] + dt - 1: ab0[ab] + dt + 127],
                                            csl[par][:, k * 512:(k + 1) * 512],
                                            start=(dt == 0 and k == 0), stop=False),
                                   "pe" if is_last_of_slab else None)
                    csl_free[par] = ctx.c["pe"]
                    sl += 1
                ctx.wait(T, "dve", ones_ready)
                for ab in range(NTB):
                    mm_n = ctx.op(T, lambda t, ab=ab, oc6=oc6: t.matmul(
                        psc[ab][:], ones_sb[:],
                        convb_sb[:, oc6 * 512:(oc6 + 1) * 512],
                        start=False, stop=True), "pe")
                    ctx.wait(V, "pe", mm_n)
                    qkvc_ready[ab] = ctx.op(V, lambda v, ab=ab, oc6=oc6: v.tensor_copy(
                        qkvc[:, ab, oc6 * 512:(oc6 + 1) * 512], psc[ab][:]), "dve")
            p2_pe = ctx.c["pe"]
            p2_dve = ctx.c["dve"]

    o2t_ready = [0] * NTB
    with ExitStack() as p3:
        c3 = sub(ctx, p3)
        o2t = c3.sb("o2t", [128, 8, TLOC], BF16)
        wq_sb = c3.sb("wq_sb", [128, 8, 1024], BF16)
        wk_sb = c3.sb("wk_sb", [128, 8, 1024], BF16)
        wv_sb = c3.sb("wv_sb", [128, 8, 1024], BF16)
        wo_sb = c3.sb("wo_sb", [128, 8, 1024], BF16)
        qkvT = [c3.sb(f"qkvT{i}", [128, 24, 128], BF16) for i in range(2)]
        qhT = [c3.sb(f"qhT{i}", [128, 8, 128], BF16) for i in range(2)]
        khT = [c3.sb(f"khT{i}", [128, 8, 128], BF16) for i in range(2)]
        vh = [c3.sb(f"vh{i}", [128, 16, 65], BF16) for i in range(2)]
        probs = [c3.sb(f"probs{i}", [128, 128], BF16) for i in range(2)]
        rz = [c3.sb(f"rz{i}", [128, 1], F32) for i in range(2)]
        oh = [c3.sb(f"oh{i}", [128, 16 * 64], BF16) for i in range(2)]
        o2 = [c3.sb(f"o2{i}", [128, 1024], BF16) for i in range(2)]
        psT3 = [c3.ps(f"psT3_{i}", [128, 512], BF16) for i in range(2)]
        psqk = [c3.ps(f"psqk{i}", [128, 128]) for i in range(2)]
        psv = c3.ps("psv", [128, 512])
        pss = [c3.ps(f"pss{i}", [128, 512]) for i in range(2)]
        pso3 = psqk

        for fn in (
            lambda g: g.dma_start(out=wq_sb[:], in_=wq.rearrange("(k p) o -> p k o", p=128)),
            lambda g: g.dma_start(out=wk_sb[:], in_=wk.rearrange("(k p) o -> p k o", p=128)),
            lambda g: g.dma_start(out=wv_sb[:], in_=wv.rearrange("(k p) o -> p k o", p=128)),
            lambda g: g.dma_start(out=wo_sb[:], in_=wo.rearrange("(k p) o -> p k o", p=128)),
        ):
            ctx.wait(G, "pe", p2_pe)
            ctx.wait(G, "dve", p2_dve)
            ctx.wait(G, "dc")
            ctx.op(G, fn, "dc", 16)
        w3_ready = ctx.c["dc"]
        for i in range(2):
            ctx.op(V, lambda v, i=i: v.memset(vh[i][:, :, 64:65], 1.0), "dve")
        vh_ones_dve = ctx.c["dve"]

        qkvT_free = [0, 0]
        psT3_free = [0, 0]
        qhT_free = [0, 0]
        khT_free = [0, 0]
        vh_free = [0, 0]
        oh_free = [0, 0]
        o2_free = [0, 0]
        psqk_free = [0, 0]
        psv_free = 0
        pss_free = [0, 0]
        pso3_free = [0, 0]
        pswo_free = 0
        probs_free = [0, 0]
        rz_free = [0, 0]

        ctx.wait(T, "dve", p2_dve)
        for ab in range(NTB):
            par = ab % 2
            ctx.wait(T, "dve", max(qkvc_ready[ab], qkvT_free[par]))
            for f4 in range(6):
                tp = f4 % 2
                ctx.wait(T, "dve", psT3_free[tp])
                for ff in range(4):
                    f = f4 * 4 + ff
                    pe_n = ctx.op(T, lambda t, ab=ab, f=f, ff=ff, tp=tp: t.transpose(
                        psT3[tp][:, ff * 128:(ff + 1) * 128],
                        qkvc[:, ab, f * 128:(f + 1) * 128], id_sb[:]),
                        "pe" if ff == 3 else None)
                ctx.wait(V, "pe", pe_n)
                psT3_free[tp] = ctx.op(V, lambda v, par=par, f4=f4, tp=tp: v.tensor_copy(
                    qkvT[par][:, f4 * 4:(f4 + 1) * 4, :], psT3[tp][:]), "dve")
            qkvT_ready = ctx.c["dve"]

            ctx.wait(T, "dc", w3_ready)
            for which, wsb, dsttile, dstfree, bias, sc in (
                    ("q", wq_sb, qhT, qhT_free, bq_sb, 0.125),
                    ("k", wk_sb, khT, khT_free, bk_sb, 1.0)):
                base = 0 if which == "q" else 8
                for hc in range(8):
                    pq = hc % 2
                    ctx.wait(T, "dve", qkvT_ready)
                    ctx.wait(T, "dve", psqk_free[pq])
                    ctx.wait(T, "act", pso3_free[pq])
                    for k in range(8):
                        ctx.op(T, lambda t, k=k, hc=hc, par=par, pq=pq,
                               wsb=wsb, base=base: t.matmul(
                            psqk[pq][:], wsb[:, k, hc * 128:(hc + 1) * 128],
                            qkvT[par][:, base + k, :],
                            start=(k == 0), stop=(k == 7)), "pe" if k == 7 else None)
                    ctx.wait(V, "pe", ctx.c["pe"])
                    if which == "q":
                        ctx.wait(V, "dve", bq_ready)
                    if ab >= 2:
                        ctx.wait(V, "pe", dstfree[par])
                    psqk_free[pq] = ctx.op(V, lambda v, hc=hc, par=par, pq=pq,
                                           dsttile=dsttile, bias=bias, sc=sc:
                                           v.tensor_scalar(
                        out=dsttile[par][:, hc, :], in0=psqk[pq][:],
                        scalar1=sc, scalar2=bias[:, hc:hc + 1],
                        op0=ALU.mult, op1=ALU.add), "dve")
            qk_ready = ctx.c["dve"]

            for vc2 in range(2):
                ctx.wait(T, "dve", max(qkvT_ready, ones_ready))
                ctx.wait(T, "act", max(psv_free, pswo_free))
                for k in range(8):
                    ctx.op(T, lambda t, k=k, vc2=vc2, par=par: t.matmul(
                        psv[:], qkvT[par][:, 16 + k, :],
                        wv_sb[:, k, vc2 * 512:(vc2 + 1) * 512],
                        start=(k == 0), stop=False))
                mm = ctx.op(T, lambda t, vc2=vc2: t.matmul(
                    psv[:], ones_sb[:], bv_sb[:, vc2 * 512:(vc2 + 1) * 512],
                    start=False, stop=True), "pe")
                ctx.wait(S, "pe", mm)
                ctx.wait(S, "dve", vh_ones_dve)
                if ab >= 2:
                    ctx.wait(S, "pe", vh_free[par])
                psv_free = ctx.op(S, lambda s, vc2=vc2, par=par: s.activation(
                    out=vh[par][:, vc2 * 8:(vc2 + 1) * 8, 0:64],
                    in_=psv[:], func=AF.Copy), "act")
            v_ready = ctx.c["act"]

            for h in range(16):
                hp = h % 2
                p0 = 64 * (h % 2)
                kt = h // 2
                ctx.wait(T, "dve", qk_ready)
                ctx.wait(T, "act", pss_free[hp])
                s_pe = ctx.op(T, lambda t, par=par, p0=p0, kt=kt, hp=hp: t.matmul(
                    pss[hp][:, 0:128], khT[par][p0:p0 + 64, kt, :], qhT[par][p0:p0 + 64, kt, :],
                    start=True, stop=True), "pe")
                ctx.wait(S, "pe", max(s_pe, probs_free[hp]))
                e_n = ctx.op(S, lambda s, hp=hp: s.activation(
                    out=probs[hp][:], in_=pss[hp][:, 0:128], func=AF.Exp), "act")
                pss_free[hp] = e_n
                ctx.wait(V, "act", e_n)
                m_n = ctx.op(V, lambda v, hp=hp: v.tensor_tensor(
                    out=probs[hp][:], in0=probs[hp][:], in1=tri_sb[:], op=ALU.mult), "dve")
                ctx.wait(T, "dve", m_n)
                ctx.wait(T, "act", max(v_ready, pso3_free[hp]))
                pv_pe = ctx.op(T, lambda t, h=h, par=par, hp=hp: t.matmul(
                    pso3[hp][:, 0:65], probs[hp][:], vh[par][:, h, :],
                    start=True, stop=True), "pe")
                probs_free[hp] = pv_pe
                ctx.wait(V, "pe", pv_pe)
                ctx.wait(V, "act", rz_free[hp])
                r_n = ctx.op(V, lambda v, hp=hp: v.reciprocal(rz[hp][:], pso3[hp][:, 64:65]), "dve")
                ctx.wait(S, "dve", r_n)
                if ab >= 2:
                    ctx.wait(S, "pe", oh_free[par])
                pso3_free[hp] = ctx.op(S, lambda s, h=h, par=par, hp=hp: s.activation(
                    out=oh[par][:, h * 64:(h + 1) * 64], in_=pso3[hp][:, 0:64],
                    func=AF.Copy, scale=rz[hp][:]), "act")
                rz_free[hp] = pso3_free[hp]
            oh_ready = ctx.c["act"]

            ctx.wait(T, "act", oh_ready)
            for f4 in range(2):
                tp = f4 % 2
                ctx.wait(T, "dve", psT3_free[tp])
                for ff in range(4):
                    f = f4 * 4 + ff
                    pe_n = ctx.op(T, lambda t, par=par, f=f, ff=ff, tp=tp: t.transpose(
                        psT3[tp][:, ff * 128:(ff + 1) * 128],
                        oh[par][:, f * 128:(f + 1) * 128], id_sb[:]),
                        "pe" if ff == 3 else None)
                ctx.wait(V, "pe", pe_n)
                psT3_free[tp] = ctx.op(V, lambda v, par=par, f4=f4, tp=tp: v.tensor_copy(
                    qkvT[par][:, f4 * 4:(f4 + 1) * 4, :], psT3[tp][:]), "dve")
            ohT_ready = ctx.c["dve"]
            oh_free[par] = ctx.c["pe"]

            for wc in range(2):
                ctx.wait(T, "dve", max(ohT_ready, ones_ready))
                ctx.wait(T, "act", max(pswo_free, v_ready))
                for k in range(8):
                    ctx.op(T, lambda t, k=k, wc=wc, par=par: t.matmul(
                        psv[:], qkvT[par][:, k, :],
                        wo_sb[:, k, wc * 512:(wc + 1) * 512],
                        start=(k == 0), stop=False))
                mm = ctx.op(T, lambda t, wc=wc: t.matmul(
                    psv[:], ones_sb[:], bo_sb[:, wc * 512:(wc + 1) * 512],
                    start=False, stop=True), "pe")
                ctx.wait(S, "pe", mm)
                if ab >= 2:
                    ctx.wait(S, "pe", o2_free[par])
                pswo_free = ctx.op(S, lambda s, wc=wc, par=par: s.activation(
                    out=o2[par][:, wc * 512:(wc + 1) * 512], in_=psv[:],
                    func=AF.Copy), "act")
            o2_ready = ctx.c["act"]

            ctx.wait(T, "act", o2_ready)
            for f4 in range(2):
                tp = f4 % 2
                ctx.wait(T, "dve", psT3_free[tp])
                for ff in range(4):
                    f = f4 * 4 + ff
                    pe_n = ctx.op(T, lambda t, par=par, f=f, ff=ff, tp=tp: t.transpose(
                        psT3[tp][:, ff * 128:(ff + 1) * 128],
                        o2[par][:, f * 128:(f + 1) * 128], id_sb[:]),
                        "pe" if ff == 3 else None)
                ctx.wait(V, "pe", pe_n)
                psT3_free[tp] = ctx.op(V, lambda v, ab=ab, f4=f4, tp=tp: v.tensor_copy(
                    o2t[:, f4 * 4:(f4 + 1) * 4, ab * 128:(ab + 1) * 128], psT3[tp][:]), "dve")
            o2t_ready[ab] = ctx.c["dve"]
            o2_free[par] = ctx.c["pe"]
            qkvT_free[par] = ctx.c["pe"]
            qhT_free[par] = ctx.c["pe"]
            khT_free[par] = ctx.c["pe"]
            vh_free[par] = ctx.c["pe"]
        p3_pe = ctx.c["pe"]
        p3_dve = ctx.c["dve"]
        p3_act = ctx.c["act"]

        with ExitStack() as p4:
            c4 = sub(ctx, p4)
            pjs = [c4.sb(f"pjs{i}", [128, 8 * 512], BF16) for i in range(2)]
            xsl = [c4.sb(f"p_xsl{i}", [128, 512], F32) for i in range(4)]
            xo_sb = [c4.sb(f"p_xo{i}", [128, 512], F32) for i in range(2)]
            psp = [psv, pss[0]]

            pjs_ready = [None, None]
            pjs_free = [0, 0]
            ctx.wait(T, "act", p3_act)
            ctx.wait(T, "dve", p3_dve)
            ctx.wait(G, "pe", p3_pe)
            ctx.wait(G, "dve", p3_dve)
            ctx.wait(G, "act", p3_act)
            ctx.wait(G, "dw0")
            ctx.wait(G, "dw1")
            ctx.op(G, lambda g: g.dma_start(out=pjs[0][:], in_=projw[0]), "dw0", 16)
            pjs_ready[0] = ("dw0", ctx.c["dw0"])

            psp_free = [0, 0]
            xsl_free = [0] * 4
            xo_free = [0, 0]
            it = 0
            for oc in range(8):
                wp = oc % 2
                if oc + 1 < 8:
                    npar = (oc + 1) % 2
                    ctx.wait(G, "pe", pjs_free[npar])
                    ctx.op(G, lambda g, oc=oc, npar=npar: g.dma_start(
                        out=pjs[npar][:], in_=projw[oc + 1]), f"dw{npar}", 16)
                    pjs_ready[npar] = (f"dw{npar}", ctx.c[f"dw{npar}"])
                for tb in range(NTB):
                    par = it % 2
                    xp = it % 4
                    ctx.wait(Y, "dve", xsl_free[xp])
                    r0 = xin_row(tb * 128)
                    ctx.op(Y, lambda sy, oc=oc, xp=xp, r0=r0: sy.dma_start(
                        out=xsl[xp][:], in_=xin[r0:r0 + 128, oc * 512:(oc + 1) * 512]),
                        f"dxs{xp}", 16)
                    xsl_n = (f"dxs{xp}", ctx.c[f"dxs{xp}"])

                    ctx.wait(T, *pjs_ready[wp])
                    ctx.wait(T, "dve", max(o2t_ready[tb], psp_free[par], ones_ready))
                    for k in range(8):
                        ctx.op(T, lambda t, tb=tb, k=k, par=par, wp=wp: t.matmul(
                            psp[par][:], o2t[:, k, tb * 128:(tb + 1) * 128],
                            pjs[wp][:, k * 512:(k + 1) * 512],
                            start=(k == 0), stop=False))
                    mm_n = ctx.op(T, lambda t, oc=oc, par=par: t.matmul(
                        psp[par][:], ones_sb[:], projb_sb[:, oc * 512:(oc + 1) * 512],
                        start=False, stop=True), "pe")
                    if tb == NTB - 1:
                        pjs_free[wp] = mm_n

                    ctx.wait(V, "pe", mm_n)
                    ctx.wait(V, *xsl_n)
                    ctx.wait(V, f"dst{par}", xo_free[par])
                    ep_n = ctx.op(V, lambda v, par=par, xp=xp: v.tensor_tensor(
                        out=xo_sb[par][:], in0=psp[par][:], in1=xsl[xp][:],
                        op=ALU.add), "dve")
                    psp_free[par] = ep_n
                    xsl_free[xp] = ep_n
                    ctx.wait(Y, "dve", ep_n)
                    xo_free[par] = ctx.op(Y, lambda sy, tb=tb, oc=oc, par=par: sy.dma_start(
                        out=xout[tb * 128:(tb + 1) * 128, oc * 512:(oc + 1) * 512],
                        in_=xo_sb[par][:]), f"dst{par}", 16)
                    it += 1


def stage_l0attn_weights(inp):
    cw = np.asarray(inp["L0_attn_conv_w"], np.float32)
    convw = np.stack([pack_slabs(np.ascontiguousarray(cw[dt]), 512) for dt in range(3)])
    wq = np.ascontiguousarray(np.asarray(inp["L0_attn_wq"], np.float32).reshape(1024, 1024))
    wk = np.ascontiguousarray(np.asarray(inp["L0_attn_wk"], np.float32).reshape(1024, 1024))
    wv = np.ascontiguousarray(np.asarray(inp["L0_attn_wv"], np.float32).reshape(1024, 1024))
    wo = np.ascontiguousarray(np.asarray(inp["L0_attn_wo"], np.float32).reshape(1024, 1024))
    bq = np.asarray(inp["L0_attn_bq"], np.float32).reshape(1024)
    bk = np.asarray(inp["L0_attn_bk"], np.float32).reshape(1024)
    bv = np.asarray(inp["L0_attn_bv"], np.float32).reshape(1024)
    bo = np.asarray(inp["L0_attn_bo"], np.float32)
    pw = np.asarray(inp["L0_attn_proj_w"], np.float32)
    pb = np.asarray(inp["L0_attn_proj_b"], np.float32)
    cb = np.asarray(inp["L0_attn_conv_b"], np.float32)
    k_idx, q_idx = np.meshgrid(np.arange(128), np.arange(128), indexing="ij")
    tri_kq = (k_idx <= q_idx).astype(np.float32)
    return {
        "convw_s": convw,
        "convb": cb.reshape(1, 3072),
        "wq_r": wq, "wk_r": wk, "wv_r": wv, "wo_r": wo,
        "bq_c": np.ascontiguousarray(bq.reshape(8, 128).T),
        "bk_c": np.ascontiguousarray(bk.reshape(8, 128).T),
        "bv_r": bv.reshape(1, 1024),
        "bo_r": bo.reshape(1, 1024),
        "projw_s": pack_slabs(pw, 512),
        "projb": pb.reshape(1, D),
        "ident": np.eye(128, dtype=np.float32),
        "trimask": tri_kq,
    }


def slice_haloed(x_full, core):
    b, (a0, a1), (b0, b1) = chunk_ranges(core)
    out = np.zeros((TIN, D), x_full.dtype)
    out[1:513] = x_full[b, a0:a1]
    out[515:1027] = x_full[b, b0:b1]
    if a0 > 0:
        out[0] = x_full[b, a0 - 1]
    if a1 < 4096:
        out[513] = x_full[b, a1]
    if b0 > 0:
        out[514] = x_full[b, b0 - 1]
    if b1 < 4096:
        out[1027] = x_full[b, b1]
    return out


def build_l1ffn_program():
    nc = bass.Bass("TRN2", target_bir_lowering=False)
    xin = nc.dram_tensor("xin", [TLOC, D], F32, kind="ExternalInput").ap()
    a2 = nc.dram_tensor("a2", [128, 64], F32, kind="ExternalInput").ap()
    b2 = nc.dram_tensor("b2", [128, 64], F32, kind="ExternalInput").ap()
    betar = nc.dram_tensor("betar", [64, 64], F32, kind="ExternalInput").ap()
    gcols = nc.dram_tensor("gcols", [128, 32], F32, kind="ExternalInput").ap()
    biasr = nc.dram_tensor("biasr", [1, D], F32, kind="ExternalInput").ap()
    ident = nc.dram_tensor("ident", [128, 128], F32, kind="ExternalInput").ap()
    xout = nc.dram_tensor("xout", [TLOC, D], F32, kind="ExternalOutput").ap()
    cdram = nc.dram_tensor("c_scratch", [64, 64], F32).ap()

    with ExitStack() as es:
        ctx = Ctx(nc, es)
        emit_l1ffn(ctx, xin, xout, a2, b2, betar, gcols, biasr, ident, cdram)
        ctx.wait("sync", "dst0")
        ctx.wait("sync", "dst1")
        ctx.flush()
    return nc


def emit_l1ffn(ctx, xin, xout, a2, b2, betar, gcols, biasr, ident, cdram, pre="f_"):
    ctx.sems("dc", "dcy", "dcy2", "dx0", "dx1", "dw0", "pe", "act", "dve", "dst0", "dst1")
    T, V, S, G, Y = "tensor", "vector", "scalar", "gpsimd", "sync"

    id_sb = ctx.sb(pre + "id", [128, 128], BF16)
    a2_sb = ctx.sb(pre + "a2", [128, 64], BF16)
    b2_sb = ctx.sb(pre + "b2", [128, 64], BF16)
    betar_sb = ctx.sb(pre + "betar", [64, 64], BF16)
    gc_sb = ctx.sb(pre + "gc", [128, 32], F32)
    bias_bc = ctx.sb(pre + "biasbc", [128, D], F32)
    cb2 = ctx.sb(pre + "cb2", [128, D], BF16)
    m1_sb = ctx.sb(pre + "m1", [64, 64], BF16)
    m1t_sb = ctx.sb(pre + "m1t", [64, 64], BF16)
    c_sb = ctx.sb(pre + "c", [64, 64], F32)

    xb = [ctx.sb(pre + f"xb{i}", [128, D], F32) for i in range(2)]
    xn2 = [ctx.sb(pre + f"xn2{i}", [128, D], BF16) for i in range(2)]
    hT = [ctx.sb(pre + f"hT{i}", [128, 32, 128], BF16) for i in range(2)]
    u_sb = [ctx.sb(pre + f"u{i}", [128, D], BF16) for i in range(2)]
    uT = [ctx.sb(pre + f"uT{i}", [128, 32, 128], BF16) for i in range(2)]
    y_sb = [ctx.sb(pre + f"y{i}", [128, D], F32) for i in range(2)]
    g_sb = ctx.sb(pre + "g", [128, D], F32)
    e1_sb = ctx.sb(pre + "e1", [128, D], F32)
    xo_sb = y_sb
    st = alloc_stats(ctx, pre + "st", sq=xn2[0])
    r1c = ctx.sb(pre + "r1", [128, 1], F32)
    v2e = ctx.sb(pre + "v2e", [128, 1], F32)
    r2c = ctx.sb(pre + "r2", [128, 1], F32)
    rstd2 = ctx.sb(pre + "rstd2", [128, 1], F32)
    s2col = ctx.sb(pre + "s2col", [128, 1], F32)
    Rc = ctx.sb(pre + "R", [128, 1], F32)
    nmR = ctx.sb(pre + "nmR", [128, 1], F32)

    psT = [ctx.ps(pre + f"psT{i}", [128, 512], BF16) for i in range(2)]
    psu = [ctx.ps(pre + f"psu{i}", [128, 64]) for i in range(2)]
    psy = [ctx.ps(pre + f"psy{i}", [128, 64]) for i in range(2)]
    psc1 = ctx.ps(pre + "psc1", [64, 64])
    psc2 = ctx.ps(pre + "psc2", [64, 64], BF16)

    for fn in (
        lambda g: g.dma_start(out=id_sb[:], in_=ident[:]),
        lambda g: g.dma_start(out=a2_sb[:], in_=a2[:]),
        lambda g: g.dma_start(out=b2_sb[:], in_=b2[:]),
        lambda g: g.dma_start(out=betar_sb[:], in_=betar[:]),
        lambda g: g.dma_start(out=gc_sb[:], in_=gcols[:]),
    ):
        ctx.wait(G, "dc")
        ctx.op(G, fn, "dc", 16)
    consts = ctx.c["dc"]
    ctx.op(Y, lambda sy: sy.dma_start(out=bias_bc[:], in_=biasr.broadcast_to([128, D])), "dcy", 16)
    bias_ready = ("dcy", ctx.c["dcy"])

    ctx.wait(T, "dc", consts)
    n = ctx.op(T, lambda t: t.matmul(psc1[:], b2_sb[0:64, :], betar_sb[:],
                                     start=True, stop=True), "pe")
    ctx.wait(V, "pe", n)
    n = ctx.op(V, lambda v: v.tensor_copy(m1_sb[:], psc1[:]), "dve")
    ctx.wait(T, "dve", n)
    n = ctx.op(T, lambda t: t.transpose(psc2[:], m1_sb[:], id_sb[0:64, 0:64]), "pe")
    ctx.wait(V, "pe", n)
    n = ctx.op(V, lambda v: v.tensor_copy(m1t_sb[:], psc2[:]), "dve")
    ctx.wait(T, "dve", n)
    n = ctx.op(T, lambda t: t.matmul(psc1[:], m1t_sb[:], a2_sb[0:64, :],
                                     start=True, stop=True), "pe")
    ctx.wait(V, "pe", n)
    n = ctx.op(V, lambda v: v.tensor_copy(c_sb[:], psc1[:]), "dve")
    ctx.wait(Y, "dve", n)
    ctx.op(Y, lambda sy: sy.dma_start(out=cdram[:], in_=c_sb[:]), "dcy2", 16)
    ctx.wait(G, "dcy2")
    ctx.wait(G, "dc", consts)
    ctx.op(G, lambda g: g.dma_start(
        out=cb2[:], in_=cdram.rearrange("q b -> (q b)").unsqueeze(0).broadcast_to([128, D])), "dc", 16)
    cb2_ready = ("dc", ctx.c["dc"])

    xb_ready = [None] * NTB
    for i in range(2):
        ctx.op(Y, lambda sy, i=i: sy.dma_start(
            out=xb[i][:], in_=xin[i * 128:(i + 1) * 128, :]), f"dx{i % 2}", 16)
        xb_ready[i] = (f"dx{i % 2}", ctx.c[f"dx{i % 2}"])

    xn2_free = [0, 0]
    hT_free = [0, 0]
    uT_free = [0, 0]
    u_free = [0, 0]
    psT_free = [0, 0]
    psT_free_sem = [None, None]
    psu_free = [0, 0]
    psy_free = [0, 0]
    y_free = [0, 0]
    xo_free = [0, 0]
    xb_free = [0, 0]
    xn2_freed_dve = [0, 0]

    for tb in range(NTB):
        par = tb % 2
        if tb >= 2:
            ctx.wait(S, "pe", xn2_free[par])
            ctx.wait(S, "dve", xn2_freed_dve[par])
        emit_ln_stats(ctx, xb[par], st, sq=xn2[par], ready=xb_ready[tb])
        ctx.wait(V, "dve")
        ctx.op(V, lambda v: v.tensor_scalar(out=v2e[:], in0=st["rvar"][:],
                                            scalar1=-LN_EPS, scalar2=1.0 + LN_EPS,
                                            op0=ALU.mult, op1=ALU.add), "dve")
        ctx.wait(V, "dve")
        ctx.op(V, lambda v: v.reciprocal(r2c[:], v2e[:]), "dve")
        ctx.wait(S, "dve")
        ctx.op(S, lambda s: s.activation(out=rstd2[:], in_=r2c[:], func=AF.Sqrt), "act")
        ctx.op(S, lambda s: s.activation(out=s2col[:], in_=v2e[:], func=AF.Sqrt), "act")
        ctx.wait(V, "act")
        ctx.op(V, lambda v: v.tensor_tensor(out=Rc[:], in0=st["rstd"][:],
                                            in1=rstd2[:], op=ALU.mult), "dve")
        ctx.wait(V, "dve")
        ctx.op(V, lambda v: v.scalar_tensor_tensor(
            out=nmR[:], in0=st["mu"][:], scalar=-1.0, in1=Rc[:],
            op0=ALU.mult, op1=ALU.mult), "dve")
        ctx.wait(V, "dve")
        if tb >= 2:
            ctx.wait(V, "pe", xn2_free[par])
        xn2_n = ctx.op(V, lambda v, par=par: v.tensor_scalar(
            out=xn2[par][:], in0=xb[par][:], scalar1=Rc[:], scalar2=nmR[:],
            op0=ALU.mult, op1=ALU.add), "dve")

        ctx.wait(T, "dc", consts)
        ctx.wait(T, "dve", xn2_n)
        for f4 in range(8):
            tp = f4 % 2
            if psT_free_sem[tp]:
                ctx.wait(T, psT_free_sem[tp], psT_free[tp])
            for ff in range(4):
                f = f4 * 4 + ff
                pe_n = ctx.op(T, lambda t, par=par, tp=tp, f=f, ff=ff: t.transpose(
                    psT[tp][:, ff * 128:(ff + 1) * 128],
                    xn2[par][:, f * 128:(f + 1) * 128], id_sb[:]),
                    "pe" if ff == 3 else None)
            ctx.wait(V, "pe", pe_n)
            if tb >= 2:
                ctx.wait(V, "pe", hT_free[par])
            for ff in range(4):
                f = f4 * 4 + ff
                n = ctx.op(V, lambda v, par=par, tp=tp, f=f, ff=ff: v.tensor_scalar(
                    out=hT[par][:, f, :], in0=psT[tp][:, ff * 128:(ff + 1) * 128],
                    scalar1=gc_sb[:, f:f + 1], scalar2=None, op0=ALU.mult), "dve")
            psT_free[tp] = n
            psT_free_sem[tp] = "dve"
        xn2_free[par] = ctx.c["pe"]
        hT_ready = ctx.c["dve"]

        ctx.wait(T, "dve", hT_ready)
        for p in range(64):
            pp = p % 2
            base = 64 * (p % 2)
            f = p // 2
            ctx.wait(T, "dve", psu_free[pp])
            mm = ctx.op(T, lambda t, par=par, f=f, base=base, pp=pp: t.matmul(
                psu[pp][:], hT[par][base:base + 64, f, :], a2_sb[base:base + 64, :],
                start=True, stop=True), "pe")
            ctx.wait(V, "pe", mm)
            psu_free[pp] = ctx.op(V, lambda v, par=par, p=p, pp=pp: v.tensor_copy(
                u_sb[par][:, p::64], psu[pp][:]), "dve")
        u_ready = ctx.c["dve"]

        ctx.wait(T, "dve", u_ready)
        for f4 in range(8):
            tp = f4 % 2
            ctx.wait(T, psT_free_sem[tp], psT_free[tp])
            for ff in range(4):
                f = f4 * 4 + ff
                pe_n = ctx.op(T, lambda t, par=par, tp=tp, f=f, ff=ff: t.transpose(
                    psT[tp][:, ff * 128:(ff + 1) * 128],
                    u_sb[par][:, f * 128:(f + 1) * 128], id_sb[:]),
                    "pe" if ff == 3 else None)
            ctx.wait(V, "pe", pe_n)
            if tb >= 2:
                ctx.wait(V, "pe", uT_free[par])
            psT_free[tp] = ctx.op(V, lambda v, par=par, tp=tp, f4=f4: v.tensor_copy(
                uT[par][:, f4 * 4:(f4 + 1) * 4, :], psT[tp][:]), "dve")
            psT_free_sem[tp] = "dve"
        uT_ready = ctx.c["dve"]
        u_freed_dve_cur = ctx.c["pe"]

        ctx.wait(T, "dve", uT_ready)
        ctx.wait(V, *cb2_ready)
        for b in range(64):
            pp = b % 2
            base = 64 * (b % 2)
            f = b // 2
            ctx.wait(T, "dve", psy_free[pp])
            mm = ctx.op(T, lambda t, par=par, f=f, base=base, pp=pp: t.matmul(
                psy[pp][:], uT[par][base:base + 64, f, :], b2_sb[base:base + 64, :],
                start=True, stop=True), "pe")
            ctx.wait(V, "pe", mm)
            if b < 2 and tb >= 2:
                ctx.wait(V, "act", y_free[par])
                ctx.wait(V, f"dst{par}", xo_free[par])
            psy_free[pp] = ctx.op(V, lambda v, par=par, b=b, pp=pp: v.scalar_tensor_tensor(
                out=y_sb[par][:, b::64], in0=psy[pp][:], scalar=1.0,
                in1=cb2[:, b::64], op0=ALU.mult, op1=ALU.add), "dve")
        y_ready = ctx.c["dve"]

        ctx.wait(S, "dve", y_ready)
        gelu_n = ctx.op(S, lambda s, par=par: s.activation(
            out=g_sb[:], in_=y_sb[par][:], func=AF.Gelu), "act")
        y_free[par] = gelu_n
        ctx.wait(V, "act", gelu_n)
        ctx.wait(V, *bias_ready)
        e1_n = ctx.op(V, lambda v: v.tensor_tensor(
            out=e1_sb[:], in0=g_sb[:], in1=bias_bc[:], op=ALU.add), "dve")
        ctx.wait(V, "dve", e1_n)
        e2_n = ctx.op(V, lambda v, par=par: v.scalar_tensor_tensor(
            out=e1_sb[:], in0=xn2[par][:], scalar=s2col[:], in1=e1_sb[:],
            op0=ALU.mult, op1=ALU.add), "dve")
        ctx.wait(V, "dve", e2_n)
        e3_n = ctx.op(V, lambda v, par=par: v.scalar_tensor_tensor(
            out=xo_sb[par][:], in0=e1_sb[:], scalar=0.5, in1=xb[par][:],
            op0=ALU.mult, op1=ALU.add), "dve")
        xb_free[par] = e3_n
        ctx.wait(Y, "dve", e3_n)
        xo_free[par] = ctx.op(Y, lambda sy, tb=tb, par=par: sy.dma_start(
            out=xout[tb * 128:(tb + 1) * 128, :], in_=xo_sb[par][:]), f"dst{par}", 16)

        if tb + 2 < NTB:
            ctx.wait(Y, "dve", e3_n)
            ctx.op(Y, lambda sy, tb=tb, par=par: sy.dma_start(
                out=xb[par][:], in_=xin[(tb + 2) * 128:(tb + 3) * 128, :]), f"dx{par}", 16)
            xb_ready[tb + 2] = (f"dx{par}", ctx.c[f"dx{par}"])

        hT_free[par] = ctx.c["pe"]
        uT_free[par] = ctx.c["pe"]
        xn2_freed_dve[par] = e2_n


def stage_l1ffn_weights(inp):
    A = np.asarray(inp["L1_ffn_A"], np.float32)
    B = np.asarray(inp["L1_ffn_B"], np.float32)
    beta = np.asarray(inp["L1_ffn_ln_beta"], np.float32)
    gamma = np.asarray(inp["L1_ffn_ln_gamma"], np.float32)
    bias = np.asarray(inp["L1_ffn_bias"], np.float32)
    return {
        "a2": np.ascontiguousarray(np.concatenate([A, A], axis=0)),
        "b2": np.ascontiguousarray(np.concatenate([B, B], axis=0)),
        "betar": np.ascontiguousarray(beta.reshape(64, 64)),
        "gcols": np.ascontiguousarray(gamma.reshape(32, 128).T),
        "biasr": bias.reshape(1, D),
        "ident": np.eye(128, dtype=np.float32),
    }


def emit_l1attn_a(ctx, xin, wqs, wks, wvs, cs, ident, qt, kt, vhm,
                  kt2d=None, vhm2d=None, agk=None, agv=None):
    ctx.sems("dc", "dx0", "dx1", "dw0", "dw1", "pe", "act", "dve", "dst0", "dst1")
    T, V, S, G, Y = "tensor", "vector", "scalar", "gpsimd", "sync"

    id_sb = ctx.sb("q_id", [128, 128], BF16)
    cs_sb = ctx.sb("q_cs", [128, NTB, 6], F32)
    xnt = ctx.sb("q_xnt", [128, 32, TLOC], BF16)
    wsl = [ctx.sb(f"q_wsl{i}", [128, 32 * 512], BF16) for i in range(2)]
    ro_sb = [ctx.sb(f"q_ro{i}", [128, 512], BF16) for i in range(2)]
    vo_sb = [ctx.sb(f"q_vo{i}", [128, 4, 129], BF16) for i in range(2)]
    tt_sb = [ctx.sb(f"q_tt{i}", [128, 4, 128], BF16) for i in range(2)]
    for i in range(2):
        ctx.op("vector", lambda v, i=i: v.memset(vo_sb[i][:, :, 128:129], 1.0), "dve")
    vo_ones = ctx.c["dve"]

    psT = [ctx.ps(f"q_psT{i}", [128, 512], BF16) for i in range(2)]
    pso = [ctx.ps(f"q_pso{i}", [128, 512]) for i in range(2)]

    ctx.wait(G, "dc")
    ctx.op(G, lambda g: g.dma_start(out=id_sb[:], in_=ident[:]), "dc", 16)
    ctx.wait(G, "dc")
    ctx.op(G, lambda g: g.dma_start(
        out=cs_sb[:], in_=cs.rearrange("(tb p) c -> p tb c", p=128)), "dc", 16)
    consts = ctx.c["dc"]

    with ExitStack() as p1:
        c1 = sub(ctx, p1)
        xb = [c1.sb(f"q_xb{i}", [128, D], F32) for i in range(2)]
        xnb = [c1.sb(f"q_xnb{i}", [128, D], BF16) for i in range(2)]
        st = alloc_stats(c1, "q_st", sq=xnb[0])

        xb_ready = [None] * NTB
        for i in range(2):
            ctx.op(Y, lambda sy, i=i: sy.dma_start(
                out=xb[i][:], in_=xin[i * 128:(i + 1) * 128, :]), f"dx{i % 2}", 16)
            xb_ready[i] = (f"dx{i % 2}", ctx.c[f"dx{i % 2}"])
        xnb_free = [0, 0]
        psT_free = [0, 0]
        for i in range(NTB):
            par = i % 2
            if i >= 2:
                ctx.wait(S, "pe", xnb_free[par])
            emit_ln_stats(ctx, xb[par], st, sq=xnb[par], ready=xb_ready[i])
            ctx.wait(V, "dve")
            if i >= 2:
                ctx.wait(V, "pe", xnb_free[par])
            ln_apply = ctx.op(V, lambda v, par=par: v.tensor_scalar(
                out=xnb[par][:], in0=xb[par][:], scalar1=st["rstd"][:],
                scalar2=st["nmr"][:], op0=ALU.mult, op1=ALU.add), "dve")
            if i + 2 < NTB:
                ctx.wait(Y, "dve", ln_apply)
                ctx.op(Y, lambda sy, i=i, par=par: sy.dma_start(
                    out=xb[par][:], in_=xin[(i + 2) * 128:(i + 3) * 128, :]), f"dx{par}", 16)
                xb_ready[i + 2] = (f"dx{par}", ctx.c[f"dx{par}"])
            ctx.wait(T, "dc", consts)
            ctx.wait(T, "dve", ln_apply)
            for f4 in range(8):
                tp = f4 % 2
                ctx.wait(T, "dve", psT_free[tp])
                for ff in range(4):
                    f = f4 * 4 + ff
                    pe_n = ctx.op(T, lambda t, par=par, tp=tp, f=f, ff=ff: t.transpose(
                        psT[tp][:, ff * 128:(ff + 1) * 128],
                        xnb[par][:, f * 128:(f + 1) * 128], id_sb[:]),
                        "pe" if ff == 3 else None)
                ctx.wait(V, "pe", pe_n)
                psT_free[tp] = ctx.op(V, lambda v, i=i, tp=tp, f4=f4: v.tensor_copy(
                    xnt[:, f4 * 4:(f4 + 1) * 4, i * 128:(i + 1) * 128], psT[tp][:]), "dve")
            xnb_free[par] = ctx.c["pe"]
        p1_pe = ctx.c["pe"]
        p1_dve = ctx.c["dve"]
        p1_act = ctx.c["act"]

    wsl_ready = [None, None]
    wsl_free = [0, 0]
    pso_free = [0, 0]
    ro_free = [0, 0]
    ro_free_sem = [None, None]
    vo_free = [0, 0]
    vo_free_sem = [None, None]
    tt_free = [0, 0]
    psT2_free = [0, 0]

    ctx.wait(G, "pe", p1_pe)
    ctx.wait(G, "dve", p1_dve)
    ctx.wait(G, "act", p1_act)

    rg = [[0, 1, 2, 3], [4, 5, 6, 7]]
    if agk is not None:
        ctx.sems("cc", "st5a")

    def store_marker():
        ctx.wait(Y, "dst0")
        ctx.wait(Y, "dst1")
        return ctx.bump(Y, "st5a", 1)

    def gather_k(g, mark):
        ctx.wait(G, "st5a", mark)
        ctx.op(G, lambda gp_: gp_.collective_compute(
            "AllGather", ALU.bypass, replica_groups=rg,
            ins=[kt2d[g * 128:(g + 1) * 128, :]],
            outs=[agk[g * 512:(g + 1) * 512, :]]), "cc", 1)

    def gather_v(g, mark):
        ctx.wait(G, "st5a", mark)
        ctx.op(G, lambda gp_: gp_.collective_compute(
            "AllGather", ALU.bypass, replica_groups=rg,
            ins=[vhm2d[g * 64:(g + 1) * 64, :]],
            outs=[agv[g * 256:(g + 1) * 256, :]]), "cc", 1)

    sl = 0
    for wi, (wsrc, dst) in enumerate(((wqs, qt), (wks, kt), (wvs, None))):
        is_v = wi == 2
        csel = (0, 3, None)[wi]
        for oc in range(8):
            if agk is not None and wi == 1 and oc >= 2 and oc % 2 == 0:
                gather_k(oc // 2 - 1, store_marker())
            if agk is not None and wi == 2 and oc % 2 == 0:
                if oc == 0:
                    gather_k(3, store_marker())
                else:
                    gather_v(oc // 2 - 1, store_marker())
            par = sl % 2
            if sl == 0:
                ctx.wait(G, "dc", consts)
                ctx.op(G, lambda g: g.dma_start(out=wsl[0][:], in_=wqs[0]), "dw0", 16)
                wsl_ready[0] = ("dw0", ctx.c["dw0"])
            if sl + 1 < 24:
                npar = (sl + 1) % 2
                nwi, noc = divmod(sl + 1, 8)
                nsrc = (wqs, wks, wvs)[nwi]
                ctx.wait(G, "pe", wsl_free[npar])
                ctx.op(G, lambda g, nsrc=nsrc, noc=noc, npar=npar: g.dma_start(
                    out=wsl[npar][:], in_=nsrc[noc]), f"dw{npar}", 16)
                wsl_ready[npar] = (f"dw{npar}", ctx.c[f"dw{npar}"])

            for tb in range(NTB):
                pp = (sl * NTB + tb) % 2
                ctx.wait(T, *wsl_ready[par])
                ctx.wait(T, "dve", pso_free[pp])
                for k in range(32):
                    mm = ctx.op(T, lambda t, tb=tb, k=k, par=par, pp=pp: t.matmul(
                        pso[pp][:], xnt[:, k, tb * 128:(tb + 1) * 128],
                        wsl[par][:, k * 512:(k + 1) * 512],
                        start=(k == 0), stop=(k == 31)), "pe" if k == 31 else None)
                if tb == NTB - 1:
                    wsl_free[par] = mm

                if is_v:
                    ctx.wait(V, "pe", mm)
                    rpar = pp
                    ctx.wait(V, "dve", vo_ones)
                    if vo_free_sem[rpar]:
                        ctx.wait(V, vo_free_sem[rpar], vo_free[rpar])
                    ev = ctx.op(V, lambda v, rpar=rpar, pp=pp: v.tensor_copy(
                        vo_sb[rpar][:, :, 0:128], pso[pp][:]), "dve")
                    pso_free[pp] = ev
                    ctx.wait(Y, "dve", ev)
                    for hh in range(4):
                        row = (oc * 4 + hh) * 8 + tb
                        ctx.op(Y, lambda sy, row=row, hh=hh, rpar=rpar: sy.dma_start(
                            out=vhm[row], in_=vo_sb[rpar][:, hh, :]),
                            f"dst{rpar}", 16)
                    vo_free[rpar] = ctx.c[f"dst{rpar}"]
                    vo_free_sem[rpar] = f"dst{rpar}"
                else:
                    rpar = pp
                    ctx.wait(V, "pe", mm)
                    if ro_free_sem[rpar]:
                        ctx.wait(V, ro_free_sem[rpar], ro_free[rpar])
                    cc = cs_sb[:, tb, csel:csel + 1]
                    ss_ = cs_sb[:, tb, csel + 1:csel + 2]
                    ns_ = cs_sb[:, tb, csel + 2:csel + 3]
                    ctx.wait(V, "dc", consts)
                    ctx.op(V, lambda v, rpar=rpar, pp=pp, cc=cc: v.tensor_scalar(
                        out=ro_sb[rpar][:, 0:512:2], in0=pso[pp][:, 0:512:2],
                        scalar1=cc, scalar2=None, op0=ALU.mult), "dve")
                    ctx.op(V, lambda v, rpar=rpar, pp=pp, cc=cc: v.tensor_scalar(
                        out=ro_sb[rpar][:, 1:512:2], in0=pso[pp][:, 1:512:2],
                        scalar1=cc, scalar2=None, op0=ALU.mult), "dve")
                    ctx.wait(V, "dve")
                    ctx.op(V, lambda v, rpar=rpar, pp=pp, ss_=ss_: v.scalar_tensor_tensor(
                        out=ro_sb[rpar][:, 1:512:2], in0=pso[pp][:, 0:512:2],
                        scalar=ss_, in1=ro_sb[rpar][:, 1:512:2],
                        op0=ALU.mult, op1=ALU.add), "dve")
                    ctx.wait(V, "dve")
                    ev = ctx.op(V, lambda v, rpar=rpar, pp=pp, ns_=ns_: v.scalar_tensor_tensor(
                        out=ro_sb[rpar][:, 0:512:2], in0=pso[pp][:, 1:512:2],
                        scalar=ns_, in1=ro_sb[rpar][:, 0:512:2],
                        op0=ALU.mult, op1=ALU.add), "dve")
                    pso_free[pp] = ev
                    tp = (sl * NTB + tb) % 2
                    ctx.wait(T, "dve", max(ev, psT2_free[tp]))
                    for hh in range(4):
                        pe_n = ctx.op(T, lambda t, rpar=rpar, hh=hh, tp=tp: t.transpose(
                            psT[tp][:, hh * 128:(hh + 1) * 128],
                            ro_sb[rpar][:, hh * 128:(hh + 1) * 128],
                            id_sb[:]), "pe" if hh == 3 else None)
                    ctx.wait(V, "pe", pe_n)
                    if tt_free[tp]:
                        ctx.wait(V, f"dst{tp}", tt_free[tp])
                    psT2_free[tp] = ctx.op(V, lambda v, tp=tp: v.tensor_copy(
                        tt_sb[tp][:], psT[tp][:]), "dve")
                    ctx.wait(Y, "dve", psT2_free[tp])
                    if wi == 0:
                        oap = dst[0:128, oc * 4:(oc + 1) * 4, tb * 128:(tb + 1) * 128]
                    else:
                        oap = dst[oc // 2, :, (oc % 2) * 4:(oc % 2) * 4 + 4,
                                  tb * 128:(tb + 1) * 128]
                    tt_free[tp] = ctx.op(Y, lambda sy, oap=oap, tp=tp: sy.dma_start(
                        out=oap, in_=tt_sb[tp][:]), f"dst{tp}", 16)
                    ro_free[rpar] = ctx.c["pe"]
                    ro_free_sem[rpar] = "pe"
            sl += 1

    if agk is not None:
        gather_v(3, store_marker())


def stage_l1attn_a_weights(inp):
    wq = np.asarray(inp["L1_attn_wq"], np.float32)
    wk = np.asarray(inp["L1_attn_wk"], np.float32)
    wv = np.asarray(inp["L1_attn_wv"], np.float32)
    return {
        "wq_s": pack_slabs(np.ascontiguousarray(wq.T), 512),
        "wk_s": pack_slabs(np.ascontiguousarray(wk.T), 512),
        "wv_s": pack_slabs(np.ascontiguousarray(wv.T), 512),
        "ident": np.eye(128, dtype=np.float32),
    }


def cs_cols_for(core):
    b, (a0, a1), (b0, b1) = chunk_ranges(core)
    pos = np.concatenate([np.arange(a0, a1), np.arange(b0, b1)]).astype(np.float32)
    c = np.cos(pos)
    s = np.sin(pos)
    sc = 1.0 / np.sqrt(128.0)
    return np.stack([c * sc, s * sc, -s * sc, c, s, -s], axis=1).astype(np.float32)


NKA, NKB = 12, 28


def unit_owner(u):
    o = u if u < 4 else 7 - u
    return o, (0 if u < 4 else 512), (0 if u < 4 else 4)


def emit_l1attn_b(ctx, xin, qt, ktloc, vhmloc, agk, agv, wos,
                  ident, trimask, vmka, vmkb, dmask, xout):
    ctx.sems("dc", "kv0", "kv1", "dxs0", "dxs1", "dxs2", "dxs3",
             "dw0", "dw1", "pe", "act", "dve", "dst0", "dst1")
    T, V, S, G, Y = "tensor", "vector", "scalar", "gpsimd", "sync"

    id_sb = ctx.sb("b_id", [128, 128], BF16)
    tri_sb = ctx.sb("b_tri", [128, 128], BF16)
    ot_sb = ctx.sb("b_ot", [128, 32, TLOC], BF16)

    agk4 = agk.rearrange("(g r) (h t) -> g r h t", g=4, h=8)
    agv3 = agv.rearrange("r (p e) -> r p e", p=128)

    ctx.wait(G, "dc")
    ctx.op(G, lambda g: g.dma_start(out=id_sb[:], in_=ident[:]), "dc", 16)
    ctx.wait(G, "dc")
    ctx.op(G, lambda g: g.dma_start(out=tri_sb[:], in_=trimask[:]), "dc", 16)

    ot_ready = [0] * NTB

    with ExitStack() as p1:
        c1 = sub(ctx, p1)
        qt_sb = c1.sb("b_qt", [128, 32, TLOC], BF16)
        mka_sb = c1.sb("b_mka", [128, NKA * 129], BF16)
        mkb_sb = c1.sb("b_mkb", [128, NKB * 129], BF16)
        ctx.wait(G, "dc")
        ctx.op(G, lambda g: g.dma_start(out=mka_sb[:], in_=vmka.broadcast_to([128, NKA * 129])), "dc", 16)
        ctx.wait(G, "dc")
        ctx.op(G, lambda g: g.dma_start(out=mkb_sb[:], in_=vmkb.broadcast_to([128, NKB * 129])), "dc", 16)
        consts = ctx.c["dc"]
        ctx.op(Y, lambda sy: sy.dma_start(out=qt_sb[:], in_=qt[:]), "dst0", 16)
        qt_ready = ("dst0", ctx.c["dst0"])
        ka = [c1.sb(f"b_ka{i}", [128, NKA * 128], BF16) for i in range(2)]
        kb_ = [c1.sb(f"b_kb{i}", [128, NKB * 128], BF16) for i in range(2)]
        kd = [c1.sb(f"b_kd{i}", [128, TLOC], BF16) for i in range(2)]
        va = [c1.sb(f"b_va{i}", [128, NKA, 129], BF16) for i in range(2)]
        vb = [c1.sb(f"b_vb{i}", [128, NKB, 129], BF16) for i in range(2)]
        vd = [c1.sb(f"b_vd{i}", [128, 8, 129], BF16) for i in range(2)]
        probs = [c1.sb(f"b_pr{i}", [128, 512], BF16) for i in range(3)]
        dmask_sb = c1.sb("b_dmask", [128, 4, 512], BF16)
        rz = [c1.sb(f"b_rz{i}", [128, 1], F32) for i in range(2)]
        oh = [c1.sb(f"b_oh{i}", [128, 128], BF16) for i in range(2)]

        pss = [c1.ps(f"b_pss{i}", [128, 512]) for i in range(3)]
        pvt = [c1.ps(f"b_pv{i}", [128, 129]) for i in range(4)]
        pv = [t[:] for t in pvt]
        psT = [c1.ps("b_psT0", [128, 128], BF16)] * 2

        ctx.wait(G, "dc")
        ctx.op(G, lambda g: g.dma_start(out=dmask_sb[:], in_=dmask[:]), "dc", 16)
        consts2 = ctx.c["dc"]
        kv_ready = [None, None]
        kv_free = [0, 0]
        pss_free = [0, 0, 0]
        probs_free = [0, 0, 0]
        pv_free = [0, 0, 0, 0]
        rz_free = [0, 0]
        oh_free = [0, 0]
        psT_free = [0, 0]

        vmask_done = [0, 0]

        def load_head(h, par):
            if h % 8 == 0:
                ctx.wait(Y, "cc", 5 + h // 8)
                ctx.wait(G, "cc", 5 + h // 8)
            ctx.wait(Y, "pe", kv_free[par])
            ctx.wait(G, "pe", kv_free[par])
            ctx.wait(G, "dve", vmask_done[par])
            for u in range(NKA // 4):
                o, loff, _ = unit_owner(u)
                ctx.op(Y, lambda sy, o=o, loff=loff, u=u, h=h, par=par: sy.dma_start(
                    out=ka[par][:, u * 512:(u + 1) * 512],
                    in_=agk4[h // 8, o * 128:(o + 1) * 128, h % 8, loff:loff + 512]),
                    f"kv{par}", 16)
            for u in range(NKB // 4):
                o, loff, _ = unit_owner(u)
                ctx.op(Y, lambda sy, o=o, loff=loff, u=u, h=h, par=par: sy.dma_start(
                    out=kb_[par][:, u * 512:(u + 1) * 512],
                    in_=agk4[h // 8, o * 128:(o + 1) * 128, h % 8, loff:loff + 512]),
                    f"kv{par}", 16)
            ctx.op(Y, lambda sy, h=h, par=par: sy.dma_start(
                out=kd[par][:], in_=ktloc[h // 8, :, h % 8, :]), f"kv{par}", 16)
            kv_ready[par] = (f"kv{par}", ctx.c[f"kv{par}"])
            for u in range(NKA // 4):
                o, _, lb0 = unit_owner(u)
                r0 = (h // 8) * 256 + o * 64 + (h % 8) * 8 + lb0
                ctx.op(G, lambda g, r0=r0, u=u, par=par: g.dma_start(
                    out=va[par][:, u * 4:(u + 1) * 4, :],
                    in_=agv3[r0:r0 + 4].rearrange("r p e -> p r e")), f"kv{2 + par}", 16)
            for u in range(NKB // 4):
                o, _, lb0 = unit_owner(u)
                r0 = (h // 8) * 256 + o * 64 + (h % 8) * 8 + lb0
                ctx.op(G, lambda g, r0=r0, u=u, par=par: g.dma_start(
                    out=vb[par][:, u * 4:(u + 1) * 4, :],
                    in_=agv3[r0:r0 + 4].rearrange("r p e -> p r e")), f"kv{2 + par}", 16)
            ctx.op(G, lambda g, h=h, par=par: g.dma_start(
                out=vd[par][:],
                in_=vhmloc[h * 8:(h + 1) * 8].rearrange("r p e -> p r e")), f"kv{2 + par}", 16)
            vload_n = (f"kv{2 + par}", ctx.c[f"kv{2 + par}"])
            ctx.wait(V, *vload_n)
            ctx.wait(V, "dc", consts)
            ctx.op(V, lambda v, par=par: v.tensor_tensor(
                out=va[par][:], in0=va[par][:], in1=mka_sb[:], op=ALU.mult), "dve")
            vmask_done[par] = ctx.op(V, lambda v, par=par: v.tensor_tensor(
                out=vb[par][:], in0=vb[par][:], in1=mkb_sb[:], op=ALU.mult), "dve")

        load_head(0, 0)
        for h in range(32):
            par = h % 2
            if h + 1 < 32:
                load_head(h + 1, (h + 1) % 2)
            ctx.wait(T, *kv_ready[par])
            ctx.wait(T, "dve", vmask_done[par])
            ctx.wait(T, *qt_ready)
            ctx.wait(T, "dc", consts2)
            for chunk in range(2):
                nkb = (NKA, NKB)[chunk]
                ksrc = (ka, kb_)[chunk][par]
                vsrc = (va, vb)[chunk][par]
                qc0 = chunk * 512
                nke = nkb + 4
                for kbi in range(nke):
                    dj = kbi - nkb
                    gp = kbi % 3
                    ctx.wait(T, "act", pss_free[gp])
                    if dj < 0:
                        kap = lambda t, kbi=kbi, ksrc=ksrc: ksrc[:, kbi * 128:(kbi + 1) * 128]
                    else:
                        kap = lambda t, lk=chunk * 4 + dj, par=par: kd[par][:, lk * 128:(lk + 1) * 128]
                    ctx.op(T, lambda t, kap=kap, gp=gp, qc0=qc0, h=h:
                           t.matmul(pss[gp][:], kap(t),
                                    qt_sb[:, h, qc0:qc0 + 512],
                                    start=True, stop=True), "pe")
                    smm = ctx.c["pe"]
                    ctx.wait(S, "pe", max(smm, probs_free[gp]))
                    e_n = ctx.op(S, lambda s, gp=gp: s.activation(
                        out=probs[gp][:], in_=pss[gp][:], func=AF.Exp), "act")
                    pss_free[gp] = e_n
                    if dj >= 0:
                        ctx.wait(V, "act", e_n)
                        m_n = ctx.op(V, lambda v, gp=gp, dj=dj: v.tensor_tensor(
                            out=probs[gp][:], in0=probs[gp][:],
                            in1=dmask_sb[:, dj, :], op=ALU.mult), "dve")
                        ctx.wait(T, "dve", m_n)
                    else:
                        ctx.wait(T, "act", e_n)
                    if kbi == 0:
                        ctx.wait(T, "act", max(pv_free[0], pv_free[1],
                                               pv_free[2], pv_free[3]))
                    for qb in range(4):
                        if dj < 0:
                            vap = lambda t, kbi=kbi, vsrc=vsrc: vsrc[:, kbi, :]
                        else:
                            vap = lambda t, lk=chunk * 4 + dj, par=par: vd[par][:, lk, :]
                        ctx.op(T, lambda t, vap=vap, qb=qb, gp=gp, kbi=kbi, nke=nke:
                               t.matmul(pv[qb],
                                        probs[gp][:, qb * 128:(qb + 1) * 128],
                                        vap(t),
                                        start=(kbi == 0), stop=(kbi == nke - 1)),
                               "pe" if qb == 3 else None)
                    probs_free[gp] = ctx.c["pe"]

                pv_mm = ctx.c["pe"]
                for qb in range(4):
                    qp = qb % 2
                    ctx.wait(V, "pe", pv_mm)
                    ctx.wait(V, "act", rz_free[qp])
                    r_n = ctx.op(V, lambda v, qb=qb, qp=qp: v.reciprocal(
                        rz[qp][:], pv[qb][:, 128:129]), "dve")
                    ctx.wait(S, "dve", r_n)
                    ctx.wait(S, "pe", oh_free[qp])
                    ev_n = ctx.op(S, lambda s, qb=qb, qp=qp: s.activation(
                        out=oh[qp][:], in_=pv[qb][:, 0:128],
                        func=AF.Copy, scale=rz[qp][:]), "act")
                    rz_free[qp] = ev_n
                    pv_free[qb] = ev_n
                    ctx.wait(T, "act", ev_n)
                    ctx.wait(T, "dve", max(psT_free[0], psT_free[1]))
                    tp_n = ctx.op(T, lambda t, qp=qp: t.transpose(
                        psT[qp][:], oh[qp][:], id_sb[:]), "pe")
                    oh_free[qp] = tp_n
                    ctx.wait(V, "pe", tp_n)
                    psT_free[qp] = ctx.op(V, lambda v, h=h, qc0=qc0, qb=qb, qp=qp: v.tensor_copy(
                        ot_sb[:, h, qc0 + qb * 128:qc0 + (qb + 1) * 128], psT[qp][:]), "dve")
            kv_free[par] = ctx.c["pe"]
        p1_pe = ctx.c["pe"]
        p1_dve = ctx.c["dve"]
        p1_act = ctx.c["act"]
        ot_done = ctx.c["dve"]

    with ExitStack() as p2:
        c2 = sub(ctx, p2)
        wsl = [c2.sb(f"b_wsl{i}", [128, 32 * 512], BF16) for i in range(2)]
        xsl = [c2.sb(f"b_xsl{i}", [128, 512], F32) for i in range(4)]
        xo_sb = [c2.sb(f"b_xo{i}", [128, 512], F32) for i in range(2)]
        psp = [c2.ps(f"b_psp{i}", [128, 512]) for i in range(2)]

        wsl_ready = [None, None]
        wsl_free = [0, 0]
        ctx.wait(G, "pe", p1_pe)
        ctx.wait(G, "dve", p1_dve)
        ctx.wait(G, "act", p1_act)
        ctx.wait(T, "act", p1_act)
        ctx.wait(Y, "pe", p1_pe)
        ctx.wait(Y, "dve", p1_dve)
        ctx.wait(Y, "act", p1_act)
        ctx.wait(T, "dve", p1_dve)
        ctx.op(G, lambda g: g.dma_start(out=wsl[0][:], in_=wos[0]), "dw0", 16)
        wsl_ready[0] = ("dw0", ctx.c["dw0"])

        psp_free = [0, 0]
        xsl_free = [0] * 4
        xo_free = [0, 0]
        it = 0
        for oc in range(8):
            wp = oc % 2
            if oc + 1 < 8:
                npar = (oc + 1) % 2
                ctx.wait(G, "pe", wsl_free[npar])
                ctx.op(G, lambda g, oc=oc, npar=npar: g.dma_start(
                    out=wsl[npar][:], in_=wos[oc + 1]), f"dw{npar}", 16)
                wsl_ready[npar] = (f"dw{npar}", ctx.c[f"dw{npar}"])
            for tb in range(NTB):
                par = it % 2
                xp = it % 4
                ctx.wait(Y, "dve", xsl_free[xp])
                ctx.op(Y, lambda sy, tb=tb, oc=oc, xp=xp: sy.dma_start(
                    out=xsl[xp][:], in_=xin[tb * 128:(tb + 1) * 128, oc * 512:(oc + 1) * 512]),
                    f"dxs{xp}", 16)
                xsl_n = (f"dxs{xp}", ctx.c[f"dxs{xp}"])

                ctx.wait(T, *wsl_ready[wp])
                ctx.wait(T, "dve", max(ot_done, psp_free[par]))
                for k in range(32):
                    mm_n = ctx.op(T, lambda t, tb=tb, k=k, par=par, wp=wp: t.matmul(
                        psp[par][:], ot_sb[:, k, tb * 128:(tb + 1) * 128],
                        wsl[wp][:, k * 512:(k + 1) * 512],
                        start=(k == 0), stop=(k == 31)), "pe" if k == 31 else None)
                if tb == NTB - 1:
                    wsl_free[wp] = mm_n
                ctx.wait(V, "pe", mm_n)
                ctx.wait(V, *xsl_n)
                ctx.wait(V, f"dst{par}", xo_free[par])
                ep_n = ctx.op(V, lambda v, par=par, xp=xp: v.tensor_tensor(
                    out=xo_sb[par][:], in0=psp[par][:], in1=xsl[xp][:],
                    op=ALU.add), "dve")
                psp_free[par] = ep_n
                xsl_free[xp] = ep_n
                ctx.wait(Y, "dve", ep_n)
                xo_free[par] = ctx.op(Y, lambda sy, tb=tb, oc=oc, par=par: sy.dma_start(
                    out=xout[tb * 128:(tb + 1) * 128, oc * 512:(oc + 1) * 512],
                    in_=xo_sb[par][:]), f"dst{par}", 16)
                it += 1


def stage_l1attn_b(inp, s5a_outs):
    import ml_dtypes
    bf = ml_dtypes.bfloat16
    kt0 = np.asarray(s5a_outs[0]["kt"])
    fast = kt0.dtype.itemsize == 2
    if fast:
        conv = lambda a: np.asarray(a).view(np.uint16)
        out = lambda a: np.ascontiguousarray(a).view(bf)
        wt = np.uint16
        ONE = np.float32(1.0).astype(bf).view(np.uint16)
    else:
        conv = lambda a: np.asarray(a)
        out = lambda a: np.ascontiguousarray(a)
        wt = kt0.dtype
        ONE = wt.type(1.0) if hasattr(wt, "type") else 1.0
    wo = np.asarray(inp["L1_attn_wo"], np.float32)
    k_idx, q_idx = np.meshgrid(np.arange(128), np.arange(128), indexing="ij")
    shared = {
        "wo_s": pack_slabs(np.ascontiguousarray(wo.T), 512),
        "ident": np.eye(128, dtype=np.float32),
        "trimask": (k_idx <= q_idx).astype(np.float32),
    }
    percore = [None] * 8
    for bb in range(2):
        cores = [bb * 4 + c for c in range(4)]
        ktg = np.zeros((128, 32, 4096), wt)
        vg = np.zeros((4096, 32, 129), wt)
        for c in range(4):
            co = cores[c]
            kt_c = conv(s5a_outs[co]["kt"])
            vt_c = conv(s5a_outs[co]["vt"]).reshape(1024, 32, 128)
            for half, u in ((0, c), (1, 7 - c)):
                ktg[:, :, u * 512:(u + 1) * 512] = kt_c[:, :, half * 512:(half + 1) * 512]
                vg[u * 512:(u + 1) * 512, :, :128] = vt_c[half * 512:(half + 1) * 512]
                vg[u * 512:(u + 1) * 512, :, 128] = ONE
        for c in range(4):
            co = cores[c]
            kta = ktg[:, :, :NKA * 128].copy()
            kta[:, :, 512 * c:] = 0
            ktb = ktg[:, :, :NKB * 128].copy()
            ktb[:, :, 512 * (7 - c):] = 0
            va = vg[:NKA * 128].copy()
            va[512 * c:] = 0
            vb = vg[:NKB * 128].copy()
            vb[512 * (7 - c):] = 0
            vd = np.zeros((1024, 32, 129), wt)
            vd[:, :, :128] = conv(s5a_outs[co]["vt"]).reshape(1024, 32, 128)
            vd[:, :, 128] = ONE

            def vhead(vv, nkb):
                a = vv.reshape(nkb, 128, 32, 129).transpose(2, 1, 0, 3)
                return out(a.reshape(32, 128, nkb * 129))

            percore[co] = {
                "qt": out(conv(s5a_outs[co]["qt"])),
                "kta": out(kta),
                "ktb": out(ktb),
                "ktd": out(conv(s5a_outs[co]["kt"])),
                "vta": vhead(va, NKA),
                "vtb": vhead(vb, NKB),
                "vtd": vhead(vd, 8),
            }
    return shared, percore


def ctx_join(ctx):
    for e in ENGINES:
        for s in ("pe", "act", "dve", "dst0", "dst1"):
            if s in ctx.c:
                ctx.wait(e, s)


def build_megakernel(upto=7):
    nc = bass.Bass("TRN2", target_bir_lowering=False)
    EI, EO = "ExternalInput", "ExternalOutput"
    xin_h = nc.dram_tensor("xin_h", [TIN, D], F32, kind=EI).ap()
    wup = nc.dram_tensor("wup_s", [8, 128, 32 * 512], F32, kind=EI).ap()
    gate_t = nc.dram_tensor("gate_t", [D, 4], F32, kind=EI).ap()
    ebias = nc.dram_tensor("ebias", [1, 4], F32, kind=EI).ap()
    lora_at = nc.dram_tensor("lora_at", [D, 32], F32, kind=EI).ap()
    lora_bs = nc.dram_tensor("lora_bs", [32, D], F32, kind=EI).ap()
    t8 = nc.dram_tensor("t8", [4, 32], F32, kind=EI).ap()
    ident = nc.dram_tensor("ident", [128, 128], F32, kind=EI).ap()
    convw = nc.dram_tensor("convw_s", [3, 6, 128, 32 * 512], F32, kind=EI).ap()
    convb = nc.dram_tensor("convb", [1, 3072], F32, kind=EI).ap()
    wq0 = nc.dram_tensor("wq_r", [1024, 1024], F32, kind=EI).ap()
    wk0 = nc.dram_tensor("wk_r", [1024, 1024], F32, kind=EI).ap()
    wv0 = nc.dram_tensor("wv_r", [1024, 1024], F32, kind=EI).ap()
    wo0 = nc.dram_tensor("wo_r", [1024, 1024], F32, kind=EI).ap()
    bqc = nc.dram_tensor("bq_c", [128, 8], F32, kind=EI).ap()
    bkc = nc.dram_tensor("bk_c", [128, 8], F32, kind=EI).ap()
    bvr = nc.dram_tensor("bv_r", [1, 1024], F32, kind=EI).ap()
    bor = nc.dram_tensor("bo_r", [1, 1024], F32, kind=EI).ap()
    projw = nc.dram_tensor("projw_s", [8, 128, 8 * 512], F32, kind=EI).ap()
    projb = nc.dram_tensor("projb", [1, D], F32, kind=EI).ap()
    trimask = nc.dram_tensor("trimask", [128, 128], F32, kind=EI).ap()
    a2 = nc.dram_tensor("a2", [128, 64], F32, kind=EI).ap()
    b2 = nc.dram_tensor("b2", [128, 64], F32, kind=EI).ap()
    betar = nc.dram_tensor("betar", [64, 64], F32, kind=EI).ap()
    gcols = nc.dram_tensor("gcols", [128, 32], F32, kind=EI).ap()
    biasr = nc.dram_tensor("biasr", [1, D], F32, kind=EI).ap()
    wqs = nc.dram_tensor("wq_s", [8, 128, 32 * 512], F32, kind=EI).ap()
    wks = nc.dram_tensor("wk_s", [8, 128, 32 * 512], F32, kind=EI).ap()
    wvs = nc.dram_tensor("wv_s", [8, 128, 32 * 512], F32, kind=EI).ap()
    wos = nc.dram_tensor("wo_s", [8, 128, 32 * 512], F32, kind=EI).ap()
    cs = nc.dram_tensor("cs_cols", [TLOC, 6], F32, kind=EI).ap()
    vmka = nc.dram_tensor("vmka", [1, NKA * 129], F32, kind=EI).ap()
    vmkb = nc.dram_tensor("vmkb", [1, NKB * 129], F32, kind=EI).ap()
    dmask = nc.dram_tensor("dmask", [128, 4, 512], F32, kind=EI).ap()
    xout = nc.dram_tensor("xout", [TLOC, D], F32, kind=EO).ap()
    x1h = nc.dram_tensor("x1h", [TIN, D], F32).ap()
    x2 = nc.dram_tensor("x2", [TLOC, D], F32).ap()
    x3 = nc.dram_tensor("x3", [TLOC, D], F32).ap()
    x4 = nc.dram_tensor("x4", [TLOC, D], F32).ap()
    x5 = nc.dram_tensor("x5", [TLOC, D], F32).ap()
    qt_t = nc.dram_tensor("qt_i", [128, 32, TLOC], BF16)
    kt_t = nc.dram_tensor("kt_i", [4, 128, 8, TLOC], BF16)
    vhm_t = nc.dram_tensor("vhm_i", [256, 128, 129], BF16)
    agk_t = nc.dram_tensor("agk_i", [2048, 8 * TLOC], BF16)
    agv_t = nc.dram_tensor("agv_i", [1024, 128 * 129], BF16)
    cdram = nc.dram_tensor("c_scratch", [64, 64], F32).ap()
    qt, kt, vhm = qt_t.ap(), kt_t.ap(), vhm_t.ap()
    agk, agv = agk_t.ap(), agv_t.ap()

    tiles9 = [(i * 128, 0, 128) for i in range(8)] + [(TIN - 128, 124, 128)]

    with ExitStack() as es:
        ctx = Ctx(nc, es)
        ctx.sems("dc", "dcy", "dcy2", "dx0", "dx1", "dxs0", "dxs1", "dxs2", "dxs3",
                 "dw0", "dw1", "kv0", "kv1", "kv2", "kv3", "pe", "act", "dve",
                 "dst0", "dst1", "cc")

        stages = []

        def stage(fn):
            stages.append(fn)

        @stage
        def s1():
            with ExitStack() as s:
                emit_l0ffn(sub(ctx, s), xin_h, x1h, wup, gate_t, ebias, lora_at,
                           lora_bs, t8, ident, pre="s1_", tiles=tiles9, ncol=256)

        @stage
        def s2():
            with ExitStack() as s:
                emit_l0attn(sub(ctx, s), x1h, x2, convw, convb, wq0, wk0, wv0, wo0,
                            bqc, bkc, bvr, bor, projw, projb, ident, trimask)

        @stage
        def s3():
            with ExitStack() as s:
                emit_l0ffn(sub(ctx, s), x2, x3, wup, gate_t, ebias, lora_at,
                           lora_bs, t8, ident, pre="s3_")

        @stage
        def s4():
            with ExitStack() as s:
                emit_l1ffn(sub(ctx, s), x3, x4, a2, b2, betar, gcols, biasr,
                           ident, cdram, pre="s4_")

        @stage
        def s5a():
            with ExitStack() as s:
                emit_l1attn_a(sub(ctx, s), x4, wqs, wks, wvs, cs, ident, qt, kt, vhm,
                              kt2d=kt.rearrange("g p h t -> (g p) (h t)"),
                              vhm2d=vhm.rearrange("r p e -> r (p e)"),
                              agk=agk, agv=agv)

        @stage
        def s5b():
            with ExitStack() as s:
                emit_l1attn_b(sub(ctx, s), x4, qt, kt, vhm, agk, agv, wos,
                              ident, trimask, vmka, vmkb, dmask, x5)

        @stage
        def s6():
            with ExitStack() as s:
                emit_l1ffn(sub(ctx, s), x5, xout, a2, b2, betar, gcols, biasr,
                           ident, cdram, pre="s6_")

        for i, fn in enumerate(stages[:upto]):
            if i:
                ctx_join(ctx)
            fn()

        if upto < 7:
            ctx_join(ctx)
            srcx = [None, x1h, x2, x3, x4, None, x5][upto]
            if upto == 1:
                ctx.op("sync", lambda sy: sy.dma_start(out=xout[0:512, :], in_=x1h[1:513, :]), "dst0", 16)
                ctx.op("sync", lambda sy: sy.dma_start(out=xout[512:1024, :], in_=x1h[515:1027, :]), "dst1", 16)
            elif srcx is not None:
                ctx.op("sync", lambda sy, srcx=srcx: sy.dma_start(out=xout[:], in_=srcx[:]), "dst0", 16)

        ctx.wait("sync", "dst0")
        ctx.wait("sync", "dst1")
        ctx.flush()
    return nc


def stage_megakernel_inputs(inputs):
    shared = {}
    shared.update(stage_l0ffn_weights(inputs))
    shared.update(stage_l0attn_weights(inputs))
    shared.update(stage_l1ffn_weights(inputs))
    shared.update(stage_l1attn_a_weights(inputs))
    wo = np.asarray(inputs["L1_attn_wo"], np.float32)
    shared["wo_s"] = pack_slabs(np.ascontiguousarray(wo.T), 512)
    dm = np.zeros((128, 4, 512), np.float32)
    kk, qq = np.meshgrid(np.arange(128), np.arange(128), indexing="ij")
    tri = (kk <= qq).astype(np.float32)
    for j in range(4):
        for qb in range(4):
            dm[:, j, qb * 128:(qb + 1) * 128] = 0.0 if qb < j else (tri if qb == j else 1.0)
    shared["dmask"] = dm
    percore = []
    for core in range(NCORES):
        cpr = core % 4
        mka = np.zeros((1, NKA * 129), np.float32)
        mkb = np.zeros((1, NKB * 129), np.float32)
        mka[0, :4 * cpr * 129] = 1.0
        mkb[0, :4 * (7 - cpr) * 129] = 1.0
        percore.append({
            "cs_cols": cs_cols_for(core),
            "vmka": mka,
            "vmkb": mkb,
        })
    return shared, percore


_PROGS = {}


def _prog(name, builder):
    if name not in _PROGS:
        _PROGS[name] = builder()
    return _PROGS[name]


def kernel(**inputs):
    from concourse.bass_utils import run_bass_kernel_spmd
    x = np.ascontiguousarray(np.asarray(inputs["L0_attn_inputs"], np.float32))
    nc = _prog("mega", build_megakernel)
    shared, percore = stage_megakernel_inputs(inputs)
    in_maps = []
    for core in range(NCORES):
        m = dict(shared)
        m.update(percore[core])
        m["xin_h"] = slice_haloed(x, core)
        in_maps.append(m)
    res = run_bass_kernel_spmd(nc, in_maps, core_ids=list(range(NCORES)))
    sh = [res.results[c]["xout"] for c in range(NCORES)]
    out = np.empty(x.shape, sh[0].dtype)
    unslice_owned(sh, out)
    return out
